# version 07
# baseline (speedup 1.0000x reference)
from contextlib import ExitStack

import numpy as np
import concourse.bass as bass
import concourse.mybir as mybir
from concourse.bass_utils import run_bass_kernel_spmd

F32 = mybir.dt.float32
BF16 = mybir.dt.bfloat16
I32 = mybir.dt.int32
AF = mybir.ActivationFunctionType
ALU = mybir.AluOpType
AX = mybir.AxisListType

NCORES = 8
BL = 2
S = 2048
T = BL * S
D = 1024
DC = D // 128
H = 16
HD = 64
NPAIR = H // 2
I3 = 3072
IC = I3 // 128
QMAX = 32767.0
PI = float(np.pi)
TWO_PI = 2.0 * PI
INV_2PI = 1.0 / TWO_PI
CLAMP = 3.1415925
LN_EPS = 1e-5
SIM_LOCAL = False
SETUP_UPTO = "all"
DEQ_UPTO = 9


class Sem:

    def __init__(self, nc, ctx, name):
        self.h = ctx.enter_context(nc.semaphore(name))
        self.v = 0

    def inc(self, instr, n=1):
        instr.then_inc(self.h, n)
        self.v += n
        return instr

    def dinc(self, instr):
        return self.inc(instr, 16)


class ChainEng:

    _PASS = {"wait_ge", "wait_op", "sem_inc", "dma_start", "engine_nop", "nop"}

    def __init__(self, eng, sem):
        self._eng = eng
        self._sem = sem

    def __getattr__(self, name):
        f = getattr(self._eng, name)
        if name in self._PASS or name.startswith("_"):
            return f

        def g(*a, **k):
            self._eng.wait_ge(self._sem.h, self._sem.v)
            ins = f(*a, **k)
            self._sem.inc(ins)
            return ins
        return g


class Prog:
    def __init__(self, upto="all", taps=()):
        self.upto = upto
        self.taps = set(taps)
        nc = bass.Bass("TRN2", target_bir_lowering=False)
        self.nc = nc
        self.ctx = ExitStack()
        self.tap_handles = {}
        self._declare_io()
        self._alloc_persistent()


    def _declare_io(self):
        nc = self.nc
        dr = lambda name, shape, dt=F32, kind="ExternalInput": nc.dram_tensor(
            name, shape, dt, kind=kind)
        self.d_x = dr("x_fm", [BL, D, S])
        self.d_c2 = dr("c2", [BL, D])
        self.d_wmod = dr("wmod", [D, 6 * D])
        self.d_Wz = dr("Wz", [D, D])
        self.d_Wq = dr("Wq", [D, D])
        self.d_Wr = dr("Wr", [D, D])
        self.d_Wh = dr("Wh", [D, D])
        self.d_Wo = dr("Wo", [D, D])
        self.d_Wphi = dr("Wphi", [HD, HD])
        self.d_vec = dr("vecs", [7, D])
        self.d_bphi = dr("bphi", [1, HD])
        self.d_wup = dr("wup_sl", [I3, 128])
        self.d_wgt = dr("wgt_sl", [I3, 128])
        self.d_wdn = dr("wdn_sl", [D, 384])
        self.d_out = dr("out", [BL, S, D], kind="ExternalOutput")
        di = lambda name, shape, dt: nc.dram_tensor(name, shape, dt, kind="Internal")
        self.d_x2f = di("x2f_hbm", [D, T], F32)
        self.d_x2t = di("x2t_hbm", [T, D], F32)
        self.d_h2 = di("h2_hbm", [D, T], BF16)
        self.d_up = di("up_hbm", [I3, T], BF16)
        self.d_gt = di("gt_hbm", [I3, T], BF16)
        self.d_g2 = di("g2_hbm", [I3, T], BF16)
        self.d_o = di("o_hbm", [I3, T], BF16)
        self.d_upT_my = di("upT_my", [128, I3], BF16)
        self.d_gtT_my = di("gtT_my", [128, I3], BF16)
        self.d_dnT_my = di("dnT_my", [384, D], BF16)
        self.d_upT = di("upT_all", [D, I3], BF16)
        self.d_gtT = di("gtT_all", [D, I3], BF16)
        self.d_dnT = di("dnT_all", [I3, D], BF16)
        self.d_arin = [di(f"arin{i}", [1, 64], F32) for i in range(4)]
        self.d_arout = [di(f"arout{i}", [1, 64], F32) for i in range(4)]
        self.t_identf = nc.inline_tensor(np.eye(128, dtype=np.float32), "identf")
        import ml_dtypes
        self.t_identb = nc.inline_tensor(
            np.eye(128, dtype=ml_dtypes.bfloat16), "identb")

    def _alloc_persistent(self):
        nc, ctx = self.nc, self.ctx
        sb = lambda name, shape, dt=F32: ctx.enter_context(
            nc.sbuf_tensor(name, shape, dt))
        self.identf = sb("identf_sb", [128, 128])
        self.identb = sb("identb_sb", [128, 128], BF16)
        self.ones_r = sb("ones_r", [1, 128])
        self.ones_rb = sb("ones_rb", [1, 128], BF16)
        self.cst_b = sb("cst_b", [128, 1], BF16)
        self.ones_cb = sb("ones_cb", [128, 1], BF16)
        self.bz_t = sb("bz_t", [128, DC])
        self.g_t = sb("g_t", [128, DC])
        self.lb_t = sb("lb_t", [128, DC])
        self.bo_t = sb("bo_t", [128, DC])
        self.bq_dup = sb("bq_dup", [128, H])
        self.br_dup = sb("br_dup", [128, H])
        self.bh_row = sb("bh_row", [1, D])
        self.bphi_v = sb("bphi_v", [128, 1])
        self.mod_t = sb("mod_t", [128, DC, 6, BL])
        self.gabo_t = sb("gabo_t", [128, DC, BL])
        self.gbbc = sb("gbbc", [128, BL, D], BF16)
        self.mu_r = sb("mu_r", [1, 512])
        self.ms_r = sb("ms_r", [1, 512])
        self.a_r = sb("a_r", [1, 512])
        self.sc_col = sb("sc_col", [128, 16])
        self.h2max = sb("h2max", [128, 1])
        self.sld = Sem(nc, ctx, "sld")
        self.gld = Sem(nc, ctx, "gld")
        self.cc = Sem(nc, ctx, "cc")
        self.tse = Sem(nc, ctx, "tse")
        self.vse = Sem(nc, ctx, "vse")
        self.ase = Sem(nc, ctx, "ase")
        self.gse = Sem(nc, ctx, "gse")
        self.sse = Sem(nc, ctx, "sse")
        self.V = None
        self.A = None
        self.gcm = Sem(nc, ctx, "gcm")
        self.scm = Sem(nc, ctx, "scm")
        self._gcommitted = 0
        self._scommitted = 0
        self.V = ChainEng(nc.vector, self.vse)
        self.A = ChainEng(nc.scalar, self.ase)
        self.G = ChainEng(nc.gpsimd, self.gse)


    def barrier(self):
        nc = self.nc
        self.gcommit()
        self.scommit()
        nc.sync.wait_ge(self.sld.h, self.sld.v)
        nc.gpsimd.wait_ge(self.gld.h, self.gld.v)
        nc.gpsimd.wait_ge(self.cc.h, self.cc.v)
        nc.sync.wait_ge(self.sse.h, self.sse.v)
        nc.sync.sem_inc(self.sse.h, 1); self.sse.v += 1
        nc.tensor.wait_ge(self.tse.h, self.tse.v)
        nc.tensor.sem_inc(self.tse.h, 1); self.tse.v += 1
        nc.vector.wait_ge(self.vse.h, self.vse.v)
        nc.vector.sem_inc(self.vse.h, 1); self.vse.v += 1
        nc.scalar.wait_ge(self.ase.h, self.ase.v)
        nc.scalar.sem_inc(self.ase.h, 1); self.ase.v += 1
        nc.gpsimd.wait_ge(self.gse.h, self.gse.v)
        nc.gpsimd.sem_inc(self.gse.h, 1); self.gse.v += 1
        for eng in (nc.sync, nc.tensor, nc.vector, nc.scalar, nc.gpsimd):
            for s in (self.sse, self.tse, self.vse, self.ase, self.gse):
                eng.wait_ge(s.h, s.v)

    def gcommit(self):
        if self.gld.v > self._gcommitted:
            nc = self.nc
            nc.gpsimd.wait_ge(self.gld.h, self.gld.v)
            nc.gpsimd.wait_ge(self.gcm.h, self.gcm.v)
            nc.gpsimd.sem_inc(self.gcm.h, 1)
            self.gcm.v += 1
            self._gcommitted = self.gld.v

    def scommit(self):
        if self.sld.v > self._scommitted:
            nc = self.nc
            nc.sync.wait_ge(self.sld.h, self.sld.v)
            nc.sync.wait_ge(self.scm.h, self.scm.v)
            nc.sync.sem_inc(self.scm.h, 1)
            self.scm.v += 1
            self._scommitted = self.sld.v

    def tok(self):
        return (self.vse.v, self.ase.v, self.gse.v)

    def wtok(self, eng, t):
        if t[0]:
            eng.wait_ge(self.vse.h, t[0])
        if t[1]:
            eng.wait_ge(self.ase.h, t[1])
        if t[2]:
            eng.wait_ge(self.gse.h, t[2])
        if len(t) > 3 and t[3]:
            eng.wait_ge(self.scm.h, t[3])
        if len(t) > 4 and t[4]:
            eng.wait_ge(self.tse.h, t[4])

    def wv(self, eng):
        eng.wait_ge(self.vse.h, self.vse.v)

    def wt(self, eng):
        eng.wait_ge(self.tse.h, self.tse.v)

    def wa(self, eng):
        eng.wait_ge(self.ase.h, self.ase.v)

    def wsld(self, eng):
        self.scommit()
        eng.wait_ge(self.scm.h, self.scm.v)

    def wgld(self, eng):
        self.gcommit()
        eng.wait_ge(self.gcm.h, self.gcm.v)

    def wg(self, eng):
        eng.wait_ge(self.gse.h, self.gse.v)

    def guard_load(self, eng):
        self.wv(eng)
        self.wa(eng)
        self.wt(eng)
        if eng is self.nc.gpsimd:
            eng.wait_ge(self.gld.h, self.gld.v)
        elif eng is self.nc.sync:
            eng.wait_ge(self.sld.h, self.sld.v)

    def load_weight(self, dram, wslot):
        nc = self.nc
        self.guard_load(nc.gpsimd)
        self.gld.dinc(nc.gpsimd.dma_start(
            out=wslot[:],
            in_=dram.ap().rearrange("(k p) j -> p k j", p=128)))
        self.wgld(nc.tensor)


    def _tap(self, name, ap, shape, dt):
        nc = self.nc
        t = nc.dram_tensor(f"tap_{name}", shape, dt, kind="ExternalOutput")
        self.tap_handles[name] = t
        self.sld.dinc(nc.sync.dma_start(out=t.ap(), in_=ap))
        nc.sync.wait_ge(self.sld.h, self.sld.v)


    def setup(self):
        nc = self.nc
        ctx = ExitStack()
        sb = lambda name, shape, dt=F32: ctx.enter_context(
            nc.sbuf_tensor(name, shape, dt))
        ps = lambda name, shape, dt=F32: ctx.enter_context(
            nc.psum_tensor(name, shape, dt))
        with ctx:
            row7 = [sb(f"row7_{i}", [1, D]) for i in range(7)]
            bphi_row = sb("bphi_row", [1, HD])
            c_sb = sb("c_sb", [BL, D])
            c_si = sb("c_si", [BL, D])
            sc_fm = sb("sc_fm", [128, DC, BL])
            mod_rows = sb("mod_rows", [BL, 6 * D])
            wmod_c = sb("wmod_c", [128, DC, 512])
            scr = sb("setup_scr", [128, DC, BL])
            tmp = sb("setup_tmp", [128, DC, BL])
            gbrow = [sb(f"gbrow{b}", [1, D]) for b in range(BL)]
            pv = ps("pv", [128, 16])
            pm = ps("pm", [BL, 512])
            pgb = ps("pgb", [128, 512])
            self.sld.dinc(nc.sync.dma_start(out=self.identf[:],
                                            in_=self.t_identf.ap()))
            self.sld.dinc(nc.sync.dma_start(out=self.identb[:],
                                            in_=self.t_identb.ap()))
            for i in range(7):
                self.sld.dinc(nc.sync.dma_start(out=row7[i][:],
                                                in_=self.d_vec.ap()[i:i + 1, :]))
            self.sld.dinc(nc.sync.dma_start(out=c_sb[:], in_=self.d_c2.ap()))
            self.sld.dinc(nc.sync.dma_start(out=bphi_row[:], in_=self.d_bphi.ap()))
            self.wsld(nc.vector)
            self.V.memset(self.ones_r[:], 1.0)
            self.V.memset(self.ones_rb[:], 1.0)
            self.V.memset(self.cst_b[:], 1.0 / 1024.0)
            self.V.memset(self.ones_cb[:], 1.0)
            self.V.memset(self.h2max[:], 0.0)
            self.wsld(nc.tensor)
            for vi, dst in ((0, self.bz_t), (1, self.g_t), (2, self.lb_t),
                            (3, self.bo_t)):
                self.wv(nc.tensor)
                for k in range(DC):
                    ins = nc.tensor.transpose(
                        pv[:, k:k + 1], row7[vi][0:1, 128 * k:128 * (k + 1)],
                        self.identf[0:1, 0:1])
                self.tse.inc(ins)
                self.wt(nc.vector)
                self.V.tensor_copy(dst[:], pv[:, 0:DC])
            for vi, dst in ((4, self.bq_dup), (5, self.br_dup)):
                self.wv(nc.tensor)
                for h in range(H):
                    ins = nc.tensor.transpose(
                        pv[0:64, h:h + 1], row7[vi][0:1, 64 * h:64 * (h + 1)],
                        self.identf[0:1, 0:1])
                self.tse.inc(ins)
                self.wt(nc.vector)
                self.V.tensor_copy(dst[0:64, :], pv[0:64, 0:H])
                self.wv(nc.gpsimd)
                self.gld.dinc(nc.gpsimd.dma_start(out=dst[64:128, :],
                                                  in_=dst[0:64, :]))
            self.V.tensor_copy(self.bh_row[:], row7[6][:])
            self.wv(nc.tensor)
            ins = nc.tensor.transpose(pv[0:64, 0:1], bphi_row[0:1, :],
                                      self.identf[0:1, 0:1])
            self.tse.inc(ins)
            self.wt(nc.vector)
            self.V.tensor_copy(self.bphi_v[0:64, 0:1],
                                               pv[0:64, 0:1])
            self.wv(nc.gpsimd)
            self.gld.dinc(nc.gpsimd.dma_start(out=self.bphi_v[64:128, 0:1],
                                              in_=self.bphi_v[0:64, 0:1]))
            self.wgld(nc.vector)
            self.V.tensor_scalar(
                self.bphi_v[0:64, 0:1], self.bphi_v[0:64, 0:1], PI / 2.0,
                None, ALU.add)
            if SETUP_UPTO == "vec":
                self.barrier()
                return
            self.wsld(nc.scalar)
            self.A.activation(c_si[:], c_sb[:], AF.Silu,
                                              bias=0.0, scale=1.0)
            self.wa(nc.tensor)
            self.wv(nc.tensor)
            for k in range(DC):
                ins = nc.tensor.transpose(
                    pv[:, 2 * k:2 * k + BL], c_si[0:BL, 128 * k:128 * (k + 1)],
                    self.identf[0:BL, 0:BL])
            self.tse.inc(ins)
            self.wt(nc.vector)
            self.V.tensor_copy(
                sc_fm[:].rearrange("p k b -> p (k b)"), pv[:, 0:2 * DC])
            for j in range(12):
                self.guard_load(nc.gpsimd)
                self.gld.dinc(nc.gpsimd.dma_start(
                    out=wmod_c[:],
                    in_=self.d_wmod.ap().rearrange("(k p) j -> p k j", p=128)
                    [:, :, 512 * j:512 * (j + 1)]))
                self.wgld(nc.tensor)
                self.wv(nc.tensor)
                for k in range(DC):
                    ins = nc.tensor.matmul(pm[:], sc_fm[:, k, :], wmod_c[:, k, :],
                                           start=(k == 0), stop=(k == DC - 1))
                self.tse.inc(ins)
                self.wt(nc.vector)
                self.V.tensor_copy(
                    mod_rows[:, 512 * j:512 * (j + 1)], pm[:])
            for param in range(6):
                for k in range(DC):
                    off = param * D + 128 * k
                    self.wv(nc.tensor)
                    ins = nc.tensor.transpose(
                        pv[:, 0:BL], mod_rows[0:BL, off:off + 128],
                        self.identf[0:BL, 0:BL])
                    self.tse.inc(ins)
                    self.wt(nc.vector)
                    self.V.tensor_copy(
                        self.mod_t[:, k, param, :], pv[:, 0:BL])
            for param in (1, 4):
                self.V.tensor_scalar(
                    self.mod_t[:, :, param, :], self.mod_t[:, :, param, :],
                    1.0, None, ALU.add)
            for param in (2, 5):
                g = self.mod_t[:, :, param, :]
                self.wv(nc.scalar)
                self.A.activation(scr[:], g, AF.Sign, bias=0.0, scale=1.0)
                self.A.activation(tmp[:], g, AF.Abs,
                                                  bias=0.0, scale=1.0)
                self.wa(nc.vector)
                self.V.tensor_scalar(tmp[:], tmp[:], 0.25, None, ALU.max)
                self.V.tensor_mul(g, tmp[:], scr[:])
            self.V.tensor_mul(
                self.gabo_t[:], self.mod_t[:, :, 2, :],
                self.bo_t[:, :, None].broadcast_to([128, DC, BL]))
            for k in range(DC):
                for b in range(BL):
                    self.wv(nc.tensor)
                    ins = nc.tensor.transpose(pm[0:1, 0:128],
                                              self.mod_t[:, k, 5, b:b + 1],
                                              self.identf[:])
                    self.tse.inc(ins)
                    self.wt(nc.vector)
                    self.V.tensor_copy(
                        gbrow[b][0:1, 128 * k:128 * (k + 1)],
                        pm[0:1, 0:128])
            for b in range(BL):
                for jc in range(2):
                    self.wv(nc.tensor)
                    ins = nc.tensor.matmul(
                        pgb[:], self.ones_r[:],
                        gbrow[b][0:1, 512 * jc:512 * (jc + 1)],
                        start=True, stop=True)
                    self.tse.inc(ins)
                    self.wt(nc.vector)
                    self.V.tensor_copy(
                        self.gbbc[:, b, 512 * jc:512 * (jc + 1)], pgb[:])
            if SETUP_UPTO != "mod":
                self._dequant_ffn_weights(ctx)
        self.barrier()
        if "mod" in self.taps:
            self._tap("mod", self.mod_t[:].rearrange("p k q b -> p (k q b)"),
                      [128, DC * 6 * BL], F32)
        if "gabo" in self.taps:
            self._tap("gabo", self.gabo_t[:].rearrange("p k b -> p (k b)"),
                      [128, DC * BL], F32)

    def _dequant_ffn_weights(self, ctx):
        nc = self.nc
        ps = lambda name, shape, dt=F32: ctx.enter_context(
            nc.psum_tensor(name, shape, dt))
        ptr = ps("ptr_dq", [128, 512], BF16)

        def dequant(dsrc, nblk, nchunk, name):
            s2 = ExitStack()
            cols = nblk * 32
            w = s2.enter_context(nc.sbuf_tensor(f"wdq_{name}",
                                                [128, nchunk, cols], F32))
            m = s2.enter_context(nc.sbuf_tensor(f"wdm_{name}",
                                                [128, nchunk * nblk], F32))
            si = s2.enter_context(nc.sbuf_tensor(f"wds_{name}",
                                                 [128, nchunk * nblk], F32))
            ki = s2.enter_context(nc.sbuf_tensor(f"wdk_{name}",
                                                 [128, nchunk, cols], I32))
            kf = s2.enter_context(nc.sbuf_tensor(f"wdf_{name}",
                                                 [128, nchunk, cols], F32))
            wb = s2.enter_context(nc.sbuf_tensor(f"wdb_{name}",
                                                 [128, nchunk, cols], BF16))
            nch = w.shape[1]
            third = (nch + 2) // 3
            for c0 in range(0, nch, third):
                c1 = min(c0 + third, nch)
                self.gld.dinc(nc.gpsimd.dma_start(
                    out=w[:, c0:c1, :],
                    in_=dsrc.ap().rearrange("(c p) i -> p c i", p=128)
                    [:, c0:c1, :]))
            self.wgld(nc.vector)
            wv4 = w[:].rearrange("p c (b j) -> p c b j", j=32)
            if DEQ_UPTO >= 2:
                self.V.tensor_reduce(
                    m[:].rearrange("p (c b) -> p c b", b=nblk), wv4, AX.X,
                    ALU.max, apply_absolute_value=True)
                self.V.tensor_scalar(m[:], m[:], 1.0 / 7.0, 1e-12,
                                        ALU.mult, ALU.max)
                self.V.reciprocal(si[:], m[:])
            else:
                self.V.memset(m[:], 1.0)
                self.V.memset(si[:], 1.0)
            if DEQ_UPTO >= 3:
                self.V.tensor_mul(
                    wv4, wv4,
                    si[:].rearrange("p (c b) -> p c b", b=nblk)
                    [:, :, :, None].broadcast_to([128, nchunk, nblk, 32]))
            if DEQ_UPTO >= 4:
                self.V.tensor_copy(ki[:], w[:])
                self.V.tensor_copy(kf[:], ki[:])
            else:
                self.V.tensor_copy(kf[:], w[:])
            if DEQ_UPTO >= 5:
                self.V.tensor_mul(
                    wb[:].rearrange("p c (b j) -> p c b j", j=32),
                    kf[:].rearrange("p c (b j) -> p c b j", j=32),
                    m[:].rearrange("p (c b) -> p c b", b=nblk)
                    [:, :, :, None].broadcast_to([128, nchunk, nblk, 32]))
            else:
                self.V.tensor_copy(wb[:], kf[:])
            return s2, wb

        for dsrc, dmy, nm in ((self.d_wup, self.d_upT_my, "up"),
                              (self.d_wgt, self.d_gtT_my, "gt")):
            s2, wb = dequant(dsrc, 4, IC, nm)
            if DEQ_UPTO < 6:
                s2.close()
                continue
            with s2:
                wt_ = s2.enter_context(nc.sbuf_tensor(f"wdt_{nm}",
                                                      [128, IC, 128], BF16))
                for ic in range(IC):
                    g = ic % 4
                    self.wv(nc.tensor)
                    ins = nc.tensor.transpose(ptr[:, 128 * g:128 * (g + 1)],
                                              wb[:, ic, :], self.identb[:])
                    self.tse.inc(ins)
                    self.wt(nc.vector)
                    self.V.tensor_copy(
                        wt_[:, ic, :], ptr[:, 128 * g:128 * (g + 1)])
                self.wv(nc.gpsimd)
                self.gld.dinc(nc.gpsimd.dma_start(
                    out=dmy.ap().rearrange("p (c j) -> p c j", j=128),
                    in_=wt_[:]))
                nc.gpsimd.wait_ge(self.gld.h, self.gld.v)
        s2, wb = dequant(self.d_wdn, 12, DC, "dn")
        if DEQ_UPTO < 6:
            s2.close()
            return
        with s2:
            wt_ = s2.enter_context(nc.sbuf_tensor("wdt_dn",
                                                  [128, 3, DC, 128], BF16))
            for it in range(3):
                for oc in range(DC):
                    g = (it * DC + oc) % 4
                    self.wv(nc.tensor)
                    ins = nc.tensor.transpose(ptr[:, 128 * g:128 * (g + 1)],
                                              wb[:, oc, 128 * it:128 * (it + 1)],
                                              self.identb[:])
                    self.tse.inc(ins)
                    self.wt(nc.vector)
                    self.V.tensor_copy(
                        wt_[:, it, oc, :], ptr[:, 128 * g:128 * (g + 1)])
            self.wv(nc.gpsimd)
            for it in range(3):
                self.gld.dinc(nc.gpsimd.dma_start(
                    out=self.d_dnT_my.ap()[128 * it:128 * (it + 1), :]
                    .rearrange("p (oc j) -> p oc j", j=128),
                    in_=wt_[:, it, :, :]))
            nc.gpsimd.wait_ge(self.gld.h, self.gld.v)
        if SETUP_UPTO == "deq":
            return
        for my, full in ((self.d_upT_my, self.d_upT),
                         (self.d_gtT_my, self.d_gtT),
                         (self.d_dnT_my, self.d_dnT)):
            if SIM_LOCAL:
                rows = my.shape[0]
                for rr in range(NCORES):
                    self.gld.dinc(nc.gpsimd.dma_start(
                        out=full.ap()[rr * rows:(rr + 1) * rows, :],
                        in_=my.ap()))
            else:
                self.cc.inc(nc.gpsimd.collective_compute(
                    "AllGather", ALU.bypass,
                    replica_groups=[list(range(NCORES))],
                    ins=[my.ap().opt()], outs=[full.ap().opt()]))
        nc.gpsimd.wait_ge(self.gld.h, self.gld.v)
        nc.gpsimd.wait_ge(self.cc.h, self.cc.v)


    def _ln_stats(self, xt, xsqt, pR, pBc):
        nc = self.nc
        self.wa(nc.tensor)
        self.wv(nc.tensor)
        for k in range(DC):
            ins = nc.tensor.matmul(pR[:], self.cst_b[:], xt[:, k, :],
                                   start=(k == 0), stop=(k == DC - 1))
        self.tse.inc(ins)
        self.wt(nc.vector)
        self.V.tensor_copy(self.mu_r[:], pR[:])
        self.wv(nc.tensor)
        for k in range(DC):
            ins = nc.tensor.matmul(pR[:], self.cst_b[:], xsqt[:, k, :],
                                   start=(k == 0), stop=(k == DC - 1))
        self.tse.inc(ins)
        self.wt(nc.vector)
        self.V.tensor_copy(self.ms_r[:], pR[:])
        self.V.tensor_mul(self.a_r[:], self.mu_r[:], self.mu_r[:])
        self.V.tensor_sub(self.a_r[:], self.ms_r[:], self.a_r[:])
        self.V.tensor_scalar(self.a_r[:], self.a_r[:], LN_EPS, None, ALU.add)
        self.wv(nc.scalar)
        self.A.activation(self.a_r[:], self.a_r[:], AF.Sqrt,
                                          bias=0.0, scale=1.0)
        self.wa(nc.vector)
        self.V.reciprocal(self.a_r[:], self.a_r[:])
        self.wv(nc.tensor)
        ins = nc.tensor.matmul(pBc[:, 0:512], self.ones_r[:], self.mu_r[:],
                               start=True, stop=True)
        self.tse.inc(ins)
        ins = nc.tensor.matmul(pBc[:, 512:1024], self.ones_r[:], self.a_r[:],
                               start=True, stop=True)
        self.tse.inc(ins)


    def a1(self, b):
        nc = self.nc
        ctx = ExitStack()
        sb = lambda name, shape, dt=F32: ctx.enter_context(
            nc.sbuf_tensor(name, shape, dt))
        ps = lambda name, shape, dt=F32: ctx.enter_context(
            nc.psum_tensor(name, shape, dt))
        with ctx:
            wslot = sb(f"a1_wslot{b}", [128, DC, D], BF16)
            self.load_weight(self.d_Wz, wslot)
            xc = sb(f"a1_xc{b}", [128, DC, 512], BF16)
            xsq = sb(f"a1_xsq{b}", [128, DC, 512], BF16)
            hc = sb(f"a1_hc{b}", [128, DC, 512], BF16)
            d1 = sb(f"a1_d1{b}", [128, DC, 512], BF16)
            d2 = sb(f"a1_d2{b}", [128, 512])
            zc = sb(f"a1_zc{b}", [128, DC, 512], BF16)
            pR = ps(f"a1_pR{b}", [1, 512])
            pBc = ps(f"a1_pBc{b}", [128, 1024])
            pZ0 = ps(f"a1_pZ0{b}", [128, 512])
            pZ1 = ps(f"a1_pZ1{b}", [128, 512])
            for ch in range(4):
                t0 = 512 * ch
                self.guard_load(nc.gpsimd)
                self.gld.dinc(nc.gpsimd.dma_start(
                    out=xc[:],
                    in_=self.d_x.ap()[b].rearrange("(k p) t -> p k t", p=128)
                    [:, :, t0:t0 + 512]))
                self.wgld(nc.scalar)
                self.A.activation(xsq[:], xc[:], AF.Square,
                                                  bias=0.0, scale=1.0)
                self._ln_stats(xc, xsq, pR, pBc)
                self.wt(nc.vector)
                self.V.tensor_sub(
                    d1[:], xc[:],
                    pBc[:, 0:512][:, None, :].broadcast_to([128, DC, 512]))
                for k in range(DC):
                    self.wa(nc.vector)
                    self.V.scalar_tensor_tensor(
                        d2[:], d1[:, k, :], self.mod_t[:, k, 1, b:b + 1],
                        pBc[:, 512:1024], ALU.mult, ALU.mult)
                    self.wv(nc.scalar)
                    self.A.activation(
                        hc[:, k, :], d2[:], AF.Identity,
                        bias=self.mod_t[:, k, 0, b:b + 1], scale=1.0)
                self.wa(nc.tensor)
                ztok = [self.tok(), self.tok()]
                for j in range(DC):
                    jp = j % 2
                    pz = pZ0 if jp == 0 else pZ1
                    self.wtok(nc.tensor, ztok[jp])
                    for k in range(DC):
                        ins = nc.tensor.matmul(
                            pz[:], wslot[:, k, 128 * j:128 * (j + 1)],
                            hc[:, k, :], start=(k == 0), stop=(k == DC - 1))
                    self.tse.inc(ins)
                    eng, E = ((nc.vector, self.V) if jp == 0
                              else (nc.scalar, self.A))
                    self.wt(eng)
                    if jp == 0:
                        E.tensor_scalar(zc[:, j, :], pz[:],
                                        self.bz_t[:, j:j + 1], None, ALU.add)
                    else:
                        E.activation(zc[:, j, :], pz[:], AF.Identity,
                                     bias=self.bz_t[:, j:j + 1], scale=1.0)
                    ztok[jp] = self.tok()
                self.wa(nc.scalar)
                self.wv(nc.scalar)
                self.A.activation(xsq[:], zc[:], AF.Square,
                                                  bias=0.0, scale=1.0)
                self._ln_stats(zc, xsq, pR, pBc)
                self.wt(nc.vector)
                self.V.tensor_sub(
                    d1[:], zc[:],
                    pBc[:, 0:512][:, None, :].broadcast_to([128, DC, 512]))
                for k in range(DC):
                    self.wa(nc.vector)
                    self.V.scalar_tensor_tensor(
                        d2[:], d1[:, k, :], self.g_t[:, k:k + 1],
                        pBc[:, 512:1024], ALU.mult, ALU.mult)
                    self.wv(nc.scalar)
                    self.A.activation(
                        self.zln[:, k, t0:t0 + 512], d2[:], AF.Identity,
                        bias=self.lb_t[:, k:k + 1], scale=1.0)


    def a2(self, b):
        nc = self.nc
        ctx = ExitStack()
        sb = lambda name, shape, dt=F32: ctx.enter_context(
            nc.sbuf_tensor(name, shape, dt))
        ps = lambda name, shape, dt=F32: ctx.enter_context(
            nc.psum_tensor(name, shape, dt))
        with ctx:
            hvx = ExitStack()
            wslot = hvx.enter_context(
                nc.sbuf_tensor(f"a2_wslot{b}", [128, DC, D], BF16))
            self.load_weight(self.d_Wh, wslot)
            pG = [ps(f"a2_pG{i}{b}", [128, 1024]) for i in range(2)]
            gtokv = [self.tok(), self.tok()]
            pOut = ps(f"a2_pOut{b}", [128, 2048])
            otok = self.tok()

            def grp(i):
                self.wtok(nc.tensor, gtokv[i])
                return pG[i]

            def evac_eng(i):
                return (nc.vector, self.V) if i % 2 == 0 else (nc.scalar, self.A)

            for tt in range(16):
                g = tt % 2
                pz = grp(g)
                for fc in range(2):
                    ins = nc.tensor.matmul(
                        pz[:, 512 * fc:512 * (fc + 1)], self.ones_r[:],
                        self.bh_row[:, 512 * fc:512 * (fc + 1)],
                        start=True, stop=False)
                    for k in range(DC):
                        ins = nc.tensor.matmul(
                            pz[:, 512 * fc:512 * (fc + 1)],
                            self.zln[:, k, 128 * tt:128 * (tt + 1)],
                            wslot[:, k, 512 * fc:512 * (fc + 1)],
                            start=False, stop=(k == DC - 1))
                self.tse.inc(ins)
                eng, E = evac_eng(tt)
                self.wt(eng)
                if tt % 2 == 0:
                    E.tensor_copy(self.hv[:, tt, :], pz[:])
                else:
                    E.activation(self.hv[:, tt, :], pz[:], AF.Identity,
                                 bias=0.0, scale=1.0)
                gtokv[g] = self.tok()
            if "Hv" in self.taps and b == 0:
                self.barrier()
                self._tap("Hv", self.hv[:].rearrange("p t f -> p (t f)"),
                          [128, 16 * D], BF16)
            hvx.close()
            wqd = sb(f"a2_wqd{b}", [128, 2, DC, 128], BF16)
            wrd = sb(f"a2_wrd{b}", [128, 2, DC, 128], BF16)
            wphid = sb(f"a2_wphid{b}", [64, 128], BF16)
            qd = sb(f"a2_qd{b}", [128, 2, 2, 2048], BF16)
            rd = sb(f"a2_rd{b}", [128, 2, 2, 2048], BF16)
            trig = sb(f"a2_trig{b}", [128, 2048])
            ki = sb(f"a2_ki{b}", [128, 2048], I32)
            cs = sb(f"a2_cs{b}", [128, 2048], BF16)
            rctm = sb(f"a2_rctm{b}", [128, 16, 128], BF16)
            msb = sb(f"a2_msb{b}", [128, 128], BF16)
            mqc = sb(f"a2_mqc{b}", [128, 2048], BF16)
            ysb = sb(f"a2_ysb{b}", [128, 16])
            itp = sb(f"a2_itp{b}", [16, 128])
            invr = sb(f"a2_invr{b}", [1, 2, 2048], BF16)
            invbc = sb(f"a2_invbc{b}", [128, 2, 2048], BF16)
            sct = sb(f"a2_sct{b}", [128, 2048], BF16)
            self.guard_load(nc.gpsimd)
            self.gld.dinc(nc.gpsimd.dma_start(out=wphid[:, 0:64],
                                              in_=self.d_Wphi.ap()))
            self.gld.dinc(nc.gpsimd.dma_start(out=wphid[:, 64:128],
                                              in_=self.d_Wphi.ap()))

            def stage_proj(p):
                pr = p % 2
                hA, hB = 2 * p, 2 * p + 1
                self.wt(nc.gpsimd)
                nc.gpsimd.wait_ge(self.gld.h, self.gld.v)
                for (dst, dram) in ((wqd, self.d_Wq), (wrd, self.d_Wr)):
                    for hi, h in enumerate((hA, hB)):
                        srcw = dram.ap().rearrange("(k p) j -> p k j", p=128)
                        srcw = srcw[:, :, HD * h:HD * (h + 1)]
                        self.gld.dinc(nc.gpsimd.dma_start(
                            out=dst[:, hi, :, 0:64], in_=srcw))
                        self.gld.dinc(nc.gpsimd.dma_start(
                            out=dst[:, hi, :, 64:128], in_=srcw))
                self.wgld(nc.tensor)
                for ti, (dst, wsl, bias) in enumerate(
                        ((qd, wqd, self.bq_dup), (rd, wrd, self.br_dup))):
                    for hi, h in enumerate((hA, hB)):
                        for g in range(2):
                            pz = grp(g)
                            for half in range(2):
                                qc = 2 * g + half
                                for k in range(DC):
                                    ins = nc.tensor.matmul(
                                        pz[:, 512 * half:512 * (half + 1)],
                                        wsl[:, hi, k, :],
                                        self.zln[:, k,
                                                 512 * qc:512 * (qc + 1)],
                                        start=(k == 0), stop=(k == DC - 1))
                            self.tse.inc(ins)
                            eng, E = evac_eng(2 * ti + hi)
                            self.wt(eng)
                            if (2 * ti + hi) % 2 == 0:
                                E.tensor_scalar(
                                    dst[:, pr, hi, 1024 * g:1024 * (g + 1)],
                                    pz[:], bias[:, h:h + 1], None, ALU.add)
                            else:
                                E.activation(
                                    dst[:, pr, hi, 1024 * g:1024 * (g + 1)],
                                    pz[:], AF.Identity,
                                    bias=bias[:, h:h + 1], scale=1.0)
                            gtokv[g] = self.tok()

            def stage_trig(p):
                pr = p % 2
                for hi in range(2):
                    for ti, srcd in enumerate((qd, rd)):
                        for g in range(2):
                            pz = grp(g)
                            for half in range(2):
                                qc = 2 * g + half
                                ins = nc.tensor.matmul(
                                    pz[:, 512 * half:512 * (half + 1)],
                                    wphid[:, :],
                                    srcd[0:64, pr, hi,
                                         512 * qc:512 * (qc + 1)],
                                    start=True, stop=True)
                                self.tse.inc(ins)
                            self.wt(nc.scalar)
                            self.A.activation(
                                trig[:, 1024 * g:1024 * (g + 1)], pz[:],
                                AF.Identity, bias=self.bphi_v[:], scale=1.0)
                            gtokv[g] = self.tok()
                        self.wa(nc.vector)
                        self.V.tensor_scalar(ki[:].bitcast(F32), trig[:],
                                             INV_2PI, None, ALU.mult)
                        self.V.tensor_copy(ki[:], ki[:].bitcast(F32))
                        self.V.scalar_tensor_tensor(
                            trig[:], ki[:], -TWO_PI, trig[:], ALU.mult,
                            ALU.add)
                        self.wv(nc.gpsimd)
                        self.G.tensor_scalar(
                            trig[:], trig[:], CLAMP, -CLAMP, ALU.min, ALU.max)
                        self.wg(nc.scalar)
                        self.A.activation(
                            cs[:], trig[:], AF.Sin, bias=0.0, scale=1.0)
                        self.wa(nc.vector)
                        self.V.tensor_mul(srcd[:, pr, hi, :],
                                          srcd[:, pr, hi, :], cs[:])

            def stage_norm(p):
                pr = p % 2
                tr_views = [pG[0][:].bitcast(BF16), pG[1][:].bitcast(BF16)]
                self.wv(nc.tensor)
                for hi in range(2):
                    for kt in range(16):
                        g = kt % 2
                        v = tr_views[g]
                        self.wtok(nc.tensor, gtokv[g])
                        ins = nc.tensor.transpose(
                            v[:, 0:128],
                            rd[:, pr, hi, 128 * kt:128 * (kt + 1)],
                            self.identb[:])
                        self.tse.inc(ins)
                        self.wt(nc.vector)
                        self.V.tensor_copy(rctm[:, kt, :], v[:, 0:128])
                        gtokv[g] = self.tok()
                    self.wv(nc.tensor)
                    pzM = grp(0)
                    for kt in range(16):
                        ins = nc.tensor.matmul(pzM[:, 0:128],
                                               rctm[:, kt, :],
                                               rctm[:, kt, :],
                                               start=(kt == 0),
                                               stop=(kt == 15))
                    self.tse.inc(ins)
                    self.wt(nc.vector)
                    self.V.tensor_copy(msb[:], pzM[:, 0:128])
                    gtokv[0] = self.tok()
                    for g in range(2):
                        pz = grp(g)
                        for half in range(2):
                            qc = 2 * g + half
                            ins = nc.tensor.matmul(
                                pz[:, 512 * half:512 * (half + 1)], msb[:],
                                qd[:, pr, hi, 512 * qc:512 * (qc + 1)],
                                start=True, stop=True)
                            self.tse.inc(ins)
                        eng, E = evac_eng(g)
                        self.wt(eng)
                        if g % 2 == 0:
                            E.tensor_copy(mqc[:, 1024 * g:1024 * (g + 1)],
                                          pz[:])
                        else:
                            E.activation(mqc[:, 1024 * g:1024 * (g + 1)],
                                         pz[:], AF.Identity, bias=0.0,
                                         scale=1.0)
                        gtokv[g] = self.tok()
                    self.wa(nc.vector)
                    self.V.tensor_mul(mqc[:], mqc[:], qd[:, pr, hi, :])
                    self.wv(nc.tensor)
                    pzY = grp(1)
                    for qt in range(16):
                        ins = nc.tensor.matmul(
                            pzY[:, qt:qt + 1],
                            mqc[:, 128 * qt:128 * (qt + 1)],
                            self.ones_cb[:], start=True, stop=True)
                        self.tse.inc(ins)
                    self.wt(nc.vector)
                    self.V.tensor_scalar(
                        ysb[:], pzY[:, 0:16], 1e-24, None, ALU.max)
                    gtokv[1] = self.tok()
                    self.wv(nc.scalar)
                    self.A.activation(ysb[:], ysb[:], AF.Sqrt,
                                      bias=0.0, scale=1.0)
                    self.wa(nc.vector)
                    self.V.reciprocal(ysb[:], ysb[:])
                    pzT = grp(0)
                    self.wv(nc.tensor)
                    ins = nc.tensor.transpose(pzT[0:16, 0:128], ysb[:],
                                              self.identf[:])
                    self.tse.inc(ins)
                    self.wt(nc.vector)
                    self.wgld(nc.vector)
                    self.V.tensor_copy(itp[:], pzT[0:16, 0:128])
                    gtokv[0] = self.tok()
                    self.wv(nc.gpsimd)
                    self.gld.dinc(nc.gpsimd.dma_start(out=invr[:, hi, :],
                                                      in_=itp[:]))
                self.wgld(nc.tensor)
                for g in range(2):
                    pz = grp(g)
                    for half in range(2):
                        qc = 2 * g + half
                        ins = nc.tensor.matmul(
                            pz[0:64, 512 * half:512 * (half + 1)],
                            self.ones_rb[:, 0:64],
                            invr[:, 0, 512 * qc:512 * (qc + 1)],
                            start=True, stop=True)
                        self.tse.inc(ins)
                        ins = nc.tensor.matmul(
                            pz[64:128, 512 * half:512 * (half + 1)],
                            self.ones_rb[:, 0:64],
                            invr[:, 1, 512 * qc:512 * (qc + 1)],
                            start=True, stop=True, tile_position=(0, 64))
                        self.tse.inc(ins)
                    eng, E = evac_eng(g)
                    self.wt(eng)
                    if g % 2 == 0:
                        E.tensor_copy(invbc[:, pr, 1024 * g:1024 * (g + 1)],
                                      pz[:])
                    else:
                        E.activation(invbc[:, pr, 1024 * g:1024 * (g + 1)],
                                     pz[:], AF.Identity, bias=0.0, scale=1.0)
                    gtokv[g] = self.tok()

            def stage_scores(p, otok):
                pr = p % 2
                hA, hB = 2 * p, 2 * p + 1
                stok = [self.tok()] * 2
                for hi, h in enumerate((hA, hB)):
                    for kt in range(16):
                        for g in range(2):
                            pz = grp(g)
                            for half in range(2):
                                qc = 2 * g + half
                                ins = nc.tensor.matmul(
                                    pz[:, 512 * half:512 * (half + 1)],
                                    rd[:, pr, hi, 128 * kt:128 * (kt + 1)],
                                    qd[:, pr, hi, 512 * qc:512 * (qc + 1)],
                                    start=True, stop=True)
                                self.tse.inc(ins)
                            eng, E = evac_eng(g + kt)
                            self.wt(eng)
                            if (g + kt) % 2 == 0:
                                E.tensor_copy(
                                    sct[:, 1024 * g:1024 * (g + 1)], pz[:])
                            else:
                                E.activation(
                                    sct[:, 1024 * g:1024 * (g + 1)],
                                    pz[:], AF.Identity, bias=0.0, scale=1.0)
                            gtokv[g] = self.tok()
                            stok[g] = gtokv[g]
                        for qc in range(4):
                            self.wtok(nc.tensor, stok[qc // 2])
                            self.wtok(nc.tensor, otok)
                            ins = nc.tensor.matmul(
                                pOut[64 * hi:64 * (hi + 1),
                                     512 * qc:512 * (qc + 1)],
                                self.hv[:, kt, HD * h:HD * (h + 1)],
                                sct[:, 512 * qc:512 * (qc + 1)],
                                start=(kt == 0), stop=(kt == 15),
                                tile_position=(0, 64 * hi))
                            self.tse.inc(ins)
                self.wt(nc.vector)
                self.wa(nc.vector)
                self.V.tensor_mul(self.outfm[:, p, :], pOut[:],
                                  invbc[:, pr, :])
                return self.tok()

            stage_proj(0)
            stage_trig(0)
            stage_norm(0)
            otok = self.tok()
            for p in range(NPAIR):
                if p + 1 < NPAIR:
                    stage_proj(p + 1)
                otok = stage_scores(p, otok)
                if p + 1 < NPAIR:
                    stage_trig(p + 1)
                    stage_norm(p + 1)
            if "Qc" in self.taps and b == 0:
                self.barrier()
                self._tap("Qc", qd[:, (NPAIR - 1) % 2, 0, :], [128, 2048], BF16)
                self._tap("Rc", rd[:, (NPAIR - 1) % 2, 0, :], [128, 2048], BF16)
                self._tap("invbc", invbc[:, (NPAIR - 1) % 2, :], [128, 2048], BF16)
            if "outfm" in self.taps and b == 0:
                self.barrier()
                self._tap("outfm", self.outfm[:].rearrange("p c q -> p (c q)"),
                          [128, DC * 2048], BF16)


    def wo_epilogue(self, b):
        nc = self.nc
        ctx = ExitStack()
        sb = lambda name, shape, dt=F32: ctx.enter_context(
            nc.sbuf_tensor(name, shape, dt))
        ps = lambda name, shape, dt=F32: ctx.enter_context(
            nc.psum_tensor(name, shape, dt))
        with ctx:
            wslot = sb(f"wo_wslot{b}", [128, DC, D], BF16)
            self.load_weight(self.d_Wo, wslot)
            pA = ps(f"wo_pA{b}", [128, 512])
            pB = ps(f"wo_pB{b}", [128, 512])
            pR = ps(f"wo_pR{b}", [1, 512])
            pBc = ps(f"wo_pBc{b}", [128, 1024])
            pTr = ps(f"wo_pTr{b}", [128, 512])
            x2c = sb(f"wo_x2c{b}", [128, DC, 512])
            x2b = sb(f"wo_x2b{b}", [128, DC, 512], BF16)
            x2sq = sb(f"wo_x2sq{b}", [128, DC, 512], BF16)
            xfc = sb(f"wo_xfc{b}", [128, DC, 512])
            h2c = sb(f"wo_h2c{b}", [128, DC, 512], BF16)
            d1 = sb(f"wo_d1{b}", [128, DC, 512], BF16)
            d2 = sb(f"wo_d2{b}", [128, 512])
            x2t = sb(f"wo_x2t{b}", [128, 4, 128])
            for qc in range(4):
                t0 = 512 * qc
                gt0 = b * S + t0
                self.guard_load(nc.sync)
                self.sld.dinc(nc.sync.dma_start(
                    out=xfc[:],
                    in_=self.d_x.ap()[b].rearrange("(k p) t -> p k t", p=128)
                    [:, :, t0:t0 + 512]))
                wtok2 = [self.tok(), self.tok()]
                for e in range(DC):
                    ep = e % 2
                    pz = pA if ep == 0 else pB
                    self.wtok(nc.tensor, wtok2[ep])
                    for k in range(DC):
                        ins = nc.tensor.matmul(
                            pz[:], wslot[:, k, 128 * e:128 * (e + 1)],
                            self.outfm[:, k, t0:t0 + 512],
                            start=(k == 0), stop=(k == DC - 1))
                    self.tse.inc(ins)
                    self.wt(nc.scalar)
                    self.A.activation(
                        x2c[:, e, :], pz[:], AF.Identity,
                        bias=self.gabo_t[:, e, b:b + 1],
                        scale=self.mod_t[:, e, 2, b:b + 1])
                    wtok2[ep] = self.tok()
                    self.wsld(nc.vector)
                    self.wa(nc.vector)
                    self.V.tensor_add(x2c[:, e, :], x2c[:, e, :],
                                         xfc[:, e, :])
                    self.V.tensor_copy(x2b[:, e, :],
                                                       x2c[:, e, :])
                    self.wv(nc.sync)
                    self.sld.dinc(nc.sync.dma_start(
                        out=self.d_x2f[128 * e:128 * (e + 1), gt0:gt0 + 512],
                        in_=x2c[:, e, :]))
                    self.wv(nc.tensor)
                    self.wgld(nc.tensor)
                    for j in range(4):
                        ins = nc.tensor.transpose(
                            pTr[:, 128 * j:128 * (j + 1)],
                            x2c[:, e, 128 * j:128 * (j + 1)], self.identf[:])
                    self.tse.inc(ins)
                    self.wt(nc.vector)
                    self.V.tensor_copy(
                        x2t[:].rearrange("p j f -> p (j f)"), pTr[:])
                    self.wv(nc.gpsimd)
                    self.gld.dinc(nc.gpsimd.dma_start(
                        out=self.d_x2t.ap()
                        [gt0:gt0 + 512, 128 * e:128 * (e + 1)]
                        .rearrange("(j p) f -> p j f", p=128),
                        in_=x2t[:]))
                self.wv(nc.scalar)
                self.A.activation(x2sq[:], x2b[:], AF.Square,
                                                  bias=0.0, scale=1.0)
                self._ln_stats(x2b, x2sq, pR, pBc)
                self.wt(nc.vector)
                self.wgld(nc.vector)
                self.V.tensor_sub(
                    d1[:], x2b[:],
                    pBc[:, 0:512][:, None, :].broadcast_to([128, DC, 512]))
                for k in range(DC):
                    self.wa(nc.vector)
                    self.V.scalar_tensor_tensor(
                        d2[:], d1[:, k, :], self.mod_t[:, k, 4, b:b + 1],
                        pBc[:, 512:1024], ALU.mult, ALU.mult)
                    self.wv(nc.scalar)
                    self.A.activation(
                        h2c[:, k, :], d2[:], AF.Identity,
                        bias=self.mod_t[:, k, 3, b:b + 1], scale=1.0)
                self.wa(nc.vector)
                self.V.tensor_reduce(
                    d2[:, 0:1], h2c[:].rearrange("p k t -> p (k t)"), AX.X,
                    ALU.max, apply_absolute_value=True)
                self.V.tensor_max(self.h2max[:],
                                                  self.h2max[:], d2[:, 0:1])
                self.wv(nc.gpsimd)
                self.gld.dinc(nc.gpsimd.dma_start(
                    out=self.d_h2[:, gt0:gt0 + 512]
                    .rearrange("(k p) t -> p k t", p=128),
                    in_=h2c[:]))
                self.wgld(nc.scalar)


    _gs_n = 0

    def global_scale(self, idx, locmax_col, nvals=1):
        nc = self.nc
        ctx = ExitStack()
        with ctx:
            ps = lambda name, shape, dt=F32: ctx.enter_context(
                nc.psum_tensor(name, shape, dt))
            sbl = lambda name, shape, dt=F32: ctx.enter_context(
                nc.sbuf_tensor(name, shape, dt))
            Prog._gs_n += 1
            u = Prog._gs_n
            pT = ps(f"gs_p{idx}_{u}", [128, 128])
            mrow = sbl(f"gs_m{idx}_{u}", [1, 64])
            m2 = sbl(f"gs_m2{idx}_{u}", [2, 1])
            srow = sbl(f"gs_s{idx}_{u}", [1, 16])
            self.wv(nc.tensor)
            self.wa(nc.tensor)
            ins = nc.tensor.transpose(pT[0:nvals, 0:128],
                                      locmax_col[:, 0:nvals], self.identf[:])
            self.tse.inc(ins)
            self.wt(nc.vector)
            self.V.memset(mrow[:], 0.0)
            self.V.tensor_reduce(m2[0:nvals, :], pT[0:nvals, 0:128], AX.X,
                                 ALU.max)
            self.wv(nc.gpsimd)
            self.gld.dinc(nc.gpsimd.dma_start(out=mrow[0:1, 0:nvals],
                                              in_=m2[0:nvals, :]))
            nc.gpsimd.wait_ge(self.gld.h, self.gld.v)
            self.gld.dinc(nc.gpsimd.dma_start(out=self.d_arin[idx].ap(),
                                              in_=mrow[:]))
            nc.gpsimd.wait_ge(self.gld.h, self.gld.v)
            if SIM_LOCAL:
                self.gld.dinc(nc.gpsimd.dma_start(
                    out=self.d_arout[idx].ap(), in_=self.d_arin[idx].ap()))
                nc.gpsimd.wait_ge(self.gld.h, self.gld.v)
            else:
                self.cc.inc(nc.gpsimd.collective_compute(
                    "AllReduce", ALU.max, replica_groups=[list(range(NCORES))],
                    ins=[self.d_arin[idx].ap().opt()],
                    outs=[self.d_arout[idx].ap().opt()]))
                nc.gpsimd.wait_ge(self.cc.h, self.cc.v)
            self.gld.dinc(nc.gpsimd.dma_start(out=mrow[:],
                                              in_=self.d_arout[idx].ap()))
            self.wgld(nc.vector)
            self.V.memset(srow[:], 1.0)
            self.V.tensor_scalar(srow[:, 0:nvals], mrow[:, 0:nvals],
                                    1.0 / QMAX, 1e-12, ALU.mult, ALU.max)
            self.V.reciprocal(srow[:, 8:8 + nvals],
                                              srow[:, 0:nvals])
            self.wv(nc.tensor)
            ins = nc.tensor.matmul(pT[:, 0:16], self.ones_r[:], srow[:],
                                   start=True, stop=True)
            self.tse.inc(ins)
            self.wt(nc.vector)
            for i in range(nvals):
                self.V.tensor_copy(
                    self.sc_col[:, 2 * (idx + i):2 * (idx + i) + 1],
                    pT[:, i:i + 1])
                self.V.tensor_copy(
                    self.sc_col[:, 2 * (idx + i) + 1:2 * (idx + i) + 2],
                    pT[:, 8 + i:8 + i + 1])


    def ffn(self):
        nc = self.nc
        ctx = ExitStack()
        sb = lambda name, shape, dt=F32: ctx.enter_context(
            nc.sbuf_tensor(name, shape, dt))
        ps = lambda name, shape, dt=F32: ctx.enter_context(
            nc.psum_tensor(name, shape, dt))
        mx2 = self.ctx.enter_context(nc.sbuf_tensor("ffn_mx2", [128, 2], F32))
        with ctx:
            hq = sb("ffn_hq", [128, DC, T], BF16)
            t1 = sb("ffn_t1", [128, 2048])
            ki = sb("ffn_ki", [128, 2048], I32)
            hb = sb("ffn_hb", [128, DC, 512], BF16)
            wu = [sb(f"ffn_wu{i}", [128, DC, 128], BF16) for i in range(2)]
            wg = [sb(f"ffn_wg{i}", [128, DC, 128], BF16) for i in range(2)]

            def load_slices(j, par):
                nc.sync.wait_ge(self.tse.h, self.tse.v)
                self.sld.dinc(nc.sync.dma_start(
                    out=wu[par][:],
                    in_=self.d_upT.ap().rearrange("(k p) i -> p k i", p=128)
                    [:, :, 128 * j:128 * (j + 1)]))
                self.sld.dinc(nc.sync.dma_start(
                    out=wg[par][:],
                    in_=self.d_gtT.ap().rearrange("(k p) i -> p k i", p=128)
                    [:, :, 128 * j:128 * (j + 1)]))

            for ch in range(8):
                t0 = 512 * ch
                self.guard_load(nc.gpsimd)
                self.gld.dinc(nc.gpsimd.dma_start(
                    out=hb[:], in_=self.d_h2[:, t0:t0 + 512]
                    .rearrange("(k p) t -> p k t", p=128)))
                self.wgld(nc.vector)
                for kg in range(2):
                    sl = slice(kg * 4, kg * 4 + 4)
                    hv_ = hb[:, sl, :].rearrange("p k t -> p (k t)")
                    t1v = t1[:].rearrange("p (k t) -> p k t", t=512)
                    kiv = ki[:].rearrange("p (k t) -> p k t", t=512)
                    self.V.tensor_scalar(t1[:], hv_, self.sc_col[:, 1:2],
                                            None, ALU.mult)
                    self.V.tensor_copy(ki[:], t1[:])
                    self.V.tensor_scalar(
                        hq[:, sl, t0:t0 + 512],
                        kiv, self.sc_col[:, 0:1], None, ALU.mult)
            if "hq" in self.taps:
                self.barrier()
                self._tap("hq", hq[:].rearrange("p k t -> p (k t)"),
                          [128, DC * T], BF16)
            if self.upto == "hq":
                self.barrier()
                return
            pU0 = ps("ffn_pU0", [128, 512])
            pU1 = ps("ffn_pU1", [128, 512])
            pG0 = ps("ffn_pG0", [128, 512])
            pG1 = ps("ffn_pG1", [128, 512])
            upc = sb("ffn_upc", [128, 4, 512], BF16)
            gtc = sb("ffn_gtc", [128, 4, 512], BF16)
            red = sb("ffn_red", [128, 1])
            self.V.memset(mx2[:], 0.0)
            load_slices(0, 0)
            utok = [self.tok(), self.tok()]
            gtok = [self.tok(), self.tok()]
            sttok = [self.tok()] * 4
            for ic in range(IC):
                wpar = ic % 2
                self.wsld(nc.tensor)
                if ic + 1 < IC:
                    load_slices(ic + 1, 1 - wpar)
                for tc in range(8):
                    par = tc % 2
                    pu = pU0 if par == 0 else pU1
                    pg = pG0 if par == 0 else pG1
                    self.wtok(nc.tensor, utok[par])
                    for k in range(DC):
                        ins = nc.tensor.matmul(
                            pu[:], wu[wpar][:, k, :],
                            hq[:, k, 512 * tc:512 * (tc + 1)],
                            start=(k == 0), stop=(k == DC - 1))
                    self.tse.inc(ins)
                    self.wtok(nc.tensor, gtok[par])
                    for k in range(DC):
                        ins = nc.tensor.matmul(
                            pg[:], wg[wpar][:, k, :],
                            hq[:, k, 512 * tc:512 * (tc + 1)],
                            start=(k == 0), stop=(k == DC - 1))
                    self.tse.inc(ins)
                    slot = (ic * 8 + tc) % 4
                    self.wt(nc.vector)
                    self.wtok(nc.vector, sttok[slot])
                    self.V.tensor_copy(upc[:, slot, :], pu[:])
                    utok[par] = self.tok()
                    self.wt(nc.scalar)
                    self.wtok(nc.scalar, sttok[slot])
                    self.A.activation(
                        gtc[:, slot, :], pg[:], AF.Identity, bias=0.0,
                        scale=1.0)
                    gtok[par] = self.tok()
                    self.wa(nc.vector)
                    self.V.tensor_reduce(red[:], upc[:, slot, :], AX.X,
                                            ALU.max, apply_absolute_value=True)
                    self.V.tensor_max(mx2[:, 0:1], mx2[:, 0:1], red[:])
                    self.V.tensor_reduce(red[:], gtc[:, slot, :], AX.X,
                                            ALU.max, apply_absolute_value=True)
                    self.V.tensor_max(mx2[:, 1:2],
                                                      mx2[:, 1:2], red[:])
                    self.wv(nc.sync)
                    self.wa(nc.sync)
                    self.sld.dinc(nc.sync.dma_start(
                        out=self.d_up[128 * ic:128 * (ic + 1),
                                      512 * tc:512 * (tc + 1)],
                        in_=upc[:, slot, :]))
                    self.sld.dinc(nc.sync.dma_start(
                        out=self.d_gt[128 * ic:128 * (ic + 1),
                                      512 * tc:512 * (tc + 1)],
                        in_=gtc[:, slot, :]))
                    if slot == 3:
                        self.scommit()
                        for s_ in range(4):
                            sttok[s_] = (0, 0, 0, self.scm.v)
        self.barrier()
        if self.upto == "s1":
            if "up" in self.taps:
                self._tap("up", self.d_up.ap(), [I3, T], BF16)
            if "gt" in self.taps:
                self._tap("gt", self.d_gt.ap(), [I3, T], BF16)
            return
        self.global_scale(1, mx2[:], nvals=2)
        self.barrier()
        ctx = ExitStack()
        sb = lambda name, shape, dt=F32: ctx.enter_context(
            nc.sbuf_tensor(name, shape, dt))
        with ctx:
            gc = sb("ffn_gc", [128, 2, T], BF16)
            sg = sb("ffn_sg", [128, 2, T], BF16)
            gq = sb("ffn_gq", [128, 2, T], BF16)
            t2 = sb("ffn_t2", [128, 2, T])
            ki2 = sb("ffn_ki2", [128, 2, T], I32)
            red = sb("ffn_red2", [128, 1])
            self.V.memset(mx2[:, 0:1], 0.0)
            ltok = [self.tok(), self.tok()]
            for ic in range(IC):
                pr = ic % 2
                self.wtok(nc.sync, ltok[pr])
                self.sld.dinc(nc.sync.dma_start(
                    out=gc[:, pr, :],
                    in_=self.d_gt[128 * ic:128 * (ic + 1), :]))
                self.wsld(nc.scalar)
                self.wv(nc.scalar)
                self.A.activation(t2[:, pr, :], gc[:, pr, :], AF.Identity,
                                  bias=0.0, scale=self.sc_col[:, 5:6])
                self.wa(nc.vector)
                self.V.tensor_copy(ki2[:, pr, :], t2[:, pr, :])
                self.V.tensor_scalar(
                    gq[:, pr, :], ki2[:, pr, :], self.sc_col[:, 4:5],
                    None, ALU.mult)
                self.wv(nc.scalar)
                self.A.activation(sg[:, pr, :], gq[:, pr, :], AF.Sigmoid,
                                  bias=0.0, scale=1.0)
                self.A.activation(t2[:, pr, :], sg[:, pr, :], AF.Identity,
                                  bias=0.0, scale=65536.0)
                self.wa(nc.vector)
                self.V.tensor_copy(ki2[:, pr, :], t2[:, pr, :])
                self.V.tensor_scalar(sg[:, pr, :], ki2[:, pr, :], 65535.0,
                                     1.0 / 65536.0, ALU.min, ALU.mult)
                self.V.tensor_mul(gc[:, pr, :], gq[:, pr, :], sg[:, pr, :])
                self.V.tensor_reduce(red[:], gc[:, pr, :], AX.X, ALU.max,
                                     apply_absolute_value=True)
                self.V.tensor_max(mx2[:, 0:1], mx2[:, 0:1], red[:])
                self.wv(nc.sync)
                self.sld.dinc(nc.sync.dma_start(
                    out=self.d_g2[128 * ic:128 * (ic + 1), :],
                    in_=gc[:, pr, :]))
                self.scommit()
                ltok[pr] = (0, 0, 0, self.scm.v)
            self.barrier()
            if self.upto == "s2":
                if "g2" in self.taps:
                    self._tap("g2", self.d_g2.ap(), [I3, T], BF16)
                return
            self.global_scale(3, mx2[:, 0:1], nvals=1)
            self.barrier()
            self.V.memset(mx2[:, 0:1], 0.0)
            ltok = [self.tok(), self.tok()]
            for ic in range(IC):
                pr = ic % 2
                self.wtok(nc.sync, ltok[pr])
                self.sld.dinc(nc.sync.dma_start(
                    out=gc[:, pr, :],
                    in_=self.d_g2[128 * ic:128 * (ic + 1), :]))
                self.sld.dinc(nc.sync.dma_start(
                    out=sg[:, pr, :],
                    in_=self.d_up[128 * ic:128 * (ic + 1), :]))
                self.wsld(nc.scalar)
                self.wv(nc.scalar)
                self.A.activation(t2[:, pr, :], sg[:, pr, :], AF.Identity,
                                  bias=0.0, scale=self.sc_col[:, 3:4])
                self.wa(nc.vector)
                self.V.tensor_copy(ki2[:, pr, :], t2[:, pr, :])
                self.V.tensor_scalar(
                    gq[:, pr, :], ki2[:, pr, :], self.sc_col[:, 2:3],
                    None, ALU.mult)
                self.wv(nc.scalar)
                self.A.activation(t2[:, pr, :], gc[:, pr, :], AF.Identity,
                                  bias=0.0, scale=self.sc_col[:, 7:8])
                self.wa(nc.vector)
                self.V.tensor_copy(ki2[:, pr, :], t2[:, pr, :])
                self.V.tensor_scalar(
                    sg[:, pr, :], ki2[:, pr, :], self.sc_col[:, 6:7],
                    None, ALU.mult)
                self.V.tensor_mul(gc[:, pr, :], gq[:, pr, :], sg[:, pr, :])
                self.V.tensor_reduce(red[:], gc[:, pr, :], AX.X, ALU.max,
                                     apply_absolute_value=True)
                self.V.tensor_max(mx2[:, 0:1], mx2[:, 0:1], red[:])
                self.wv(nc.sync)
                self.sld.dinc(nc.sync.dma_start(
                    out=self.d_o[128 * ic:128 * (ic + 1), :],
                    in_=gc[:, pr, :]))
                self.scommit()
                ltok[pr] = (0, 0, 0, self.scm.v)
        self.barrier()
        if self.upto == "s3":
            if "o" in self.taps:
                self._tap("o", self.d_o.ap(), [I3, T], BF16)
            return
        self.global_scale(0, mx2[:, 0:1], nvals=1)
        self.barrier()
        ctx = ExitStack()
        sb = lambda name, shape, dt=F32: ctx.enter_context(
            nc.sbuf_tensor(name, shape, dt))
        ps = lambda name, shape, dt=F32: ctx.enter_context(
            nc.psum_tensor(name, shape, dt))
        with ctx:
            wdnT = sb("ffn_wdnT", [128, IC, D], BF16)
            ot = sb("ffn_ot", [128, 2, IC, 128], BF16)
            otf = sb("ffn_otf", [128, 2, IC * 128])
            oki = sb("ffn_oki", [128, 2, IC * 128], I32)
            x2tc = sb("ffn_x2tc", [128, 2, 2, 512])
            osb = sb("ffn_osb", [128, 2, 512])
            pD0 = ps("ffn_pD0", [128, 512])
            pD1 = ps("ffn_pD1", [128, 512])
            for c0 in range(0, IC, 8):
                self.sld.dinc(nc.sync.dma_start(
                    out=wdnT[:, c0:c0 + 8, :],
                    in_=self.d_dnT.ap().rearrange("(c p) d -> p c d", p=128)
                    [:, c0:c0 + 8, :]))
            self.wsld(nc.tensor)
            dtok = [self.tok(), self.tok()]
            ostok = [self.tok(), self.tok()]
            sotok = [self.tok(), self.tok()]
            qtoks = [None] * 33

            def s4_load(tt):
                par = tt % 2
                t0 = 128 * tt
                self.wtok(nc.gpsimd, dtok[par])
                nc.gpsimd.wait_ge(self.gld.h, self.gld.v)
                for c0 in range(0, IC, 8):
                    self.gld.dinc(nc.gpsimd.dma_start(
                        out=ot[:, par, c0:c0 + 8, :], in_=self.d_o.ap()
                        .rearrange("(c p) t -> p c t", p=128)
                        [:, c0:c0 + 8, t0:t0 + 128]))
                self.wtok(nc.sync, dtok[par])
                self.sld.dinc(nc.sync.dma_start(
                    out=x2tc[:, par, :, :],
                    in_=self.d_x2t.ap()[t0:t0 + 128, :]
                    .rearrange("p (a f) -> p a f", f=512)))

            def s4_qdq(tt):
                par = tt % 2
                self.wgld(nc.vector)
                self.wsld(nc.vector)
                otv = ot[:, par, :, :].rearrange("p c t -> p (c t)")
                self.V.tensor_scalar(otf[:, par, :], otv,
                                     self.sc_col[:, 1:2], None, ALU.mult)
                self.V.tensor_copy(oki[:, par, :], otf[:, par, :])
                self.V.tensor_scalar(
                    otv, oki[:, par, :], self.sc_col[:, 0:1], None, ALU.mult)
                qtoks[tt] = self.tok()

            s4_load(0)
            s4_qdq(0)
            s4_load(1)
            for tt in range(32):
                b = tt // 16
                t0 = 128 * tt
                par = tt % 2
                if tt + 1 < 32:
                    s4_qdq(tt + 1)
                for a in range(2):
                    pz = pD0 if a == 0 else pD1
                    self.wtok(nc.tensor, qtoks[tt])
                    self.wtok(nc.tensor, ostok[a])
                    for c in range(IC):
                        ins = nc.tensor.matmul(
                            pz[:], ot[:, par, c, :],
                            wdnT[:, c, 512 * a:512 * (a + 1)],
                            start=(c == 0), stop=(c == IC - 1))
                    self.tse.inc(ins)
                    self.wt(nc.vector)
                    self.wtok(nc.vector, sotok[a])
                    self.V.tensor_mul(
                        osb[:, a, :], pz[:],
                        self.gbbc[:, b, 512 * a:512 * (a + 1)])
                    self.V.tensor_add(
                        osb[:, a, :], osb[:, a, :], x2tc[:, par, a, :])
                    ostok[a] = self.tok()
                dtok[par] = (self.vse.v, 0, 0, 0, self.tse.v)
                if tt + 2 < 32:
                    s4_load(tt + 2)
                self.wv(nc.sync)
                self.sld.dinc(nc.sync.dma_start(
                    out=self.d_out.ap()[b, (t0 - b * S):(t0 - b * S) + 128, :],
                    in_=osb[:].rearrange("p a f -> p (a f)")))
                self.scommit()
                sotok[0] = (0, 0, 0, self.scm.v)
                sotok[1] = (0, 0, 0, self.scm.v)

    def build(self):
        nc = self.nc
        self.setup()
        if self.upto == "setup":
            self._finish()
            return nc
        actx = ExitStack()
        sb = lambda name, shape, dt=F32: actx.enter_context(
            nc.sbuf_tensor(name, shape, dt))
        with actx:
            self.zln = sb("zln", [128, DC, S], BF16)
            self.hv = sb("hv", [128, 16, D], BF16)
            self.outfm = sb("outfm", [128, DC, S], BF16)
            for b in range(BL):
                self.a1(b)
                self.barrier()
                if self.upto == "a1" and b == 0:
                    if "zln" in self.taps:
                        self._tap("zln",
                                  self.zln[:].rearrange("p k t -> p (k t)"),
                                  [128, DC * S], BF16)
                    self._finish()
                    return nc
                self.a2(b)
                self.barrier()
                if self.upto == "a2" and b == 0:
                    self._finish()
                    return nc
                self.wo_epilogue(b)
                self.barrier()
        if self.upto == "attn":
            if "x2f" in self.taps:
                self._tap("x2f", self.d_x2f.ap(), [D, T], F32)
            if "x2t" in self.taps:
                self._tap("x2t", self.d_x2t.ap(), [T, D], F32)
            if "h2" in self.taps:
                self._tap("h2", self.d_h2.ap(), [D, T], BF16)
            self._finish()
            return nc
        self.global_scale(0, self.h2max[:], nvals=1)
        self.barrier()
        self.ffn()
        self.barrier()
        self._finish()
        return nc

    def _finish(self):
        nc = self.nc
        self.gcommit()
        self.scommit()
        nc.sync.wait_ge(self.sld.h, self.sld.v)
        nc.gpsimd.wait_ge(self.gld.h, self.gld.v)
        nc.gpsimd.wait_ge(self.cc.h, self.cc.v)


def build_program(upto="all", taps=()):
    p = Prog(upto=upto, taps=taps)
    nc = p.build()
    return nc, p


def make_feeds(inputs):
    x = np.asarray(inputs["L0_attn_query"])
    c = np.asarray(inputs["L0_c"])
    base = {
        "wmod": np.ascontiguousarray(inputs["L0_wmod"]),
        "Wz": np.ascontiguousarray(inputs["L0_attn_Wz"]),
        "Wq": np.ascontiguousarray(inputs["L0_attn_Wq"]),
        "Wr": np.ascontiguousarray(inputs["L0_attn_Wr"]),
        "Wh": np.ascontiguousarray(inputs["L0_attn_Wh"]),
        "Wo": np.ascontiguousarray(inputs["L0_attn_Wo"]),
        "Wphi": np.ascontiguousarray(inputs["L0_attn_Wphi"]),
        "bphi": np.ascontiguousarray(
            np.asarray(inputs["L0_attn_bphi"]).reshape(1, HD)),
        "vecs": np.ascontiguousarray(np.stack([
            np.asarray(inputs["L0_attn_bz"]),
            np.asarray(inputs["L0_attn_ln_g"]),
            np.asarray(inputs["L0_attn_ln_b"]),
            np.asarray(inputs["L0_attn_bo"]),
            np.asarray(inputs["L0_attn_bq"]),
            np.asarray(inputs["L0_attn_br"]),
            np.asarray(inputs["L0_attn_bh"]),
        ], axis=0)),
    }
    wup = np.asarray(inputs["L0_ffn_w_up"])
    wgt = np.asarray(inputs["L0_ffn_w_gate"])
    wdn = np.asarray(inputs["L0_ffn_w_down"])
    feeds = []
    for core in range(NCORES):
        f = dict(base)
        xb = x[2 * core:2 * core + 2]
        f["x_fm"] = np.ascontiguousarray(xb.transpose(0, 2, 1))
        f["c2"] = np.ascontiguousarray(c[2 * core:2 * core + 2])
        f["wup_sl"] = np.ascontiguousarray(wup[:, 128 * core:128 * (core + 1)])
        f["wgt_sl"] = np.ascontiguousarray(wgt[:, 128 * core:128 * (core + 1)])
        f["wdn_sl"] = np.ascontiguousarray(wdn[:, 384 * core:384 * (core + 1)])
        feeds.append(f)
    return feeds


def kernel(**inputs):
    feeds = make_feeds(inputs)
    nc, _ = build_program()
    res = run_bass_kernel_spmd(nc, feeds, core_ids=list(range(NCORES)))
    out = np.concatenate([np.asarray(r["out"]) for r in res.results], axis=0)
    return out.astype(np.float32, copy=False)


# version 08
# speedup vs baseline: 1.0358x; 1.0358x over previous
from contextlib import ExitStack

import numpy as np
import concourse.bass as bass
import concourse.mybir as mybir
from concourse.bass_utils import run_bass_kernel_spmd

F32 = mybir.dt.float32
BF16 = mybir.dt.bfloat16
I32 = mybir.dt.int32
AF = mybir.ActivationFunctionType
ALU = mybir.AluOpType
AX = mybir.AxisListType

NCORES = 8
BL = 2
S = 2048
T = BL * S
D = 1024
DC = D // 128
H = 16
HD = 64
NPAIR = H // 2
I3 = 3072
IC = I3 // 128
QMAX = 32767.0
PI = float(np.pi)
TWO_PI = 2.0 * PI
INV_2PI = 1.0 / TWO_PI
CLAMP = 3.1415925
LN_EPS = 1e-5
SIM_LOCAL = False
SETUP_UPTO = "all"
DEQ_UPTO = 9


class Sem:

    def __init__(self, nc, ctx, name):
        self.h = ctx.enter_context(nc.semaphore(name))
        self.v = 0

    def inc(self, instr, n=1):
        instr.then_inc(self.h, n)
        self.v += n
        return instr

    def dinc(self, instr):
        return self.inc(instr, 16)


class ChainEng:

    _PASS = {"wait_ge", "wait_op", "sem_inc", "dma_start", "engine_nop", "nop"}

    def __init__(self, eng, sem):
        self._eng = eng
        self._sem = sem

    def __getattr__(self, name):
        f = getattr(self._eng, name)
        if name in self._PASS or name.startswith("_"):
            return f

        def g(*a, **k):
            self._eng.wait_ge(self._sem.h, self._sem.v)
            ins = f(*a, **k)
            self._sem.inc(ins)
            return ins
        return g


class Prog:
    def __init__(self, upto="all", taps=()):
        self.upto = upto
        self.taps = set(taps)
        nc = bass.Bass("TRN2", target_bir_lowering=False)
        self.nc = nc
        self.ctx = ExitStack()
        self.tap_handles = {}
        self._declare_io()
        self._alloc_persistent()


    def _declare_io(self):
        nc = self.nc
        dr = lambda name, shape, dt=F32, kind="ExternalInput": nc.dram_tensor(
            name, shape, dt, kind=kind)
        self.d_x = dr("x_fm", [BL, D, S])
        self.d_c2 = dr("c2", [BL, D])
        self.d_wmod = dr("wmod", [D, 6 * D])
        self.d_Wz = dr("Wz", [D, D])
        self.d_Wq = dr("Wq", [D, D])
        self.d_Wr = dr("Wr", [D, D])
        self.d_Wh = dr("Wh", [D, D])
        self.d_Wo = dr("Wo", [D, D])
        self.d_Wphi = dr("Wphi", [HD, HD])
        self.d_vec = dr("vecs", [7, D])
        self.d_bphi = dr("bphi", [1, HD])
        self.d_wup = dr("wup_sl", [I3, 128])
        self.d_wgt = dr("wgt_sl", [I3, 128])
        self.d_wdn = dr("wdn_sl", [D, 384])
        self.d_out = dr("out", [BL, S, D], kind="ExternalOutput")
        di = lambda name, shape, dt: nc.dram_tensor(name, shape, dt, kind="Internal")
        self.d_x2f = di("x2f_hbm", [D, T], F32)
        self.d_x2t = di("x2t_hbm", [T, D], F32)
        self.d_h2 = di("h2_hbm", [D, T], BF16)
        self.d_up = di("up_hbm", [I3, T], BF16)
        self.d_gt = di("gt_hbm", [I3, T], BF16)
        self.d_g2 = di("g2_hbm", [I3, T], BF16)
        self.d_o = di("o_hbm", [I3, T], BF16)
        self.d_upT_my = di("upT_my", [128, I3], BF16)
        self.d_gtT_my = di("gtT_my", [128, I3], BF16)
        self.d_dnT_my = di("dnT_my", [384, D], BF16)
        self.d_upT = di("upT_all", [D, I3], BF16)
        self.d_gtT = di("gtT_all", [D, I3], BF16)
        self.d_dnT = di("dnT_all", [I3, D], BF16)
        self.d_arin = [di(f"arin{i}", [1, 64], F32) for i in range(4)]
        self.d_arout = [di(f"arout{i}", [1, 64], F32) for i in range(4)]
        self.t_identf = nc.inline_tensor(np.eye(128, dtype=np.float32), "identf")
        import ml_dtypes
        self.t_identb = nc.inline_tensor(
            np.eye(128, dtype=ml_dtypes.bfloat16), "identb")

    def _alloc_persistent(self):
        nc, ctx = self.nc, self.ctx
        sb = lambda name, shape, dt=F32: ctx.enter_context(
            nc.sbuf_tensor(name, shape, dt))
        self.identf = sb("identf_sb", [128, 128])
        self.identb = sb("identb_sb", [128, 128], BF16)
        self.ones_r = sb("ones_r", [1, 128])
        self.ones_rb = sb("ones_rb", [1, 128], BF16)
        self.cst_b = sb("cst_b", [128, 1], BF16)
        self.ones_cb = sb("ones_cb", [128, 1], BF16)
        self.bz_t = sb("bz_t", [128, DC])
        self.g_t = sb("g_t", [128, DC])
        self.lb_t = sb("lb_t", [128, DC])
        self.bo_t = sb("bo_t", [128, DC])
        self.bq_dup = sb("bq_dup", [128, H])
        self.br_dup = sb("br_dup", [128, H])
        self.bh_row = sb("bh_row", [1, D])
        self.bphi_v = sb("bphi_v", [128, 1])
        self.mod_t = sb("mod_t", [128, DC, 6, BL])
        self.gabo_t = sb("gabo_t", [128, DC, BL])
        self.gbbc = sb("gbbc", [128, BL, D], BF16)
        self.mu_r = sb("mu_r", [1, 512])
        self.ms_r = sb("ms_r", [1, 512])
        self.a_r = sb("a_r", [1, 512])
        self.sc_col = sb("sc_col", [128, 16])
        self.h2max = sb("h2max", [128, 1])
        self.sld = Sem(nc, ctx, "sld")
        self.gld = Sem(nc, ctx, "gld")
        self.cc = Sem(nc, ctx, "cc")
        self.tse = Sem(nc, ctx, "tse")
        self.vse = Sem(nc, ctx, "vse")
        self.ase = Sem(nc, ctx, "ase")
        self.gse = Sem(nc, ctx, "gse")
        self.sse = Sem(nc, ctx, "sse")
        self.V = None
        self.A = None
        self.gcm = Sem(nc, ctx, "gcm")
        self.scm = Sem(nc, ctx, "scm")
        self._gcommitted = 0
        self._scommitted = 0
        self.V = ChainEng(nc.vector, self.vse)
        self.A = ChainEng(nc.scalar, self.ase)
        self.G = ChainEng(nc.gpsimd, self.gse)


    def barrier(self):
        nc = self.nc
        self.gcommit()
        self.scommit()
        nc.sync.wait_ge(self.sld.h, self.sld.v)
        nc.gpsimd.wait_ge(self.gld.h, self.gld.v)
        nc.gpsimd.wait_ge(self.cc.h, self.cc.v)
        nc.sync.wait_ge(self.sse.h, self.sse.v)
        nc.sync.sem_inc(self.sse.h, 1); self.sse.v += 1
        nc.tensor.wait_ge(self.tse.h, self.tse.v)
        nc.tensor.sem_inc(self.tse.h, 1); self.tse.v += 1
        nc.vector.wait_ge(self.vse.h, self.vse.v)
        nc.vector.sem_inc(self.vse.h, 1); self.vse.v += 1
        nc.scalar.wait_ge(self.ase.h, self.ase.v)
        nc.scalar.sem_inc(self.ase.h, 1); self.ase.v += 1
        nc.gpsimd.wait_ge(self.gse.h, self.gse.v)
        nc.gpsimd.sem_inc(self.gse.h, 1); self.gse.v += 1
        for eng in (nc.sync, nc.tensor, nc.vector, nc.scalar, nc.gpsimd):
            for s in (self.sse, self.tse, self.vse, self.ase, self.gse):
                eng.wait_ge(s.h, s.v)

    def gcommit(self):
        if self.gld.v > self._gcommitted:
            nc = self.nc
            nc.gpsimd.wait_ge(self.gld.h, self.gld.v)
            nc.gpsimd.wait_ge(self.gcm.h, self.gcm.v)
            nc.gpsimd.sem_inc(self.gcm.h, 1)
            self.gcm.v += 1
            self._gcommitted = self.gld.v

    def scommit(self):
        if self.sld.v > self._scommitted:
            nc = self.nc
            nc.sync.wait_ge(self.sld.h, self.sld.v)
            nc.sync.wait_ge(self.scm.h, self.scm.v)
            nc.sync.sem_inc(self.scm.h, 1)
            self.scm.v += 1
            self._scommitted = self.sld.v

    def tok(self):
        return (self.vse.v, self.ase.v, self.gse.v)

    def wtok(self, eng, t):
        if t[0]:
            eng.wait_ge(self.vse.h, t[0])
        if t[1]:
            eng.wait_ge(self.ase.h, t[1])
        if t[2]:
            eng.wait_ge(self.gse.h, t[2])
        if len(t) > 3 and t[3]:
            eng.wait_ge(self.scm.h, t[3])
        if len(t) > 4 and t[4]:
            eng.wait_ge(self.tse.h, t[4])

    def wv(self, eng):
        eng.wait_ge(self.vse.h, self.vse.v)

    def wt(self, eng):
        eng.wait_ge(self.tse.h, self.tse.v)

    def wa(self, eng):
        eng.wait_ge(self.ase.h, self.ase.v)

    def wsld(self, eng):
        self.scommit()
        eng.wait_ge(self.scm.h, self.scm.v)

    def wgld(self, eng):
        self.gcommit()
        eng.wait_ge(self.gcm.h, self.gcm.v)

    def wg(self, eng):
        eng.wait_ge(self.gse.h, self.gse.v)

    def guard_load(self, eng):
        self.wv(eng)
        self.wa(eng)
        self.wt(eng)
        if eng is self.nc.gpsimd:
            eng.wait_ge(self.gld.h, self.gld.v)
        elif eng is self.nc.sync:
            eng.wait_ge(self.sld.h, self.sld.v)

    def load_weight(self, dram, wslot):
        nc = self.nc
        self.guard_load(nc.gpsimd)
        self.gld.dinc(nc.gpsimd.dma_start(
            out=wslot[:],
            in_=dram.ap().rearrange("(k p) j -> p k j", p=128)))
        self.wgld(nc.tensor)


    def _tap(self, name, ap, shape, dt):
        nc = self.nc
        t = nc.dram_tensor(f"tap_{name}", shape, dt, kind="ExternalOutput")
        self.tap_handles[name] = t
        self.sld.dinc(nc.sync.dma_start(out=t.ap(), in_=ap))
        nc.sync.wait_ge(self.sld.h, self.sld.v)


    def setup(self):
        nc = self.nc
        ctx = ExitStack()
        sb = lambda name, shape, dt=F32: ctx.enter_context(
            nc.sbuf_tensor(name, shape, dt))
        ps = lambda name, shape, dt=F32: ctx.enter_context(
            nc.psum_tensor(name, shape, dt))
        with ctx:
            row7 = [sb(f"row7_{i}", [1, D]) for i in range(7)]
            bphi_row = sb("bphi_row", [1, HD])
            c_sb = sb("c_sb", [BL, D])
            c_si = sb("c_si", [BL, D])
            sc_fm = sb("sc_fm", [128, DC, BL])
            mod_rows = sb("mod_rows", [BL, 6 * D])
            wmod_c = sb("wmod_c", [128, DC, 512])
            scr = sb("setup_scr", [128, DC, BL])
            tmp = sb("setup_tmp", [128, DC, BL])
            gbrow = [sb(f"gbrow{b}", [1, D]) for b in range(BL)]
            pv = ps("pv", [128, 16])
            pm = ps("pm", [BL, 512])
            pgb = ps("pgb", [128, 512])
            self.sld.dinc(nc.sync.dma_start(out=self.identf[:],
                                            in_=self.t_identf.ap()))
            self.sld.dinc(nc.sync.dma_start(out=self.identb[:],
                                            in_=self.t_identb.ap()))
            for i in range(7):
                self.sld.dinc(nc.sync.dma_start(out=row7[i][:],
                                                in_=self.d_vec.ap()[i:i + 1, :]))
            self.sld.dinc(nc.sync.dma_start(out=c_sb[:], in_=self.d_c2.ap()))
            self.sld.dinc(nc.sync.dma_start(out=bphi_row[:], in_=self.d_bphi.ap()))
            self.wsld(nc.vector)
            self.V.memset(self.ones_r[:], 1.0)
            self.V.memset(self.ones_rb[:], 1.0)
            self.V.memset(self.cst_b[:], 1.0 / 1024.0)
            self.V.memset(self.ones_cb[:], 1.0)
            self.V.memset(self.h2max[:], 0.0)
            self.wsld(nc.tensor)
            for vi, dst in ((0, self.bz_t), (1, self.g_t), (2, self.lb_t),
                            (3, self.bo_t)):
                self.wv(nc.tensor)
                for k in range(DC):
                    ins = nc.tensor.transpose(
                        pv[:, k:k + 1], row7[vi][0:1, 128 * k:128 * (k + 1)],
                        self.identf[0:1, 0:1])
                self.tse.inc(ins)
                self.wt(nc.vector)
                self.V.tensor_copy(dst[:], pv[:, 0:DC])
            for vi, dst in ((4, self.bq_dup), (5, self.br_dup)):
                self.wv(nc.tensor)
                for h in range(H):
                    ins = nc.tensor.transpose(
                        pv[0:64, h:h + 1], row7[vi][0:1, 64 * h:64 * (h + 1)],
                        self.identf[0:1, 0:1])
                self.tse.inc(ins)
                self.wt(nc.vector)
                self.V.tensor_copy(dst[0:64, :], pv[0:64, 0:H])
                self.wv(nc.gpsimd)
                self.gld.dinc(nc.gpsimd.dma_start(out=dst[64:128, :],
                                                  in_=dst[0:64, :]))
            self.V.tensor_copy(self.bh_row[:], row7[6][:])
            self.wv(nc.tensor)
            ins = nc.tensor.transpose(pv[0:64, 0:1], bphi_row[0:1, :],
                                      self.identf[0:1, 0:1])
            self.tse.inc(ins)
            self.wt(nc.vector)
            self.V.tensor_copy(self.bphi_v[0:64, 0:1],
                                               pv[0:64, 0:1])
            self.wv(nc.gpsimd)
            self.gld.dinc(nc.gpsimd.dma_start(out=self.bphi_v[64:128, 0:1],
                                              in_=self.bphi_v[0:64, 0:1]))
            self.wgld(nc.vector)
            self.V.tensor_scalar(
                self.bphi_v[0:64, 0:1], self.bphi_v[0:64, 0:1], PI / 2.0,
                None, ALU.add)
            if SETUP_UPTO == "vec":
                self.barrier()
                return
            self.wsld(nc.scalar)
            self.A.activation(c_si[:], c_sb[:], AF.Silu,
                                              bias=0.0, scale=1.0)
            self.wa(nc.tensor)
            self.wv(nc.tensor)
            for k in range(DC):
                ins = nc.tensor.transpose(
                    pv[:, 2 * k:2 * k + BL], c_si[0:BL, 128 * k:128 * (k + 1)],
                    self.identf[0:BL, 0:BL])
            self.tse.inc(ins)
            self.wt(nc.vector)
            self.V.tensor_copy(
                sc_fm[:].rearrange("p k b -> p (k b)"), pv[:, 0:2 * DC])
            for j in range(12):
                self.guard_load(nc.gpsimd)
                self.gld.dinc(nc.gpsimd.dma_start(
                    out=wmod_c[:],
                    in_=self.d_wmod.ap().rearrange("(k p) j -> p k j", p=128)
                    [:, :, 512 * j:512 * (j + 1)]))
                self.wgld(nc.tensor)
                self.wv(nc.tensor)
                for k in range(DC):
                    ins = nc.tensor.matmul(pm[:], sc_fm[:, k, :], wmod_c[:, k, :],
                                           start=(k == 0), stop=(k == DC - 1))
                self.tse.inc(ins)
                self.wt(nc.vector)
                self.V.tensor_copy(
                    mod_rows[:, 512 * j:512 * (j + 1)], pm[:])
            for param in range(6):
                for k in range(DC):
                    off = param * D + 128 * k
                    self.wv(nc.tensor)
                    ins = nc.tensor.transpose(
                        pv[:, 0:BL], mod_rows[0:BL, off:off + 128],
                        self.identf[0:BL, 0:BL])
                    self.tse.inc(ins)
                    self.wt(nc.vector)
                    self.V.tensor_copy(
                        self.mod_t[:, k, param, :], pv[:, 0:BL])
            for param in (1, 4):
                self.V.tensor_scalar(
                    self.mod_t[:, :, param, :], self.mod_t[:, :, param, :],
                    1.0, None, ALU.add)
            for param in (2, 5):
                g = self.mod_t[:, :, param, :]
                self.wv(nc.scalar)
                self.A.activation(scr[:], g, AF.Sign, bias=0.0, scale=1.0)
                self.A.activation(tmp[:], g, AF.Abs,
                                                  bias=0.0, scale=1.0)
                self.wa(nc.vector)
                self.V.tensor_scalar(tmp[:], tmp[:], 0.25, None, ALU.max)
                self.V.tensor_mul(g, tmp[:], scr[:])
            self.V.tensor_mul(
                self.gabo_t[:], self.mod_t[:, :, 2, :],
                self.bo_t[:, :, None].broadcast_to([128, DC, BL]))
            for k in range(DC):
                for b in range(BL):
                    self.wv(nc.tensor)
                    ins = nc.tensor.transpose(pm[0:1, 0:128],
                                              self.mod_t[:, k, 5, b:b + 1],
                                              self.identf[:])
                    self.tse.inc(ins)
                    self.wt(nc.vector)
                    self.V.tensor_copy(
                        gbrow[b][0:1, 128 * k:128 * (k + 1)],
                        pm[0:1, 0:128])
            for b in range(BL):
                for jc in range(2):
                    self.wv(nc.tensor)
                    ins = nc.tensor.matmul(
                        pgb[:], self.ones_r[:],
                        gbrow[b][0:1, 512 * jc:512 * (jc + 1)],
                        start=True, stop=True)
                    self.tse.inc(ins)
                    self.wt(nc.vector)
                    self.V.tensor_copy(
                        self.gbbc[:, b, 512 * jc:512 * (jc + 1)], pgb[:])
            if SETUP_UPTO != "mod":
                self._dequant_ffn_weights(ctx)
        self.barrier()
        if "mod" in self.taps:
            self._tap("mod", self.mod_t[:].rearrange("p k q b -> p (k q b)"),
                      [128, DC * 6 * BL], F32)
        if "gabo" in self.taps:
            self._tap("gabo", self.gabo_t[:].rearrange("p k b -> p (k b)"),
                      [128, DC * BL], F32)

    def _dequant_ffn_weights(self, ctx):
        nc = self.nc
        ps = lambda name, shape, dt=F32: ctx.enter_context(
            nc.psum_tensor(name, shape, dt))
        ptr = ps("ptr_dq", [128, 512], BF16)

        def dequant(dsrc, nblk, nchunk, name):
            s2 = ExitStack()
            cols = nblk * 32
            w = s2.enter_context(nc.sbuf_tensor(f"wdq_{name}",
                                                [128, nchunk, cols], F32))
            m = s2.enter_context(nc.sbuf_tensor(f"wdm_{name}",
                                                [128, nchunk * nblk], F32))
            si = s2.enter_context(nc.sbuf_tensor(f"wds_{name}",
                                                 [128, nchunk * nblk], F32))
            ki = s2.enter_context(nc.sbuf_tensor(f"wdk_{name}",
                                                 [128, nchunk, cols], I32))
            kf = s2.enter_context(nc.sbuf_tensor(f"wdf_{name}",
                                                 [128, nchunk, cols], F32))
            wb = s2.enter_context(nc.sbuf_tensor(f"wdb_{name}",
                                                 [128, nchunk, cols], BF16))
            nch = w.shape[1]
            third = (nch + 2) // 3
            for c0 in range(0, nch, third):
                c1 = min(c0 + third, nch)
                self.gld.dinc(nc.gpsimd.dma_start(
                    out=w[:, c0:c1, :],
                    in_=dsrc.ap().rearrange("(c p) i -> p c i", p=128)
                    [:, c0:c1, :]))
            self.wgld(nc.vector)
            wv4 = w[:].rearrange("p c (b j) -> p c b j", j=32)
            if DEQ_UPTO >= 2:
                self.V.tensor_reduce(
                    m[:].rearrange("p (c b) -> p c b", b=nblk), wv4, AX.X,
                    ALU.max, apply_absolute_value=True)
                self.V.tensor_scalar(m[:], m[:], 1.0 / 7.0, 1e-12,
                                        ALU.mult, ALU.max)
                self.V.reciprocal(si[:], m[:])
            else:
                self.V.memset(m[:], 1.0)
                self.V.memset(si[:], 1.0)
            if DEQ_UPTO >= 3:
                self.V.tensor_mul(
                    wv4, wv4,
                    si[:].rearrange("p (c b) -> p c b", b=nblk)
                    [:, :, :, None].broadcast_to([128, nchunk, nblk, 32]))
            if DEQ_UPTO >= 4:
                self.V.tensor_copy(ki[:], w[:])
                self.V.tensor_copy(kf[:], ki[:])
            else:
                self.V.tensor_copy(kf[:], w[:])
            if DEQ_UPTO >= 5:
                self.V.tensor_mul(
                    wb[:].rearrange("p c (b j) -> p c b j", j=32),
                    kf[:].rearrange("p c (b j) -> p c b j", j=32),
                    m[:].rearrange("p (c b) -> p c b", b=nblk)
                    [:, :, :, None].broadcast_to([128, nchunk, nblk, 32]))
            else:
                self.V.tensor_copy(wb[:], kf[:])
            return s2, wb

        for dsrc, dmy, nm in ((self.d_wup, self.d_upT_my, "up"),
                              (self.d_wgt, self.d_gtT_my, "gt")):
            s2, wb = dequant(dsrc, 4, IC, nm)
            if DEQ_UPTO < 6:
                s2.close()
                continue
            with s2:
                wt_ = s2.enter_context(nc.sbuf_tensor(f"wdt_{nm}",
                                                      [128, IC, 128], BF16))
                for ic in range(IC):
                    g = ic % 4
                    self.wv(nc.tensor)
                    ins = nc.tensor.transpose(ptr[:, 128 * g:128 * (g + 1)],
                                              wb[:, ic, :], self.identb[:])
                    self.tse.inc(ins)
                    self.wt(nc.vector)
                    self.V.tensor_copy(
                        wt_[:, ic, :], ptr[:, 128 * g:128 * (g + 1)])
                self.wv(nc.gpsimd)
                self.gld.dinc(nc.gpsimd.dma_start(
                    out=dmy.ap().rearrange("p (c j) -> p c j", j=128),
                    in_=wt_[:]))
                nc.gpsimd.wait_ge(self.gld.h, self.gld.v)
        s2, wb = dequant(self.d_wdn, 12, DC, "dn")
        if DEQ_UPTO < 6:
            s2.close()
            return
        with s2:
            wt_ = s2.enter_context(nc.sbuf_tensor("wdt_dn",
                                                  [128, 3, DC, 128], BF16))
            for it in range(3):
                for oc in range(DC):
                    g = (it * DC + oc) % 4
                    self.wv(nc.tensor)
                    ins = nc.tensor.transpose(ptr[:, 128 * g:128 * (g + 1)],
                                              wb[:, oc, 128 * it:128 * (it + 1)],
                                              self.identb[:])
                    self.tse.inc(ins)
                    self.wt(nc.vector)
                    self.V.tensor_copy(
                        wt_[:, it, oc, :], ptr[:, 128 * g:128 * (g + 1)])
            self.wv(nc.gpsimd)
            for it in range(3):
                self.gld.dinc(nc.gpsimd.dma_start(
                    out=self.d_dnT_my.ap()[128 * it:128 * (it + 1), :]
                    .rearrange("p (oc j) -> p oc j", j=128),
                    in_=wt_[:, it, :, :]))
            nc.gpsimd.wait_ge(self.gld.h, self.gld.v)
        if SETUP_UPTO == "deq":
            return
        for my, full in ((self.d_upT_my, self.d_upT),
                         (self.d_gtT_my, self.d_gtT),
                         (self.d_dnT_my, self.d_dnT)):
            if SIM_LOCAL:
                rows = my.shape[0]
                for rr in range(NCORES):
                    self.gld.dinc(nc.gpsimd.dma_start(
                        out=full.ap()[rr * rows:(rr + 1) * rows, :],
                        in_=my.ap()))
            else:
                self.cc.inc(nc.gpsimd.collective_compute(
                    "AllGather", ALU.bypass,
                    replica_groups=[list(range(NCORES))],
                    ins=[my.ap().opt()], outs=[full.ap().opt()]))
        nc.gpsimd.wait_ge(self.gld.h, self.gld.v)
        nc.gpsimd.wait_ge(self.cc.h, self.cc.v)


    def _ln_stats(self, xt, xsqt, pR, pBc):
        nc = self.nc
        self.wa(nc.tensor)
        self.wv(nc.tensor)
        for k in range(DC):
            ins = nc.tensor.matmul(pR[:], self.cst_b[:], xt[:, k, :],
                                   start=(k == 0), stop=(k == DC - 1))
        self.tse.inc(ins)
        self.wt(nc.vector)
        self.V.tensor_copy(self.mu_r[:], pR[:])
        self.wv(nc.tensor)
        for k in range(DC):
            ins = nc.tensor.matmul(pR[:], self.cst_b[:], xsqt[:, k, :],
                                   start=(k == 0), stop=(k == DC - 1))
        self.tse.inc(ins)
        self.wt(nc.vector)
        self.V.tensor_copy(self.ms_r[:], pR[:])
        self.V.tensor_mul(self.a_r[:], self.mu_r[:], self.mu_r[:])
        self.V.tensor_sub(self.a_r[:], self.ms_r[:], self.a_r[:])
        self.V.tensor_scalar(self.a_r[:], self.a_r[:], LN_EPS, None, ALU.add)
        self.wv(nc.scalar)
        self.A.activation(self.a_r[:], self.a_r[:], AF.Sqrt,
                                          bias=0.0, scale=1.0)
        self.wa(nc.vector)
        self.V.reciprocal(self.a_r[:], self.a_r[:])
        self.wv(nc.tensor)
        ins = nc.tensor.matmul(pBc[:, 0:512], self.ones_r[:], self.mu_r[:],
                               start=True, stop=True)
        self.tse.inc(ins)
        ins = nc.tensor.matmul(pBc[:, 512:1024], self.ones_r[:], self.a_r[:],
                               start=True, stop=True)
        self.tse.inc(ins)


    def a1(self, b):
        nc = self.nc
        ctx = ExitStack()
        sb = lambda name, shape, dt=F32: ctx.enter_context(
            nc.sbuf_tensor(name, shape, dt))
        ps = lambda name, shape, dt=F32: ctx.enter_context(
            nc.psum_tensor(name, shape, dt))
        with ctx:
            wslot = sb(f"a1_wslot{b}", [128, DC, D], BF16)
            self.load_weight(self.d_Wz, wslot)
            xc = sb(f"a1_xc{b}", [128, DC, 512], BF16)
            xsq = sb(f"a1_xsq{b}", [128, DC, 512], BF16)
            hc = sb(f"a1_hc{b}", [128, DC, 512], BF16)
            d1 = sb(f"a1_d1{b}", [128, DC, 512], BF16)
            d2 = sb(f"a1_d2{b}", [128, 512])
            zc = sb(f"a1_zc{b}", [128, DC, 512], BF16)
            pR = ps(f"a1_pR{b}", [1, 512])
            pBc = ps(f"a1_pBc{b}", [128, 1024])
            pZ0 = ps(f"a1_pZ0{b}", [128, 512])
            pZ1 = ps(f"a1_pZ1{b}", [128, 512])
            for ch in range(4):
                t0 = 512 * ch
                self.guard_load(nc.gpsimd)
                self.gld.dinc(nc.gpsimd.dma_start(
                    out=xc[:],
                    in_=self.d_x.ap()[b].rearrange("(k p) t -> p k t", p=128)
                    [:, :, t0:t0 + 512]))
                self.wgld(nc.scalar)
                self.A.activation(xsq[:], xc[:], AF.Square,
                                                  bias=0.0, scale=1.0)
                self._ln_stats(xc, xsq, pR, pBc)
                self.wt(nc.vector)
                self.V.tensor_sub(
                    d1[:], xc[:],
                    pBc[:, 0:512][:, None, :].broadcast_to([128, DC, 512]))
                for k in range(DC):
                    self.wa(nc.vector)
                    self.V.scalar_tensor_tensor(
                        d2[:], d1[:, k, :], self.mod_t[:, k, 1, b:b + 1],
                        pBc[:, 512:1024], ALU.mult, ALU.mult)
                    self.wv(nc.scalar)
                    self.A.activation(
                        hc[:, k, :], d2[:], AF.Identity,
                        bias=self.mod_t[:, k, 0, b:b + 1], scale=1.0)
                self.wa(nc.tensor)
                ztok = [self.tok(), self.tok()]
                for j in range(DC):
                    jp = j % 2
                    pz = pZ0 if jp == 0 else pZ1
                    self.wtok(nc.tensor, ztok[jp])
                    for k in range(DC):
                        ins = nc.tensor.matmul(
                            pz[:], wslot[:, k, 128 * j:128 * (j + 1)],
                            hc[:, k, :], start=(k == 0), stop=(k == DC - 1))
                    self.tse.inc(ins)
                    eng, E = ((nc.vector, self.V) if jp == 0
                              else (nc.scalar, self.A))
                    self.wt(eng)
                    if jp == 0:
                        E.tensor_scalar(zc[:, j, :], pz[:],
                                        self.bz_t[:, j:j + 1], None, ALU.add)
                    else:
                        E.activation(zc[:, j, :], pz[:], AF.Identity,
                                     bias=self.bz_t[:, j:j + 1], scale=1.0)
                    ztok[jp] = self.tok()
                self.wa(nc.scalar)
                self.wv(nc.scalar)
                self.A.activation(xsq[:], zc[:], AF.Square,
                                                  bias=0.0, scale=1.0)
                self._ln_stats(zc, xsq, pR, pBc)
                self.wt(nc.vector)
                self.V.tensor_sub(
                    d1[:], zc[:],
                    pBc[:, 0:512][:, None, :].broadcast_to([128, DC, 512]))
                for k in range(DC):
                    self.wa(nc.vector)
                    self.V.scalar_tensor_tensor(
                        d2[:], d1[:, k, :], self.g_t[:, k:k + 1],
                        pBc[:, 512:1024], ALU.mult, ALU.mult)
                    self.wv(nc.scalar)
                    self.A.activation(
                        self.zln[:, k, t0:t0 + 512], d2[:], AF.Identity,
                        bias=self.lb_t[:, k:k + 1], scale=1.0)


    def a2(self, b):
        nc = self.nc
        ctx = ExitStack()
        sb = lambda name, shape, dt=F32: ctx.enter_context(
            nc.sbuf_tensor(name, shape, dt))
        ps = lambda name, shape, dt=F32: ctx.enter_context(
            nc.psum_tensor(name, shape, dt))
        with ctx:
            hvx = ExitStack()
            wslot = hvx.enter_context(
                nc.sbuf_tensor(f"a2_wslot{b}", [128, DC, D], BF16))
            self.load_weight(self.d_Wh, wslot)
            pG = [ps(f"a2_pG{i}{b}", [128, 1024]) for i in range(2)]
            gtokv = [self.tok(), self.tok()]
            pOut = ps(f"a2_pOut{b}", [128, 2048])
            otok = self.tok()

            def grp(i):
                self.wtok(nc.tensor, gtokv[i])
                return pG[i]

            def evac_eng(i):
                return (nc.vector, self.V) if i % 2 == 0 else (nc.scalar, self.A)

            for tt in range(16):
                g = tt % 2
                pz = grp(g)
                for fc in range(2):
                    ins = nc.tensor.matmul(
                        pz[:, 512 * fc:512 * (fc + 1)], self.ones_r[:],
                        self.bh_row[:, 512 * fc:512 * (fc + 1)],
                        start=True, stop=False)
                    for k in range(DC):
                        ins = nc.tensor.matmul(
                            pz[:, 512 * fc:512 * (fc + 1)],
                            self.zln[:, k, 128 * tt:128 * (tt + 1)],
                            wslot[:, k, 512 * fc:512 * (fc + 1)],
                            start=False, stop=(k == DC - 1))
                self.tse.inc(ins)
                eng, E = evac_eng(tt)
                self.wt(eng)
                if tt % 2 == 0:
                    E.tensor_copy(self.hv[:, tt, :], pz[:])
                else:
                    E.activation(self.hv[:, tt, :], pz[:], AF.Identity,
                                 bias=0.0, scale=1.0)
                gtokv[g] = self.tok()
            if "Hv" in self.taps and b == 0:
                self.barrier()
                self._tap("Hv", self.hv[:].rearrange("p t f -> p (t f)"),
                          [128, 16 * D], BF16)
            hvx.close()
            wqd = sb(f"a2_wqd{b}", [128, 2, DC, 128], BF16)
            wrd = sb(f"a2_wrd{b}", [128, 2, DC, 128], BF16)
            wphid = sb(f"a2_wphid{b}", [64, 128], BF16)
            qd = sb(f"a2_qd{b}", [128, 2, 2, 2048], BF16)
            rd = sb(f"a2_rd{b}", [128, 2, 2, 2048], BF16)
            trig = sb(f"a2_trig{b}", [128, 2048])
            ki = sb(f"a2_ki{b}", [128, 2048], I32)
            cs = sb(f"a2_cs{b}", [128, 2048], BF16)
            rctm = sb(f"a2_rctm{b}", [128, 2048], BF16)
            msb = sb(f"a2_msb{b}", [128, 128], BF16)
            mqc = sb(f"a2_mqc{b}", [128, 2048], BF16)
            ysb = sb(f"a2_ysb{b}", [128, 16])
            itp = sb(f"a2_itp{b}", [16, 128])
            invr = sb(f"a2_invr{b}", [1, 2, 2048], BF16)
            invbc = sb(f"a2_invbc{b}", [128, 2, 2048], BF16)
            sct = sb(f"a2_sct{b}", [128, 2048], BF16)
            self.guard_load(nc.gpsimd)
            self.gld.dinc(nc.gpsimd.dma_start(out=wphid[:, 0:64],
                                              in_=self.d_Wphi.ap()))
            self.gld.dinc(nc.gpsimd.dma_start(out=wphid[:, 64:128],
                                              in_=self.d_Wphi.ap()))

            gtk0 = [None]
            gtk1 = [self.tok()]

            def et(E):
                s = self.vse if E is self.V else self.ase
                return (s.h, s.v)

            def wet(eng, t):
                if t is not None:
                    eng.wait_ge(t[0], t[1])

            def grp1():
                self.wtok(nc.tensor, gtk1[0])
                return pG[1]

            def chain_gen(p):
                pr = p % 2
                hA, hB = 2 * p, 2 * p + 1
                self.wt(nc.gpsimd)
                nc.gpsimd.wait_ge(self.gld.h, self.gld.v)
                for (dst, dram) in ((wqd, self.d_Wq), (wrd, self.d_Wr)):
                    for hi, h in enumerate((hA, hB)):
                        srcw = dram.ap().rearrange("(k p) j -> p k j", p=128)
                        srcw = srcw[:, :, HD * h:HD * (h + 1)]
                        self.gld.dinc(nc.gpsimd.dma_start(
                            out=dst[:, hi, :, 0:64], in_=srcw))
                        self.gld.dinc(nc.gpsimd.dma_start(
                            out=dst[:, hi, :, 64:128], in_=srcw))
                yield
                self.wgld(nc.tensor)
                for ti, (dst, wsl, bias) in enumerate(
                        ((qd, wqd, self.bq_dup), (rd, wrd, self.br_dup))):
                    for hi, h in enumerate((hA, hB)):
                        for g2 in range(2):
                            pz = grp1()
                            for half in range(2):
                                qc = 2 * g2 + half
                                for k in range(DC):
                                    ins = nc.tensor.matmul(
                                        pz[:, 512 * half:512 * (half + 1)],
                                        wsl[:, hi, k, :],
                                        self.zln[:, k,
                                                 512 * qc:512 * (qc + 1)],
                                        start=(k == 0), stop=(k == DC - 1))
                            self.tse.inc(ins)
                            eng, E = evac_eng(2 * ti + hi + g2)
                            self.wt(eng)
                            if (2 * ti + hi + g2) % 2 == 0:
                                E.tensor_scalar(
                                    dst[:, pr, hi, 1024 * g2:1024 * (g2 + 1)],
                                    pz[:], bias[:, h:h + 1], None, ALU.add)
                            else:
                                E.activation(
                                    dst[:, pr, hi, 1024 * g2:1024 * (g2 + 1)],
                                    pz[:], AF.Identity,
                                    bias=bias[:, h:h + 1], scale=1.0)
                            gtk1[0] = self.tok()
                            yield
                for hi in range(2):
                    for ti, srcd in enumerate((qd, rd)):
                        for g2 in range(2):
                            pz = grp1()
                            for half in range(2):
                                qc = 2 * g2 + half
                                ins = nc.tensor.matmul(
                                    pz[:, 512 * half:512 * (half + 1)],
                                    wphid[:, :],
                                    srcd[0:64, pr, hi,
                                         512 * qc:512 * (qc + 1)],
                                    start=True, stop=True)
                                self.tse.inc(ins)
                            self.wt(nc.scalar)
                            self.A.activation(
                                trig[:, 1024 * g2:1024 * (g2 + 1)], pz[:],
                                AF.Identity, bias=self.bphi_v[:], scale=1.0)
                            gtk1[0] = self.tok()
                            yield
                        self.wa(nc.vector)
                        self.V.tensor_scalar(ki[:].bitcast(F32), trig[:],
                                             INV_2PI, None, ALU.mult)
                        self.V.tensor_copy(ki[:], ki[:].bitcast(F32))
                        self.V.scalar_tensor_tensor(
                            trig[:], ki[:], -TWO_PI, trig[:], ALU.mult,
                            ALU.add)
                        yield
                        self.wv(nc.gpsimd)
                        self.G.tensor_scalar(
                            trig[:], trig[:], CLAMP, -CLAMP, ALU.min, ALU.max)
                        yield
                        self.wg(nc.scalar)
                        self.A.activation(
                            cs[:], trig[:], AF.Sin, bias=0.0, scale=1.0)
                        yield
                        self.wa(nc.vector)
                        self.V.tensor_mul(srcd[:, pr, hi, :],
                                          srcd[:, pr, hi, :], cs[:])
                        yield
                for hi in range(2):
                    self.wv(nc.tensor)
                    for kq in range(2):
                        pz = grp1()
                        v = pz[:].bitcast(BF16)
                        for j in range(8):
                            kt = 8 * kq + j
                            ins = nc.tensor.transpose(
                                v[:, 1024 * kq + 128 * j:
                                  1024 * kq + 128 * (j + 1)],
                                rd[:, pr, hi, 128 * kt:128 * (kt + 1)],
                                self.identb[:])
                            self.tse.inc(ins)
                        self.wt(nc.vector)
                        self.V.tensor_copy(
                            rctm[:, 1024 * kq:1024 * (kq + 1)],
                            v[:, 1024 * kq:1024 * (kq + 1)])
                        gtk1[0] = self.tok()
                        yield
                    pzM = grp1()
                    for kt in range(16):
                        ins = nc.tensor.matmul(
                            pzM[:, 0:128], rctm[:, 128 * kt:128 * (kt + 1)],
                            rctm[:, 128 * kt:128 * (kt + 1)],
                            start=(kt == 0), stop=(kt == 15))
                    self.tse.inc(ins)
                    self.wt(nc.vector)
                    self.V.tensor_copy(msb[:], pzM[:, 0:128])
                    gtk1[0] = self.tok()
                    yield
                    for g2 in range(2):
                        pz = grp1()
                        for half in range(2):
                            qc = 2 * g2 + half
                            ins = nc.tensor.matmul(
                                pz[:, 512 * half:512 * (half + 1)], msb[:],
                                qd[:, pr, hi, 512 * qc:512 * (qc + 1)],
                                start=True, stop=True)
                            self.tse.inc(ins)
                        eng, E = evac_eng(g2)
                        self.wt(eng)
                        if g2 % 2 == 0:
                            E.tensor_copy(mqc[:, 1024 * g2:1024 * (g2 + 1)],
                                          pz[:])
                        else:
                            E.activation(mqc[:, 1024 * g2:1024 * (g2 + 1)],
                                         pz[:], AF.Identity, bias=0.0,
                                         scale=1.0)
                        gtk1[0] = self.tok()
                        yield
                    self.wa(nc.vector)
                    self.V.tensor_mul(mqc[:], mqc[:], qd[:, pr, hi, :])
                    gtk1[0] = self.tok()
                    yield
                    pzY = grp1()
                    for qt in range(16):
                        ins = nc.tensor.matmul(
                            pzY[:, qt:qt + 1],
                            mqc[:, 128 * qt:128 * (qt + 1)],
                            self.ones_cb[:], start=True, stop=True)
                        self.tse.inc(ins)
                    self.wt(nc.vector)
                    self.V.tensor_scalar(
                        ysb[:], pzY[:, 0:16], 1e-24, None, ALU.max)
                    gtk1[0] = self.tok()
                    yield
                    self.wv(nc.scalar)
                    self.A.activation(ysb[:], ysb[:], AF.Sqrt,
                                      bias=0.0, scale=1.0)
                    self.wa(nc.vector)
                    self.V.reciprocal(ysb[:], ysb[:])
                    yield
                    pzT = grp1()
                    self.wv(nc.tensor)
                    ins = nc.tensor.transpose(pzT[0:16, 0:128], ysb[:],
                                              self.identf[:])
                    self.tse.inc(ins)
                    self.wt(nc.vector)
                    self.wgld(nc.vector)
                    self.V.tensor_copy(itp[:], pzT[0:16, 0:128])
                    gtk1[0] = self.tok()
                    self.wv(nc.gpsimd)
                    self.gld.dinc(nc.gpsimd.dma_start(out=invr[:, hi, :],
                                                      in_=itp[:]))
                    yield
                self.wgld(nc.tensor)
                for g2 in range(2):
                    pz = grp1()
                    for half in range(2):
                        qc = 2 * g2 + half
                        ins = nc.tensor.matmul(
                            pz[0:64, 512 * half:512 * (half + 1)],
                            self.ones_rb[:, 0:64],
                            invr[:, 0, 512 * qc:512 * (qc + 1)],
                            start=True, stop=True)
                        self.tse.inc(ins)
                        ins = nc.tensor.matmul(
                            pz[64:128, 512 * half:512 * (half + 1)],
                            self.ones_rb[:, 0:64],
                            invr[:, 1, 512 * qc:512 * (qc + 1)],
                            start=True, stop=True, tile_position=(0, 64))
                        self.tse.inc(ins)
                    eng, E = evac_eng(g2)
                    self.wt(eng)
                    if g2 % 2 == 0:
                        E.tensor_copy(invbc[:, pr, 1024 * g2:1024 * (g2 + 1)],
                                      pz[:])
                    else:
                        E.activation(invbc[:, pr, 1024 * g2:1024 * (g2 + 1)],
                                     pz[:], AF.Identity, bias=0.0, scale=1.0)
                    gtk1[0] = self.tok()
                    yield

            def scores_gen(p, otc):
                pr = p % 2
                hA, hB = 2 * p, 2 * p + 1
                for hi, h in enumerate((hA, hB)):
                    for kt in range(16):
                        stok = [None, None]
                        for g2 in range(2):
                            wet(nc.tensor, gtk0[0])
                            for half in range(2):
                                qc = 2 * g2 + half
                                ins = nc.tensor.matmul(
                                    pG[0][:, 512 * half:512 * (half + 1)],
                                    rd[:, pr, hi, 128 * kt:128 * (kt + 1)],
                                    qd[:, pr, hi, 512 * qc:512 * (qc + 1)],
                                    start=True, stop=True)
                                self.tse.inc(ins)
                            eng, E = evac_eng(g2 + kt)
                            self.wt(eng)
                            if (g2 + kt) % 2 == 0:
                                E.tensor_copy(
                                    sct[:, 1024 * g2:1024 * (g2 + 1)],
                                    pG[0][:])
                            else:
                                E.activation(
                                    sct[:, 1024 * g2:1024 * (g2 + 1)],
                                    pG[0][:], AF.Identity, bias=0.0,
                                    scale=1.0)
                            gtk0[0] = et(E)
                            stok[g2] = gtk0[0]
                        for qc in range(4):
                            wet(nc.tensor, stok[qc // 2])
                            if kt == 0:
                                wet(nc.tensor, otc[0])
                            ins = nc.tensor.matmul(
                                pOut[64 * hi:64 * (hi + 1),
                                     512 * qc:512 * (qc + 1)],
                                self.hv[:, kt, HD * h:HD * (h + 1)],
                                sct[:, 512 * qc:512 * (qc + 1)],
                                start=(kt == 0), stop=(kt == 15),
                                tile_position=(0, 64 * hi))
                            self.tse.inc(ins)
                        yield
                self.wt(nc.vector)
                self.wa(nc.vector)
                self.V.tensor_mul(self.outfm[:, p, :], pOut[:],
                                  invbc[:, pr, :])
                otc[0] = et(self.V)

            _SENT = object()

            def advance(gen, n):
                if gen is None:
                    return False
                for _ in range(n):
                    if next(gen, _SENT) is _SENT:
                        return False
                return True

            gtk0[0] = et(self.V)
            cg = chain_gen(0)
            while advance(cg, 8):
                pass
            otc = [None]
            for p in range(NPAIR):
                sg = scores_gen(p, otc)
                cg = chain_gen(p + 1) if p + 1 < NPAIR else None
                s_alive, c_alive = True, True
                while s_alive or c_alive:
                    s_alive = advance(sg, 1)
                    c_alive = advance(cg, 2)
            if "Qc" in self.taps and b == 0:
                self.barrier()
                self._tap("Qc", qd[:, (NPAIR - 1) % 2, 0, :], [128, 2048], BF16)
                self._tap("Rc", rd[:, (NPAIR - 1) % 2, 0, :], [128, 2048], BF16)
                self._tap("invbc", invbc[:, (NPAIR - 1) % 2, :], [128, 2048], BF16)
            if "outfm" in self.taps and b == 0:
                self.barrier()
                self._tap("outfm", self.outfm[:].rearrange("p c q -> p (c q)"),
                          [128, DC * 2048], BF16)


    def wo_epilogue(self, b):
        nc = self.nc
        ctx = ExitStack()
        sb = lambda name, shape, dt=F32: ctx.enter_context(
            nc.sbuf_tensor(name, shape, dt))
        ps = lambda name, shape, dt=F32: ctx.enter_context(
            nc.psum_tensor(name, shape, dt))
        with ctx:
            wslot = sb(f"wo_wslot{b}", [128, DC, D], BF16)
            self.load_weight(self.d_Wo, wslot)
            pA = ps(f"wo_pA{b}", [128, 512])
            pB = ps(f"wo_pB{b}", [128, 512])
            pR = ps(f"wo_pR{b}", [1, 512])
            pBc = ps(f"wo_pBc{b}", [128, 1024])
            pTr = ps(f"wo_pTr{b}", [128, 512])
            x2c = sb(f"wo_x2c{b}", [128, DC, 512])
            x2b = sb(f"wo_x2b{b}", [128, DC, 512], BF16)
            x2sq = sb(f"wo_x2sq{b}", [128, DC, 512], BF16)
            xfc = sb(f"wo_xfc{b}", [128, DC, 512])
            h2c = sb(f"wo_h2c{b}", [128, DC, 512], BF16)
            d1 = sb(f"wo_d1{b}", [128, DC, 512], BF16)
            d2 = sb(f"wo_d2{b}", [128, 512])
            x2t = sb(f"wo_x2t{b}", [128, 4, 128])
            for qc in range(4):
                t0 = 512 * qc
                gt0 = b * S + t0
                self.guard_load(nc.sync)
                self.sld.dinc(nc.sync.dma_start(
                    out=xfc[:],
                    in_=self.d_x.ap()[b].rearrange("(k p) t -> p k t", p=128)
                    [:, :, t0:t0 + 512]))
                wtok2 = [self.tok(), self.tok()]
                for e in range(DC):
                    ep = e % 2
                    pz = pA if ep == 0 else pB
                    self.wtok(nc.tensor, wtok2[ep])
                    for k in range(DC):
                        ins = nc.tensor.matmul(
                            pz[:], wslot[:, k, 128 * e:128 * (e + 1)],
                            self.outfm[:, k, t0:t0 + 512],
                            start=(k == 0), stop=(k == DC - 1))
                    self.tse.inc(ins)
                    self.wt(nc.scalar)
                    self.A.activation(
                        x2c[:, e, :], pz[:], AF.Identity,
                        bias=self.gabo_t[:, e, b:b + 1],
                        scale=self.mod_t[:, e, 2, b:b + 1])
                    wtok2[ep] = self.tok()
                    self.wsld(nc.vector)
                    self.wa(nc.vector)
                    self.V.tensor_add(x2c[:, e, :], x2c[:, e, :],
                                         xfc[:, e, :])
                    self.V.tensor_copy(x2b[:, e, :],
                                                       x2c[:, e, :])
                    self.wv(nc.sync)
                    self.sld.dinc(nc.sync.dma_start(
                        out=self.d_x2f[128 * e:128 * (e + 1), gt0:gt0 + 512],
                        in_=x2c[:, e, :]))
                    self.wv(nc.tensor)
                    self.wgld(nc.tensor)
                    for j in range(4):
                        ins = nc.tensor.transpose(
                            pTr[:, 128 * j:128 * (j + 1)],
                            x2c[:, e, 128 * j:128 * (j + 1)], self.identf[:])
                    self.tse.inc(ins)
                    self.wt(nc.vector)
                    self.V.tensor_copy(
                        x2t[:].rearrange("p j f -> p (j f)"), pTr[:])
                    self.wv(nc.gpsimd)
                    self.gld.dinc(nc.gpsimd.dma_start(
                        out=self.d_x2t.ap()
                        [gt0:gt0 + 512, 128 * e:128 * (e + 1)]
                        .rearrange("(j p) f -> p j f", p=128),
                        in_=x2t[:]))
                self.wv(nc.scalar)
                self.A.activation(x2sq[:], x2b[:], AF.Square,
                                                  bias=0.0, scale=1.0)
                self._ln_stats(x2b, x2sq, pR, pBc)
                self.wt(nc.vector)
                self.wgld(nc.vector)
                self.V.tensor_sub(
                    d1[:], x2b[:],
                    pBc[:, 0:512][:, None, :].broadcast_to([128, DC, 512]))
                for k in range(DC):
                    self.wa(nc.vector)
                    self.V.scalar_tensor_tensor(
                        d2[:], d1[:, k, :], self.mod_t[:, k, 4, b:b + 1],
                        pBc[:, 512:1024], ALU.mult, ALU.mult)
                    self.wv(nc.scalar)
                    self.A.activation(
                        h2c[:, k, :], d2[:], AF.Identity,
                        bias=self.mod_t[:, k, 3, b:b + 1], scale=1.0)
                self.wa(nc.vector)
                self.V.tensor_reduce(
                    d2[:, 0:1], h2c[:].rearrange("p k t -> p (k t)"), AX.X,
                    ALU.max, apply_absolute_value=True)
                self.V.tensor_max(self.h2max[:],
                                                  self.h2max[:], d2[:, 0:1])
                self.wv(nc.gpsimd)
                self.gld.dinc(nc.gpsimd.dma_start(
                    out=self.d_h2[:, gt0:gt0 + 512]
                    .rearrange("(k p) t -> p k t", p=128),
                    in_=h2c[:]))
                self.wgld(nc.scalar)


    _gs_n = 0

    def global_scale(self, idx, locmax_col, nvals=1):
        nc = self.nc
        ctx = ExitStack()
        with ctx:
            ps = lambda name, shape, dt=F32: ctx.enter_context(
                nc.psum_tensor(name, shape, dt))
            sbl = lambda name, shape, dt=F32: ctx.enter_context(
                nc.sbuf_tensor(name, shape, dt))
            Prog._gs_n += 1
            u = Prog._gs_n
            pT = ps(f"gs_p{idx}_{u}", [128, 128])
            mrow = sbl(f"gs_m{idx}_{u}", [1, 64])
            m2 = sbl(f"gs_m2{idx}_{u}", [2, 1])
            srow = sbl(f"gs_s{idx}_{u}", [1, 16])
            self.wv(nc.tensor)
            self.wa(nc.tensor)
            ins = nc.tensor.transpose(pT[0:nvals, 0:128],
                                      locmax_col[:, 0:nvals], self.identf[:])
            self.tse.inc(ins)
            self.wt(nc.vector)
            self.V.memset(mrow[:], 0.0)
            self.V.tensor_reduce(m2[0:nvals, :], pT[0:nvals, 0:128], AX.X,
                                 ALU.max)
            self.wv(nc.gpsimd)
            self.gld.dinc(nc.gpsimd.dma_start(out=mrow[0:1, 0:nvals],
                                              in_=m2[0:nvals, :]))
            nc.gpsimd.wait_ge(self.gld.h, self.gld.v)
            self.gld.dinc(nc.gpsimd.dma_start(out=self.d_arin[idx].ap(),
                                              in_=mrow[:]))
            nc.gpsimd.wait_ge(self.gld.h, self.gld.v)
            if SIM_LOCAL:
                self.gld.dinc(nc.gpsimd.dma_start(
                    out=self.d_arout[idx].ap(), in_=self.d_arin[idx].ap()))
                nc.gpsimd.wait_ge(self.gld.h, self.gld.v)
            else:
                self.cc.inc(nc.gpsimd.collective_compute(
                    "AllReduce", ALU.max, replica_groups=[list(range(NCORES))],
                    ins=[self.d_arin[idx].ap().opt()],
                    outs=[self.d_arout[idx].ap().opt()]))
                nc.gpsimd.wait_ge(self.cc.h, self.cc.v)
            self.gld.dinc(nc.gpsimd.dma_start(out=mrow[:],
                                              in_=self.d_arout[idx].ap()))
            self.wgld(nc.vector)
            self.V.memset(srow[:], 1.0)
            self.V.tensor_scalar(srow[:, 0:nvals], mrow[:, 0:nvals],
                                    1.0 / QMAX, 1e-12, ALU.mult, ALU.max)
            self.V.reciprocal(srow[:, 8:8 + nvals],
                                              srow[:, 0:nvals])
            self.wv(nc.tensor)
            ins = nc.tensor.matmul(pT[:, 0:16], self.ones_r[:], srow[:],
                                   start=True, stop=True)
            self.tse.inc(ins)
            self.wt(nc.vector)
            for i in range(nvals):
                self.V.tensor_copy(
                    self.sc_col[:, 2 * (idx + i):2 * (idx + i) + 1],
                    pT[:, i:i + 1])
                self.V.tensor_copy(
                    self.sc_col[:, 2 * (idx + i) + 1:2 * (idx + i) + 2],
                    pT[:, 8 + i:8 + i + 1])


    def ffn(self):
        nc = self.nc
        ctx = ExitStack()
        sb = lambda name, shape, dt=F32: ctx.enter_context(
            nc.sbuf_tensor(name, shape, dt))
        ps = lambda name, shape, dt=F32: ctx.enter_context(
            nc.psum_tensor(name, shape, dt))
        mx2 = self.ctx.enter_context(nc.sbuf_tensor("ffn_mx2", [128, 2], F32))
        with ctx:
            hq = sb("ffn_hq", [128, DC, T], BF16)
            t1 = sb("ffn_t1", [128, 2048])
            ki = sb("ffn_ki", [128, 2048], I32)
            hb = sb("ffn_hb", [128, DC, 512], BF16)
            wu = [sb(f"ffn_wu{i}", [128, DC, 128], BF16) for i in range(2)]
            wg = [sb(f"ffn_wg{i}", [128, DC, 128], BF16) for i in range(2)]

            def load_slices(j, par):
                nc.sync.wait_ge(self.tse.h, self.tse.v)
                self.sld.dinc(nc.sync.dma_start(
                    out=wu[par][:],
                    in_=self.d_upT.ap().rearrange("(k p) i -> p k i", p=128)
                    [:, :, 128 * j:128 * (j + 1)]))
                self.sld.dinc(nc.sync.dma_start(
                    out=wg[par][:],
                    in_=self.d_gtT.ap().rearrange("(k p) i -> p k i", p=128)
                    [:, :, 128 * j:128 * (j + 1)]))

            for ch in range(8):
                t0 = 512 * ch
                self.guard_load(nc.gpsimd)
                self.gld.dinc(nc.gpsimd.dma_start(
                    out=hb[:], in_=self.d_h2[:, t0:t0 + 512]
                    .rearrange("(k p) t -> p k t", p=128)))
                self.wgld(nc.vector)
                for kg in range(2):
                    sl = slice(kg * 4, kg * 4 + 4)
                    hv_ = hb[:, sl, :].rearrange("p k t -> p (k t)")
                    t1v = t1[:].rearrange("p (k t) -> p k t", t=512)
                    kiv = ki[:].rearrange("p (k t) -> p k t", t=512)
                    self.V.tensor_scalar(t1[:], hv_, self.sc_col[:, 1:2],
                                            None, ALU.mult)
                    self.V.tensor_copy(ki[:], t1[:])
                    self.V.tensor_scalar(
                        hq[:, sl, t0:t0 + 512],
                        kiv, self.sc_col[:, 0:1], None, ALU.mult)
            if "hq" in self.taps:
                self.barrier()
                self._tap("hq", hq[:].rearrange("p k t -> p (k t)"),
                          [128, DC * T], BF16)
            if self.upto == "hq":
                self.barrier()
                return
            pU0 = ps("ffn_pU0", [128, 512])
            pU1 = ps("ffn_pU1", [128, 512])
            pG0 = ps("ffn_pG0", [128, 512])
            pG1 = ps("ffn_pG1", [128, 512])
            upc = sb("ffn_upc", [128, 4, 512], BF16)
            gtc = sb("ffn_gtc", [128, 4, 512], BF16)
            red = sb("ffn_red", [128, 1])
            self.V.memset(mx2[:], 0.0)
            load_slices(0, 0)
            utok = [self.tok(), self.tok()]
            gtok = [self.tok(), self.tok()]
            sttok = [self.tok()] * 4
            for ic in range(IC):
                wpar = ic % 2
                self.wsld(nc.tensor)
                if ic + 1 < IC:
                    load_slices(ic + 1, 1 - wpar)
                for tc in range(8):
                    par = tc % 2
                    pu = pU0 if par == 0 else pU1
                    pg = pG0 if par == 0 else pG1
                    self.wtok(nc.tensor, utok[par])
                    for k in range(DC):
                        ins = nc.tensor.matmul(
                            pu[:], wu[wpar][:, k, :],
                            hq[:, k, 512 * tc:512 * (tc + 1)],
                            start=(k == 0), stop=(k == DC - 1))
                    self.tse.inc(ins)
                    self.wtok(nc.tensor, gtok[par])
                    for k in range(DC):
                        ins = nc.tensor.matmul(
                            pg[:], wg[wpar][:, k, :],
                            hq[:, k, 512 * tc:512 * (tc + 1)],
                            start=(k == 0), stop=(k == DC - 1))
                    self.tse.inc(ins)
                    slot = (ic * 8 + tc) % 4
                    self.wt(nc.vector)
                    self.wtok(nc.vector, sttok[slot])
                    self.V.tensor_copy(upc[:, slot, :], pu[:])
                    utok[par] = self.tok()
                    self.wt(nc.scalar)
                    self.wtok(nc.scalar, sttok[slot])
                    self.A.activation(
                        gtc[:, slot, :], pg[:], AF.Identity, bias=0.0,
                        scale=1.0)
                    gtok[par] = self.tok()
                    self.wa(nc.vector)
                    self.V.tensor_reduce(red[:], upc[:, slot, :], AX.X,
                                            ALU.max, apply_absolute_value=True)
                    self.V.tensor_max(mx2[:, 0:1], mx2[:, 0:1], red[:])
                    self.V.tensor_reduce(red[:], gtc[:, slot, :], AX.X,
                                            ALU.max, apply_absolute_value=True)
                    self.V.tensor_max(mx2[:, 1:2],
                                                      mx2[:, 1:2], red[:])
                    self.wv(nc.sync)
                    self.wa(nc.sync)
                    self.sld.dinc(nc.sync.dma_start(
                        out=self.d_up[128 * ic:128 * (ic + 1),
                                      512 * tc:512 * (tc + 1)],
                        in_=upc[:, slot, :]))
                    self.sld.dinc(nc.sync.dma_start(
                        out=self.d_gt[128 * ic:128 * (ic + 1),
                                      512 * tc:512 * (tc + 1)],
                        in_=gtc[:, slot, :]))
                    if slot == 3:
                        self.scommit()
                        for s_ in range(4):
                            sttok[s_] = (0, 0, 0, self.scm.v)
        self.barrier()
        if self.upto == "s1":
            if "up" in self.taps:
                self._tap("up", self.d_up.ap(), [I3, T], BF16)
            if "gt" in self.taps:
                self._tap("gt", self.d_gt.ap(), [I3, T], BF16)
            return
        self.global_scale(1, mx2[:], nvals=2)
        self.barrier()
        ctx = ExitStack()
        sb = lambda name, shape, dt=F32: ctx.enter_context(
            nc.sbuf_tensor(name, shape, dt))
        with ctx:
            gc = sb("ffn_gc", [128, 2, T], BF16)
            sg = sb("ffn_sg", [128, 2, T], BF16)
            gq = sb("ffn_gq", [128, 2, T], BF16)
            t2 = sb("ffn_t2", [128, 2, T])
            ki2 = sb("ffn_ki2", [128, 2, T], I32)
            red = sb("ffn_red2", [128, 1])
            self.V.memset(mx2[:, 0:1], 0.0)
            ltok = [self.tok(), self.tok()]
            for ic in range(IC):
                pr = ic % 2
                self.wtok(nc.sync, ltok[pr])
                self.sld.dinc(nc.sync.dma_start(
                    out=gc[:, pr, :],
                    in_=self.d_gt[128 * ic:128 * (ic + 1), :]))
                self.wsld(nc.scalar)
                self.wv(nc.scalar)
                self.A.activation(t2[:, pr, :], gc[:, pr, :], AF.Identity,
                                  bias=0.0, scale=self.sc_col[:, 5:6])
                self.wa(nc.vector)
                self.V.tensor_copy(ki2[:, pr, :], t2[:, pr, :])
                self.V.tensor_scalar(
                    gq[:, pr, :], ki2[:, pr, :], self.sc_col[:, 4:5],
                    None, ALU.mult)
                self.wv(nc.scalar)
                self.A.activation(sg[:, pr, :], gq[:, pr, :], AF.Sigmoid,
                                  bias=0.0, scale=1.0)
                self.A.activation(t2[:, pr, :], sg[:, pr, :], AF.Identity,
                                  bias=0.0, scale=65536.0)
                self.wa(nc.vector)
                self.V.tensor_copy(ki2[:, pr, :], t2[:, pr, :])
                self.V.tensor_scalar(sg[:, pr, :], ki2[:, pr, :], 65535.0,
                                     1.0 / 65536.0, ALU.min, ALU.mult)
                self.V.tensor_mul(gc[:, pr, :], gq[:, pr, :], sg[:, pr, :])
                self.V.tensor_reduce(red[:], gc[:, pr, :], AX.X, ALU.max,
                                     apply_absolute_value=True)
                self.V.tensor_max(mx2[:, 0:1], mx2[:, 0:1], red[:])
                self.wv(nc.sync)
                self.sld.dinc(nc.sync.dma_start(
                    out=self.d_g2[128 * ic:128 * (ic + 1), :],
                    in_=gc[:, pr, :]))
                self.scommit()
                ltok[pr] = (0, 0, 0, self.scm.v)
            self.barrier()
            if self.upto == "s2":
                if "g2" in self.taps:
                    self._tap("g2", self.d_g2.ap(), [I3, T], BF16)
                return
            self.global_scale(3, mx2[:, 0:1], nvals=1)
            self.barrier()
            self.V.memset(mx2[:, 0:1], 0.0)
            ltok = [self.tok(), self.tok()]
            for ic in range(IC):
                pr = ic % 2
                self.wtok(nc.sync, ltok[pr])
                self.sld.dinc(nc.sync.dma_start(
                    out=gc[:, pr, :],
                    in_=self.d_g2[128 * ic:128 * (ic + 1), :]))
                self.sld.dinc(nc.sync.dma_start(
                    out=sg[:, pr, :],
                    in_=self.d_up[128 * ic:128 * (ic + 1), :]))
                self.wsld(nc.scalar)
                self.wv(nc.scalar)
                self.A.activation(t2[:, pr, :], sg[:, pr, :], AF.Identity,
                                  bias=0.0, scale=self.sc_col[:, 3:4])
                self.wa(nc.vector)
                self.V.tensor_copy(ki2[:, pr, :], t2[:, pr, :])
                self.V.tensor_scalar(
                    gq[:, pr, :], ki2[:, pr, :], self.sc_col[:, 2:3],
                    None, ALU.mult)
                self.wv(nc.scalar)
                self.A.activation(t2[:, pr, :], gc[:, pr, :], AF.Identity,
                                  bias=0.0, scale=self.sc_col[:, 7:8])
                self.wa(nc.vector)
                self.V.tensor_copy(ki2[:, pr, :], t2[:, pr, :])
                self.V.tensor_scalar(
                    sg[:, pr, :], ki2[:, pr, :], self.sc_col[:, 6:7],
                    None, ALU.mult)
                self.V.tensor_mul(gc[:, pr, :], gq[:, pr, :], sg[:, pr, :])
                self.V.tensor_reduce(red[:], gc[:, pr, :], AX.X, ALU.max,
                                     apply_absolute_value=True)
                self.V.tensor_max(mx2[:, 0:1], mx2[:, 0:1], red[:])
                self.wv(nc.sync)
                self.sld.dinc(nc.sync.dma_start(
                    out=self.d_o[128 * ic:128 * (ic + 1), :],
                    in_=gc[:, pr, :]))
                self.scommit()
                ltok[pr] = (0, 0, 0, self.scm.v)
        self.barrier()
        if self.upto == "s3":
            if "o" in self.taps:
                self._tap("o", self.d_o.ap(), [I3, T], BF16)
            return
        self.global_scale(0, mx2[:, 0:1], nvals=1)
        self.barrier()
        ctx = ExitStack()
        sb = lambda name, shape, dt=F32: ctx.enter_context(
            nc.sbuf_tensor(name, shape, dt))
        ps = lambda name, shape, dt=F32: ctx.enter_context(
            nc.psum_tensor(name, shape, dt))
        with ctx:
            wdnT = sb("ffn_wdnT", [128, IC, D], BF16)
            ot = sb("ffn_ot", [128, 2, IC, 128], BF16)
            otf = sb("ffn_otf", [128, 2, IC * 128])
            oki = sb("ffn_oki", [128, 2, IC * 128], I32)
            x2tc = sb("ffn_x2tc", [128, 2, 2, 512])
            osb = sb("ffn_osb", [128, 2, 512])
            pD0 = ps("ffn_pD0", [128, 512])
            pD1 = ps("ffn_pD1", [128, 512])
            for c0 in range(0, IC, 8):
                self.sld.dinc(nc.sync.dma_start(
                    out=wdnT[:, c0:c0 + 8, :],
                    in_=self.d_dnT.ap().rearrange("(c p) d -> p c d", p=128)
                    [:, c0:c0 + 8, :]))
            self.wsld(nc.tensor)
            dtok = [self.tok(), self.tok()]
            ostok = [self.tok(), self.tok()]
            sotok = [self.tok(), self.tok()]
            qtoks = [None] * 33

            def s4_load(tt):
                par = tt % 2
                t0 = 128 * tt
                self.wtok(nc.gpsimd, dtok[par])
                nc.gpsimd.wait_ge(self.gld.h, self.gld.v)
                for c0 in range(0, IC, 8):
                    self.gld.dinc(nc.gpsimd.dma_start(
                        out=ot[:, par, c0:c0 + 8, :], in_=self.d_o.ap()
                        .rearrange("(c p) t -> p c t", p=128)
                        [:, c0:c0 + 8, t0:t0 + 128]))
                self.wtok(nc.sync, dtok[par])
                self.sld.dinc(nc.sync.dma_start(
                    out=x2tc[:, par, :, :],
                    in_=self.d_x2t.ap()[t0:t0 + 128, :]
                    .rearrange("p (a f) -> p a f", f=512)))

            def s4_qdq(tt):
                par = tt % 2
                self.wgld(nc.vector)
                self.wsld(nc.vector)
                otv = ot[:, par, :, :].rearrange("p c t -> p (c t)")
                self.V.tensor_scalar(otf[:, par, :], otv,
                                     self.sc_col[:, 1:2], None, ALU.mult)
                self.V.tensor_copy(oki[:, par, :], otf[:, par, :])
                self.V.tensor_scalar(
                    otv, oki[:, par, :], self.sc_col[:, 0:1], None, ALU.mult)
                qtoks[tt] = self.tok()

            s4_load(0)
            s4_qdq(0)
            s4_load(1)
            for tt in range(32):
                b = tt // 16
                t0 = 128 * tt
                par = tt % 2
                if tt + 1 < 32:
                    s4_qdq(tt + 1)
                for a in range(2):
                    pz = pD0 if a == 0 else pD1
                    self.wtok(nc.tensor, qtoks[tt])
                    self.wtok(nc.tensor, ostok[a])
                    for c in range(IC):
                        ins = nc.tensor.matmul(
                            pz[:], ot[:, par, c, :],
                            wdnT[:, c, 512 * a:512 * (a + 1)],
                            start=(c == 0), stop=(c == IC - 1))
                    self.tse.inc(ins)
                    self.wt(nc.vector)
                    self.wtok(nc.vector, sotok[a])
                    self.V.tensor_mul(
                        osb[:, a, :], pz[:],
                        self.gbbc[:, b, 512 * a:512 * (a + 1)])
                    self.V.tensor_add(
                        osb[:, a, :], osb[:, a, :], x2tc[:, par, a, :])
                    ostok[a] = self.tok()
                dtok[par] = (self.vse.v, 0, 0, 0, self.tse.v)
                if tt + 2 < 32:
                    s4_load(tt + 2)
                self.wv(nc.sync)
                self.sld.dinc(nc.sync.dma_start(
                    out=self.d_out.ap()[b, (t0 - b * S):(t0 - b * S) + 128, :],
                    in_=osb[:].rearrange("p a f -> p (a f)")))
                self.scommit()
                sotok[0] = (0, 0, 0, self.scm.v)
                sotok[1] = (0, 0, 0, self.scm.v)

    def build(self):
        nc = self.nc
        self.setup()
        if self.upto == "setup":
            self._finish()
            return nc
        actx = ExitStack()
        sb = lambda name, shape, dt=F32: actx.enter_context(
            nc.sbuf_tensor(name, shape, dt))
        with actx:
            self.zln = sb("zln", [128, DC, S], BF16)
            self.hv = sb("hv", [128, 16, D], BF16)
            self.outfm = sb("outfm", [128, DC, S], BF16)
            for b in range(BL):
                self.a1(b)
                self.barrier()
                if self.upto == "a1" and b == 0:
                    if "zln" in self.taps:
                        self._tap("zln",
                                  self.zln[:].rearrange("p k t -> p (k t)"),
                                  [128, DC * S], BF16)
                    self._finish()
                    return nc
                self.a2(b)
                self.barrier()
                if self.upto == "a2" and b == 0:
                    self._finish()
                    return nc
                self.wo_epilogue(b)
                self.barrier()
        if self.upto == "attn":
            if "x2f" in self.taps:
                self._tap("x2f", self.d_x2f.ap(), [D, T], F32)
            if "x2t" in self.taps:
                self._tap("x2t", self.d_x2t.ap(), [T, D], F32)
            if "h2" in self.taps:
                self._tap("h2", self.d_h2.ap(), [D, T], BF16)
            self._finish()
            return nc
        self.global_scale(0, self.h2max[:], nvals=1)
        self.barrier()
        self.ffn()
        self.barrier()
        self._finish()
        return nc

    def _finish(self):
        nc = self.nc
        self.gcommit()
        self.scommit()
        nc.sync.wait_ge(self.sld.h, self.sld.v)
        nc.gpsimd.wait_ge(self.gld.h, self.gld.v)
        nc.gpsimd.wait_ge(self.cc.h, self.cc.v)


def build_program(upto="all", taps=()):
    p = Prog(upto=upto, taps=taps)
    nc = p.build()
    return nc, p


def make_feeds(inputs):
    x = np.asarray(inputs["L0_attn_query"])
    c = np.asarray(inputs["L0_c"])
    base = {
        "wmod": np.ascontiguousarray(inputs["L0_wmod"]),
        "Wz": np.ascontiguousarray(inputs["L0_attn_Wz"]),
        "Wq": np.ascontiguousarray(inputs["L0_attn_Wq"]),
        "Wr": np.ascontiguousarray(inputs["L0_attn_Wr"]),
        "Wh": np.ascontiguousarray(inputs["L0_attn_Wh"]),
        "Wo": np.ascontiguousarray(inputs["L0_attn_Wo"]),
        "Wphi": np.ascontiguousarray(inputs["L0_attn_Wphi"]),
        "bphi": np.ascontiguousarray(
            np.asarray(inputs["L0_attn_bphi"]).reshape(1, HD)),
        "vecs": np.ascontiguousarray(np.stack([
            np.asarray(inputs["L0_attn_bz"]),
            np.asarray(inputs["L0_attn_ln_g"]),
            np.asarray(inputs["L0_attn_ln_b"]),
            np.asarray(inputs["L0_attn_bo"]),
            np.asarray(inputs["L0_attn_bq"]),
            np.asarray(inputs["L0_attn_br"]),
            np.asarray(inputs["L0_attn_bh"]),
        ], axis=0)),
    }
    wup = np.asarray(inputs["L0_ffn_w_up"])
    wgt = np.asarray(inputs["L0_ffn_w_gate"])
    wdn = np.asarray(inputs["L0_ffn_w_down"])
    feeds = []
    for core in range(NCORES):
        f = dict(base)
        xb = x[2 * core:2 * core + 2]
        f["x_fm"] = np.ascontiguousarray(xb.transpose(0, 2, 1))
        f["c2"] = np.ascontiguousarray(c[2 * core:2 * core + 2])
        f["wup_sl"] = np.ascontiguousarray(wup[:, 128 * core:128 * (core + 1)])
        f["wgt_sl"] = np.ascontiguousarray(wgt[:, 128 * core:128 * (core + 1)])
        f["wdn_sl"] = np.ascontiguousarray(wdn[:, 384 * core:384 * (core + 1)])
        feeds.append(f)
    return feeds


def kernel(**inputs):
    feeds = make_feeds(inputs)
    nc, _ = build_program()
    res = run_bass_kernel_spmd(nc, feeds, core_ids=list(range(NCORES)))
    out = np.concatenate([np.asarray(r["out"]) for r in res.results], axis=0)
    return out.astype(np.float32, copy=False)


# version 09
# speedup vs baseline: 1.0425x; 1.0065x over previous
from contextlib import ExitStack

import numpy as np
import concourse.bass as bass
import concourse.mybir as mybir
from concourse.bass_utils import run_bass_kernel_spmd

F32 = mybir.dt.float32
BF16 = mybir.dt.bfloat16
I32 = mybir.dt.int32
AF = mybir.ActivationFunctionType
ALU = mybir.AluOpType
AX = mybir.AxisListType

NCORES = 8
BL = 2
S = 2048
T = BL * S
D = 1024
DC = D // 128
H = 16
HD = 64
NPAIR = H // 2
I3 = 3072
IC = I3 // 128
QMAX = 32767.0
PI = float(np.pi)
TWO_PI = 2.0 * PI
INV_2PI = 1.0 / TWO_PI
CLAMP = 3.1415925
LN_EPS = 1e-5
SIM_LOCAL = False
SETUP_UPTO = "all"
DEQ_UPTO = 9


class Sem:

    def __init__(self, nc, ctx, name):
        self.h = ctx.enter_context(nc.semaphore(name))
        self.v = 0

    def inc(self, instr, n=1):
        instr.then_inc(self.h, n)
        self.v += n
        return instr

    def dinc(self, instr):
        return self.inc(instr, 16)


class ChainEng:

    _PASS = {"wait_ge", "wait_op", "sem_inc", "dma_start", "engine_nop", "nop"}

    def __init__(self, eng, sem):
        self._eng = eng
        self._sem = sem

    def __getattr__(self, name):
        f = getattr(self._eng, name)
        if name in self._PASS or name.startswith("_"):
            return f

        def g(*a, **k):
            self._eng.wait_ge(self._sem.h, self._sem.v)
            ins = f(*a, **k)
            self._sem.inc(ins)
            return ins
        return g


class Prog:
    def __init__(self, upto="all", taps=()):
        self.upto = upto
        self.taps = set(taps)
        nc = bass.Bass("TRN2", target_bir_lowering=False)
        self.nc = nc
        self.ctx = ExitStack()
        self.tap_handles = {}
        self._declare_io()
        self._alloc_persistent()


    def _declare_io(self):
        nc = self.nc
        dr = lambda name, shape, dt=F32, kind="ExternalInput": nc.dram_tensor(
            name, shape, dt, kind=kind)
        self.d_x = dr("x_fm", [BL, D, S])
        self.d_c2 = dr("c2", [BL, D])
        self.d_wmod = dr("wmod", [D, 6 * D])
        self.d_Wz = dr("Wz", [D, D])
        self.d_Wq = dr("Wq", [D, D])
        self.d_Wr = dr("Wr", [D, D])
        self.d_Wh = dr("Wh", [D, D])
        self.d_Wo = dr("Wo", [D, D])
        self.d_Wphi = dr("Wphi", [HD, HD])
        self.d_vec = dr("vecs", [7, D])
        self.d_bphi = dr("bphi", [1, HD])
        self.d_wup = dr("wup_sl", [I3, 128])
        self.d_wgt = dr("wgt_sl", [I3, 128])
        self.d_wdn = dr("wdn_sl", [D, 384])
        self.d_out = dr("out", [BL, S, D], kind="ExternalOutput")
        di = lambda name, shape, dt: nc.dram_tensor(name, shape, dt, kind="Internal")
        self.d_x2f = di("x2f_hbm", [D, T], F32)
        self.d_x2t = di("x2t_hbm", [T, D], F32)
        self.d_h2 = di("h2_hbm", [D, T], BF16)
        self.d_up = di("up_hbm", [I3, T], BF16)
        self.d_gt = di("gt_hbm", [I3, T], BF16)
        self.d_g2 = di("g2_hbm", [I3, T], BF16)
        self.d_o = di("o_hbm", [I3, T], BF16)
        self.d_upT_my = di("upT_my", [128, I3], BF16)
        self.d_gtT_my = di("gtT_my", [128, I3], BF16)
        self.d_dnT_my = di("dnT_my", [384, D], BF16)
        self.d_upT = di("upT_all", [D, I3], BF16)
        self.d_gtT = di("gtT_all", [D, I3], BF16)
        self.d_dnT = di("dnT_all", [I3, D], BF16)
        self.d_arin = [di(f"arin{i}", [1, 64], F32) for i in range(4)]
        self.d_arout = [di(f"arout{i}", [1, 64], F32) for i in range(4)]
        self.t_identf = nc.inline_tensor(np.eye(128, dtype=np.float32), "identf")
        import ml_dtypes
        self.t_identb = nc.inline_tensor(
            np.eye(128, dtype=ml_dtypes.bfloat16), "identb")

    def _alloc_persistent(self):
        nc, ctx = self.nc, self.ctx
        sb = lambda name, shape, dt=F32: ctx.enter_context(
            nc.sbuf_tensor(name, shape, dt))
        self.identf = sb("identf_sb", [128, 128])
        self.identb = sb("identb_sb", [128, 128], BF16)
        self.ones_r = sb("ones_r", [1, 128])
        self.ones_rb = sb("ones_rb", [1, 128], BF16)
        self.cst_b = sb("cst_b", [128, 1], BF16)
        self.ones_cb = sb("ones_cb", [128, 1], BF16)
        self.bz_t = sb("bz_t", [128, DC])
        self.g_t = sb("g_t", [128, DC])
        self.lb_t = sb("lb_t", [128, DC])
        self.bo_t = sb("bo_t", [128, DC])
        self.bq_dup = sb("bq_dup", [128, H])
        self.br_dup = sb("br_dup", [128, H])
        self.bh_row = sb("bh_row", [1, D])
        self.bphi_v = sb("bphi_v", [128, 1])
        self.mod_t = sb("mod_t", [128, DC, 6, BL])
        self.gabo_t = sb("gabo_t", [128, DC, BL])
        self.gbbc = sb("gbbc", [128, BL, D], BF16)
        self.mu_r = sb("mu_r", [1, 512])
        self.ms_r = sb("ms_r", [1, 512])
        self.a_r = sb("a_r", [1, 512])
        self.sc_col = sb("sc_col", [128, 16])
        self.h2max = sb("h2max", [128, 1])
        self.sld = Sem(nc, ctx, "sld")
        self.gld = Sem(nc, ctx, "gld")
        self.cc = Sem(nc, ctx, "cc")
        self.tse = Sem(nc, ctx, "tse")
        self.vse = Sem(nc, ctx, "vse")
        self.ase = Sem(nc, ctx, "ase")
        self.gse = Sem(nc, ctx, "gse")
        self.sse = Sem(nc, ctx, "sse")
        self.V = None
        self.A = None
        self.gcm = Sem(nc, ctx, "gcm")
        self.scm = Sem(nc, ctx, "scm")
        self._gcommitted = 0
        self._scommitted = 0
        self.V = ChainEng(nc.vector, self.vse)
        self.A = ChainEng(nc.scalar, self.ase)
        self.G = ChainEng(nc.gpsimd, self.gse)


    def barrier(self):
        nc = self.nc
        self.gcommit()
        self.scommit()
        nc.sync.wait_ge(self.sld.h, self.sld.v)
        nc.gpsimd.wait_ge(self.gld.h, self.gld.v)
        nc.gpsimd.wait_ge(self.cc.h, self.cc.v)
        nc.sync.wait_ge(self.sse.h, self.sse.v)
        nc.sync.sem_inc(self.sse.h, 1); self.sse.v += 1
        nc.tensor.wait_ge(self.tse.h, self.tse.v)
        nc.tensor.sem_inc(self.tse.h, 1); self.tse.v += 1
        nc.vector.wait_ge(self.vse.h, self.vse.v)
        nc.vector.sem_inc(self.vse.h, 1); self.vse.v += 1
        nc.scalar.wait_ge(self.ase.h, self.ase.v)
        nc.scalar.sem_inc(self.ase.h, 1); self.ase.v += 1
        nc.gpsimd.wait_ge(self.gse.h, self.gse.v)
        nc.gpsimd.sem_inc(self.gse.h, 1); self.gse.v += 1
        for eng in (nc.sync, nc.tensor, nc.vector, nc.scalar, nc.gpsimd):
            for s in (self.sse, self.tse, self.vse, self.ase, self.gse):
                eng.wait_ge(s.h, s.v)

    def gcommit(self):
        if self.gld.v > self._gcommitted:
            nc = self.nc
            nc.gpsimd.wait_ge(self.gld.h, self.gld.v)
            nc.gpsimd.wait_ge(self.gcm.h, self.gcm.v)
            nc.gpsimd.sem_inc(self.gcm.h, 1)
            self.gcm.v += 1
            self._gcommitted = self.gld.v

    def scommit(self):
        if self.sld.v > self._scommitted:
            nc = self.nc
            nc.sync.wait_ge(self.sld.h, self.sld.v)
            nc.sync.wait_ge(self.scm.h, self.scm.v)
            nc.sync.sem_inc(self.scm.h, 1)
            self.scm.v += 1
            self._scommitted = self.sld.v

    def tok(self):
        return (self.vse.v, self.ase.v, self.gse.v)

    def wtok(self, eng, t):
        if t[0]:
            eng.wait_ge(self.vse.h, t[0])
        if t[1]:
            eng.wait_ge(self.ase.h, t[1])
        if t[2]:
            eng.wait_ge(self.gse.h, t[2])
        if len(t) > 3 and t[3]:
            eng.wait_ge(self.scm.h, t[3])
        if len(t) > 4 and t[4]:
            eng.wait_ge(self.tse.h, t[4])

    def wv(self, eng):
        eng.wait_ge(self.vse.h, self.vse.v)

    def wt(self, eng):
        eng.wait_ge(self.tse.h, self.tse.v)

    def wa(self, eng):
        eng.wait_ge(self.ase.h, self.ase.v)

    def wsld(self, eng):
        self.scommit()
        eng.wait_ge(self.scm.h, self.scm.v)

    def wgld(self, eng):
        self.gcommit()
        eng.wait_ge(self.gcm.h, self.gcm.v)

    def wg(self, eng):
        eng.wait_ge(self.gse.h, self.gse.v)

    def guard_load(self, eng):
        self.wv(eng)
        self.wa(eng)
        self.wt(eng)
        if eng is self.nc.gpsimd:
            eng.wait_ge(self.gld.h, self.gld.v)
        elif eng is self.nc.sync:
            eng.wait_ge(self.sld.h, self.sld.v)

    def load_weight(self, dram, wslot):
        nc = self.nc
        self.guard_load(nc.gpsimd)
        self.gld.dinc(nc.gpsimd.dma_start(
            out=wslot[:],
            in_=dram.ap().rearrange("(k p) j -> p k j", p=128)))
        self.wgld(nc.tensor)


    def _tap(self, name, ap, shape, dt):
        nc = self.nc
        t = nc.dram_tensor(f"tap_{name}", shape, dt, kind="ExternalOutput")
        self.tap_handles[name] = t
        self.sld.dinc(nc.sync.dma_start(out=t.ap(), in_=ap))
        nc.sync.wait_ge(self.sld.h, self.sld.v)


    def setup(self):
        nc = self.nc
        ctx = ExitStack()
        sb = lambda name, shape, dt=F32: ctx.enter_context(
            nc.sbuf_tensor(name, shape, dt))
        ps = lambda name, shape, dt=F32: ctx.enter_context(
            nc.psum_tensor(name, shape, dt))
        with ctx:
            row7 = [sb(f"row7_{i}", [1, D]) for i in range(7)]
            bphi_row = sb("bphi_row", [1, HD])
            c_sb = sb("c_sb", [BL, D])
            c_si = sb("c_si", [BL, D])
            sc_fm = sb("sc_fm", [128, DC, BL])
            mod_rows = sb("mod_rows", [BL, 6 * D])
            wmod_c = sb("wmod_c", [128, DC, 512])
            scr = sb("setup_scr", [128, DC, BL])
            tmp = sb("setup_tmp", [128, DC, BL])
            gbrow = [sb(f"gbrow{b}", [1, D]) for b in range(BL)]
            pv = ps("pv", [128, 16])
            pm = ps("pm", [BL, 512])
            pgb = ps("pgb", [128, 512])
            self.sld.dinc(nc.sync.dma_start(out=self.identf[:],
                                            in_=self.t_identf.ap()))
            self.sld.dinc(nc.sync.dma_start(out=self.identb[:],
                                            in_=self.t_identb.ap()))
            for i in range(7):
                self.sld.dinc(nc.sync.dma_start(out=row7[i][:],
                                                in_=self.d_vec.ap()[i:i + 1, :]))
            self.sld.dinc(nc.sync.dma_start(out=c_sb[:], in_=self.d_c2.ap()))
            self.sld.dinc(nc.sync.dma_start(out=bphi_row[:], in_=self.d_bphi.ap()))
            self.wsld(nc.vector)
            self.V.memset(self.ones_r[:], 1.0)
            self.V.memset(self.ones_rb[:], 1.0)
            self.V.memset(self.cst_b[:], 1.0 / 1024.0)
            self.V.memset(self.ones_cb[:], 1.0)
            self.V.memset(self.h2max[:], 0.0)
            self.wsld(nc.tensor)
            for vi, dst in ((0, self.bz_t), (1, self.g_t), (2, self.lb_t),
                            (3, self.bo_t)):
                self.wv(nc.tensor)
                for k in range(DC):
                    ins = nc.tensor.transpose(
                        pv[:, k:k + 1], row7[vi][0:1, 128 * k:128 * (k + 1)],
                        self.identf[0:1, 0:1])
                self.tse.inc(ins)
                self.wt(nc.vector)
                self.V.tensor_copy(dst[:], pv[:, 0:DC])
            for vi, dst in ((4, self.bq_dup), (5, self.br_dup)):
                self.wv(nc.tensor)
                for h in range(H):
                    ins = nc.tensor.transpose(
                        pv[0:64, h:h + 1], row7[vi][0:1, 64 * h:64 * (h + 1)],
                        self.identf[0:1, 0:1])
                self.tse.inc(ins)
                self.wt(nc.vector)
                self.V.tensor_copy(dst[0:64, :], pv[0:64, 0:H])
                self.wv(nc.gpsimd)
                self.gld.dinc(nc.gpsimd.dma_start(out=dst[64:128, :],
                                                  in_=dst[0:64, :]))
            self.V.tensor_copy(self.bh_row[:], row7[6][:])
            self.wv(nc.tensor)
            ins = nc.tensor.transpose(pv[0:64, 0:1], bphi_row[0:1, :],
                                      self.identf[0:1, 0:1])
            self.tse.inc(ins)
            self.wt(nc.vector)
            self.V.tensor_copy(self.bphi_v[0:64, 0:1],
                                               pv[0:64, 0:1])
            self.wv(nc.gpsimd)
            self.gld.dinc(nc.gpsimd.dma_start(out=self.bphi_v[64:128, 0:1],
                                              in_=self.bphi_v[0:64, 0:1]))
            self.wgld(nc.vector)
            self.V.tensor_scalar(
                self.bphi_v[0:64, 0:1], self.bphi_v[0:64, 0:1], PI / 2.0,
                None, ALU.add)
            if SETUP_UPTO == "vec":
                self.barrier()
                return
            self.wsld(nc.scalar)
            self.A.activation(c_si[:], c_sb[:], AF.Silu,
                                              bias=0.0, scale=1.0)
            self.wa(nc.tensor)
            self.wv(nc.tensor)
            for k in range(DC):
                ins = nc.tensor.transpose(
                    pv[:, 2 * k:2 * k + BL], c_si[0:BL, 128 * k:128 * (k + 1)],
                    self.identf[0:BL, 0:BL])
            self.tse.inc(ins)
            self.wt(nc.vector)
            self.V.tensor_copy(
                sc_fm[:].rearrange("p k b -> p (k b)"), pv[:, 0:2 * DC])
            for j in range(12):
                self.guard_load(nc.gpsimd)
                self.gld.dinc(nc.gpsimd.dma_start(
                    out=wmod_c[:],
                    in_=self.d_wmod.ap().rearrange("(k p) j -> p k j", p=128)
                    [:, :, 512 * j:512 * (j + 1)]))
                self.wgld(nc.tensor)
                self.wv(nc.tensor)
                for k in range(DC):
                    ins = nc.tensor.matmul(pm[:], sc_fm[:, k, :], wmod_c[:, k, :],
                                           start=(k == 0), stop=(k == DC - 1))
                self.tse.inc(ins)
                self.wt(nc.vector)
                self.V.tensor_copy(
                    mod_rows[:, 512 * j:512 * (j + 1)], pm[:])
            for param in range(6):
                for k in range(DC):
                    off = param * D + 128 * k
                    self.wv(nc.tensor)
                    ins = nc.tensor.transpose(
                        pv[:, 0:BL], mod_rows[0:BL, off:off + 128],
                        self.identf[0:BL, 0:BL])
                    self.tse.inc(ins)
                    self.wt(nc.vector)
                    self.V.tensor_copy(
                        self.mod_t[:, k, param, :], pv[:, 0:BL])
            for param in (1, 4):
                self.V.tensor_scalar(
                    self.mod_t[:, :, param, :], self.mod_t[:, :, param, :],
                    1.0, None, ALU.add)
            for param in (2, 5):
                g = self.mod_t[:, :, param, :]
                self.wv(nc.scalar)
                self.A.activation(scr[:], g, AF.Sign, bias=0.0, scale=1.0)
                self.A.activation(tmp[:], g, AF.Abs,
                                                  bias=0.0, scale=1.0)
                self.wa(nc.vector)
                self.V.tensor_scalar(tmp[:], tmp[:], 0.25, None, ALU.max)
                self.V.tensor_mul(g, tmp[:], scr[:])
            self.V.tensor_mul(
                self.gabo_t[:], self.mod_t[:, :, 2, :],
                self.bo_t[:, :, None].broadcast_to([128, DC, BL]))
            for k in range(DC):
                for b in range(BL):
                    self.wv(nc.tensor)
                    ins = nc.tensor.transpose(pm[0:1, 0:128],
                                              self.mod_t[:, k, 5, b:b + 1],
                                              self.identf[:])
                    self.tse.inc(ins)
                    self.wt(nc.vector)
                    self.V.tensor_copy(
                        gbrow[b][0:1, 128 * k:128 * (k + 1)],
                        pm[0:1, 0:128])
            for b in range(BL):
                for jc in range(2):
                    self.wv(nc.tensor)
                    ins = nc.tensor.matmul(
                        pgb[:], self.ones_r[:],
                        gbrow[b][0:1, 512 * jc:512 * (jc + 1)],
                        start=True, stop=True)
                    self.tse.inc(ins)
                    self.wt(nc.vector)
                    self.V.tensor_copy(
                        self.gbbc[:, b, 512 * jc:512 * (jc + 1)], pgb[:])
            if SETUP_UPTO != "mod":
                self._dequant_ffn_weights(ctx)
        self.barrier()
        if "mod" in self.taps:
            self._tap("mod", self.mod_t[:].rearrange("p k q b -> p (k q b)"),
                      [128, DC * 6 * BL], F32)
        if "gabo" in self.taps:
            self._tap("gabo", self.gabo_t[:].rearrange("p k b -> p (k b)"),
                      [128, DC * BL], F32)

    def _dequant_ffn_weights(self, ctx):
        nc = self.nc
        ps = lambda name, shape, dt=F32: ctx.enter_context(
            nc.psum_tensor(name, shape, dt))
        ptr = ps("ptr_dq", [128, 512], BF16)

        def dequant(dsrc, nblk, nchunk, name):
            s2 = ExitStack()
            cols = nblk * 32
            w = s2.enter_context(nc.sbuf_tensor(f"wdq_{name}",
                                                [128, nchunk, cols], F32))
            m = s2.enter_context(nc.sbuf_tensor(f"wdm_{name}",
                                                [128, nchunk * nblk], F32))
            si = s2.enter_context(nc.sbuf_tensor(f"wds_{name}",
                                                 [128, nchunk * nblk], F32))
            ki = s2.enter_context(nc.sbuf_tensor(f"wdk_{name}",
                                                 [128, nchunk, cols], I32))
            kf = s2.enter_context(nc.sbuf_tensor(f"wdf_{name}",
                                                 [128, nchunk, cols], F32))
            wb = s2.enter_context(nc.sbuf_tensor(f"wdb_{name}",
                                                 [128, nchunk, cols], BF16))
            nch = w.shape[1]
            third = (nch + 2) // 3
            for c0 in range(0, nch, third):
                c1 = min(c0 + third, nch)
                self.gld.dinc(nc.gpsimd.dma_start(
                    out=w[:, c0:c1, :],
                    in_=dsrc.ap().rearrange("(c p) i -> p c i", p=128)
                    [:, c0:c1, :]))
            self.wgld(nc.vector)
            wv4 = w[:].rearrange("p c (b j) -> p c b j", j=32)
            if DEQ_UPTO >= 2:
                self.V.tensor_reduce(
                    m[:].rearrange("p (c b) -> p c b", b=nblk), wv4, AX.X,
                    ALU.max, apply_absolute_value=True)
                self.V.tensor_scalar(m[:], m[:], 1.0 / 7.0, 1e-12,
                                        ALU.mult, ALU.max)
                self.V.reciprocal(si[:], m[:])
            else:
                self.V.memset(m[:], 1.0)
                self.V.memset(si[:], 1.0)
            if DEQ_UPTO >= 3:
                self.V.tensor_mul(
                    wv4, wv4,
                    si[:].rearrange("p (c b) -> p c b", b=nblk)
                    [:, :, :, None].broadcast_to([128, nchunk, nblk, 32]))
            if DEQ_UPTO >= 4:
                self.V.tensor_copy(ki[:], w[:])
                self.V.tensor_copy(kf[:], ki[:])
            else:
                self.V.tensor_copy(kf[:], w[:])
            if DEQ_UPTO >= 5:
                self.V.tensor_mul(
                    wb[:].rearrange("p c (b j) -> p c b j", j=32),
                    kf[:].rearrange("p c (b j) -> p c b j", j=32),
                    m[:].rearrange("p (c b) -> p c b", b=nblk)
                    [:, :, :, None].broadcast_to([128, nchunk, nblk, 32]))
            else:
                self.V.tensor_copy(wb[:], kf[:])
            return s2, wb

        for dsrc, dmy, nm in ((self.d_wup, self.d_upT_my, "up"),
                              (self.d_wgt, self.d_gtT_my, "gt")):
            s2, wb = dequant(dsrc, 4, IC, nm)
            if DEQ_UPTO < 6:
                s2.close()
                continue
            with s2:
                wt_ = s2.enter_context(nc.sbuf_tensor(f"wdt_{nm}",
                                                      [128, IC, 128], BF16))
                for ic in range(IC):
                    g = ic % 4
                    self.wv(nc.tensor)
                    ins = nc.tensor.transpose(ptr[:, 128 * g:128 * (g + 1)],
                                              wb[:, ic, :], self.identb[:])
                    self.tse.inc(ins)
                    self.wt(nc.vector)
                    self.V.tensor_copy(
                        wt_[:, ic, :], ptr[:, 128 * g:128 * (g + 1)])
                self.wv(nc.gpsimd)
                self.gld.dinc(nc.gpsimd.dma_start(
                    out=dmy.ap().rearrange("p (c j) -> p c j", j=128),
                    in_=wt_[:]))
                nc.gpsimd.wait_ge(self.gld.h, self.gld.v)
        s2, wb = dequant(self.d_wdn, 12, DC, "dn")
        if DEQ_UPTO < 6:
            s2.close()
            return
        with s2:
            wt_ = s2.enter_context(nc.sbuf_tensor("wdt_dn",
                                                  [128, 3, DC, 128], BF16))
            for it in range(3):
                for oc in range(DC):
                    g = (it * DC + oc) % 4
                    self.wv(nc.tensor)
                    ins = nc.tensor.transpose(ptr[:, 128 * g:128 * (g + 1)],
                                              wb[:, oc, 128 * it:128 * (it + 1)],
                                              self.identb[:])
                    self.tse.inc(ins)
                    self.wt(nc.vector)
                    self.V.tensor_copy(
                        wt_[:, it, oc, :], ptr[:, 128 * g:128 * (g + 1)])
            self.wv(nc.gpsimd)
            for it in range(3):
                self.gld.dinc(nc.gpsimd.dma_start(
                    out=self.d_dnT_my.ap()[128 * it:128 * (it + 1), :]
                    .rearrange("p (oc j) -> p oc j", j=128),
                    in_=wt_[:, it, :, :]))
            nc.gpsimd.wait_ge(self.gld.h, self.gld.v)
        if SETUP_UPTO == "deq":
            return
        for my, full in ((self.d_upT_my, self.d_upT),
                         (self.d_gtT_my, self.d_gtT),
                         (self.d_dnT_my, self.d_dnT)):
            if SIM_LOCAL:
                rows = my.shape[0]
                for rr in range(NCORES):
                    self.gld.dinc(nc.gpsimd.dma_start(
                        out=full.ap()[rr * rows:(rr + 1) * rows, :],
                        in_=my.ap()))
            else:
                self.cc.inc(nc.gpsimd.collective_compute(
                    "AllGather", ALU.bypass,
                    replica_groups=[list(range(NCORES))],
                    ins=[my.ap().opt()], outs=[full.ap().opt()]))
        nc.gpsimd.wait_ge(self.gld.h, self.gld.v)
        nc.gpsimd.wait_ge(self.cc.h, self.cc.v)


    def _ln_stats(self, xt, xsqt, pR, pBc):
        nc = self.nc
        self.wa(nc.tensor)
        self.wv(nc.tensor)
        for k in range(DC):
            ins = nc.tensor.matmul(pR[:], self.cst_b[:], xt[:, k, :],
                                   start=(k == 0), stop=(k == DC - 1))
        self.tse.inc(ins)
        self.wt(nc.vector)
        self.V.tensor_copy(self.mu_r[:], pR[:])
        self.wv(nc.tensor)
        for k in range(DC):
            ins = nc.tensor.matmul(pR[:], self.cst_b[:], xsqt[:, k, :],
                                   start=(k == 0), stop=(k == DC - 1))
        self.tse.inc(ins)
        self.wt(nc.vector)
        self.V.tensor_copy(self.ms_r[:], pR[:])
        self.V.tensor_mul(self.a_r[:], self.mu_r[:], self.mu_r[:])
        self.V.tensor_sub(self.a_r[:], self.ms_r[:], self.a_r[:])
        self.V.tensor_scalar(self.a_r[:], self.a_r[:], LN_EPS, None, ALU.add)
        self.wv(nc.scalar)
        self.A.activation(self.a_r[:], self.a_r[:], AF.Sqrt,
                                          bias=0.0, scale=1.0)
        self.wa(nc.vector)
        self.V.reciprocal(self.a_r[:], self.a_r[:])
        self.wv(nc.tensor)
        ins = nc.tensor.matmul(pBc[:, 0:512], self.ones_r[:], self.mu_r[:],
                               start=True, stop=True)
        self.tse.inc(ins)
        ins = nc.tensor.matmul(pBc[:, 512:1024], self.ones_r[:], self.a_r[:],
                               start=True, stop=True)
        self.tse.inc(ins)


    def a1(self, b):
        nc = self.nc
        ctx = ExitStack()
        sb = lambda name, shape, dt=F32: ctx.enter_context(
            nc.sbuf_tensor(name, shape, dt))
        ps = lambda name, shape, dt=F32: ctx.enter_context(
            nc.psum_tensor(name, shape, dt))
        with ctx:
            wslot = sb(f"a1_wslot{b}", [128, DC, D], BF16)
            self.load_weight(self.d_Wz, wslot)
            xc = sb(f"a1_xc{b}", [128, DC, 512], BF16)
            xsq = sb(f"a1_xsq{b}", [128, DC, 512], BF16)
            hc = sb(f"a1_hc{b}", [128, DC, 512], BF16)
            d1 = sb(f"a1_d1{b}", [128, DC, 512], BF16)
            d2 = sb(f"a1_d2{b}", [128, 512])
            zc = sb(f"a1_zc{b}", [128, DC, 512], BF16)
            pR = ps(f"a1_pR{b}", [1, 512])
            pBc = ps(f"a1_pBc{b}", [128, 1024])
            pZ0 = ps(f"a1_pZ0{b}", [128, 512])
            pZ1 = ps(f"a1_pZ1{b}", [128, 512])
            for ch in range(4):
                t0 = 512 * ch
                self.guard_load(nc.gpsimd)
                self.gld.dinc(nc.gpsimd.dma_start(
                    out=xc[:],
                    in_=self.d_x.ap()[b].rearrange("(k p) t -> p k t", p=128)
                    [:, :, t0:t0 + 512]))
                self.wgld(nc.scalar)
                self.A.activation(xsq[:], xc[:], AF.Square,
                                                  bias=0.0, scale=1.0)
                self._ln_stats(xc, xsq, pR, pBc)
                self.wt(nc.vector)
                self.V.tensor_sub(
                    d1[:], xc[:],
                    pBc[:, 0:512][:, None, :].broadcast_to([128, DC, 512]))
                for k in range(DC):
                    self.wa(nc.vector)
                    self.V.scalar_tensor_tensor(
                        d2[:], d1[:, k, :], self.mod_t[:, k, 1, b:b + 1],
                        pBc[:, 512:1024], ALU.mult, ALU.mult)
                    self.wv(nc.scalar)
                    self.A.activation(
                        hc[:, k, :], d2[:], AF.Identity,
                        bias=self.mod_t[:, k, 0, b:b + 1], scale=1.0)
                self.wa(nc.tensor)
                ztok = [self.tok(), self.tok()]
                for j in range(DC):
                    jp = j % 2
                    pz = pZ0 if jp == 0 else pZ1
                    self.wtok(nc.tensor, ztok[jp])
                    for k in range(DC):
                        ins = nc.tensor.matmul(
                            pz[:], wslot[:, k, 128 * j:128 * (j + 1)],
                            hc[:, k, :], start=(k == 0), stop=(k == DC - 1))
                    self.tse.inc(ins)
                    eng, E = ((nc.vector, self.V) if jp == 0
                              else (nc.scalar, self.A))
                    self.wt(eng)
                    if jp == 0:
                        E.tensor_scalar(zc[:, j, :], pz[:],
                                        self.bz_t[:, j:j + 1], None, ALU.add)
                    else:
                        E.activation(zc[:, j, :], pz[:], AF.Identity,
                                     bias=self.bz_t[:, j:j + 1], scale=1.0)
                    ztok[jp] = self.tok()
                self.wa(nc.scalar)
                self.wv(nc.scalar)
                self.A.activation(xsq[:], zc[:], AF.Square,
                                                  bias=0.0, scale=1.0)
                self._ln_stats(zc, xsq, pR, pBc)
                self.wt(nc.vector)
                self.V.tensor_sub(
                    d1[:], zc[:],
                    pBc[:, 0:512][:, None, :].broadcast_to([128, DC, 512]))
                for k in range(DC):
                    self.wa(nc.vector)
                    self.V.scalar_tensor_tensor(
                        d2[:], d1[:, k, :], self.g_t[:, k:k + 1],
                        pBc[:, 512:1024], ALU.mult, ALU.mult)
                    self.wv(nc.scalar)
                    self.A.activation(
                        self.zln[:, k, t0:t0 + 512], d2[:], AF.Identity,
                        bias=self.lb_t[:, k:k + 1], scale=1.0)


    def a2(self, b):
        nc = self.nc
        ctx = ExitStack()
        sb = lambda name, shape, dt=F32: ctx.enter_context(
            nc.sbuf_tensor(name, shape, dt))
        ps = lambda name, shape, dt=F32: ctx.enter_context(
            nc.psum_tensor(name, shape, dt))
        with ctx:
            hvx = ExitStack()
            wslot = hvx.enter_context(
                nc.sbuf_tensor(f"a2_wslot{b}", [128, DC, D], BF16))
            self.load_weight(self.d_Wh, wslot)
            pG = [ps(f"a2_pG{i}{b}", [128, 1024]) for i in range(2)]
            gtokv = [self.tok(), self.tok()]
            pOut = ps(f"a2_pOut{b}", [128, 2048])
            otok = self.tok()

            def grp(i):
                self.wtok(nc.tensor, gtokv[i])
                return pG[i]

            def evac_eng(i):
                return (nc.vector, self.V) if i % 2 == 0 else (nc.scalar, self.A)

            for tt in range(16):
                g = tt % 2
                pz = grp(g)
                for fc in range(2):
                    ins = nc.tensor.matmul(
                        pz[:, 512 * fc:512 * (fc + 1)], self.ones_r[:],
                        self.bh_row[:, 512 * fc:512 * (fc + 1)],
                        start=True, stop=False)
                    for k in range(DC):
                        ins = nc.tensor.matmul(
                            pz[:, 512 * fc:512 * (fc + 1)],
                            self.zln[:, k, 128 * tt:128 * (tt + 1)],
                            wslot[:, k, 512 * fc:512 * (fc + 1)],
                            start=False, stop=(k == DC - 1))
                self.tse.inc(ins)
                eng, E = evac_eng(tt)
                self.wt(eng)
                if tt % 2 == 0:
                    E.tensor_copy(self.hv[:, tt, :], pz[:])
                else:
                    E.activation(self.hv[:, tt, :], pz[:], AF.Identity,
                                 bias=0.0, scale=1.0)
                gtokv[g] = self.tok()
            if "Hv" in self.taps and b == 0:
                self.barrier()
                self._tap("Hv", self.hv[:].rearrange("p t f -> p (t f)"),
                          [128, 16 * D], BF16)
            hvx.close()
            wqd = sb(f"a2_wqd{b}", [128, 2, DC, 128], BF16)
            wrd = sb(f"a2_wrd{b}", [128, 2, DC, 128], BF16)
            wphid = sb(f"a2_wphid{b}", [64, 128], BF16)
            qd = sb(f"a2_qd{b}", [128, 2, 2, 2048], BF16)
            rd = sb(f"a2_rd{b}", [128, 2, 2, 2048], BF16)
            trig = sb(f"a2_trig{b}", [128, 2048])
            ki = sb(f"a2_ki{b}", [128, 2048], I32)
            cs = sb(f"a2_cs{b}", [128, 2048], BF16)
            rctm = sb(f"a2_rctm{b}", [128, 2048], BF16)
            msb = sb(f"a2_msb{b}", [128, 128], BF16)
            mqc = sb(f"a2_mqc{b}", [128, 2048], BF16)
            ysb = sb(f"a2_ysb{b}", [128, 16])
            itp = sb(f"a2_itp{b}", [16, 128])
            invr = sb(f"a2_invr{b}", [1, 2, 2048], BF16)
            invbc = sb(f"a2_invbc{b}", [128, 2, 2048], BF16)
            sct = sb(f"a2_sct{b}", [128, 2048], BF16)
            self.guard_load(nc.gpsimd)
            self.gld.dinc(nc.gpsimd.dma_start(out=wphid[:, 0:64],
                                              in_=self.d_Wphi.ap()))
            self.gld.dinc(nc.gpsimd.dma_start(out=wphid[:, 64:128],
                                              in_=self.d_Wphi.ap()))

            gtk0 = [None]
            gtk1 = [self.tok()]

            def et(E):
                s = self.vse if E is self.V else self.ase
                return (s.h, s.v)

            def wet(eng, t):
                if t is not None:
                    eng.wait_ge(t[0], t[1])

            def grp1():
                self.wtok(nc.tensor, gtk1[0])
                return pG[1]

            def chain_gen(p):
                pr = p % 2
                hA, hB = 2 * p, 2 * p + 1
                self.wt(nc.gpsimd)
                nc.gpsimd.wait_ge(self.gld.h, self.gld.v)
                for (dst, dram) in ((wqd, self.d_Wq), (wrd, self.d_Wr)):
                    for hi, h in enumerate((hA, hB)):
                        srcw = dram.ap().rearrange("(k p) j -> p k j", p=128)
                        srcw = srcw[:, :, HD * h:HD * (h + 1)]
                        self.gld.dinc(nc.gpsimd.dma_start(
                            out=dst[:, hi, :, 0:64], in_=srcw))
                        self.gld.dinc(nc.gpsimd.dma_start(
                            out=dst[:, hi, :, 64:128], in_=srcw))
                yield
                self.wgld(nc.tensor)
                for ti, (dst, wsl, bias) in enumerate(
                        ((qd, wqd, self.bq_dup), (rd, wrd, self.br_dup))):
                    for hi, h in enumerate((hA, hB)):
                        for g2 in range(2):
                            pz = grp1()
                            for half in range(2):
                                qc = 2 * g2 + half
                                for k in range(DC):
                                    ins = nc.tensor.matmul(
                                        pz[:, 512 * half:512 * (half + 1)],
                                        wsl[:, hi, k, :],
                                        self.zln[:, k,
                                                 512 * qc:512 * (qc + 1)],
                                        start=(k == 0), stop=(k == DC - 1))
                            self.tse.inc(ins)
                            eng, E = evac_eng(2 * ti + hi + g2)
                            self.wt(eng)
                            if (2 * ti + hi + g2) % 2 == 0:
                                E.tensor_scalar(
                                    dst[:, pr, hi, 1024 * g2:1024 * (g2 + 1)],
                                    pz[:], bias[:, h:h + 1], None, ALU.add)
                            else:
                                E.activation(
                                    dst[:, pr, hi, 1024 * g2:1024 * (g2 + 1)],
                                    pz[:], AF.Identity,
                                    bias=bias[:, h:h + 1], scale=1.0)
                            gtk1[0] = self.tok()
                            yield
                for hi in range(2):
                    for ti, srcd in enumerate((qd, rd)):
                        for g2 in range(2):
                            pz = grp1()
                            for half in range(2):
                                qc = 2 * g2 + half
                                ins = nc.tensor.matmul(
                                    pz[:, 512 * half:512 * (half + 1)],
                                    wphid[:, :],
                                    srcd[0:64, pr, hi,
                                         512 * qc:512 * (qc + 1)],
                                    start=True, stop=True)
                                self.tse.inc(ins)
                            self.wt(nc.scalar)
                            self.A.activation(
                                trig[:, 1024 * g2:1024 * (g2 + 1)], pz[:],
                                AF.Identity, bias=self.bphi_v[:], scale=1.0)
                            gtk1[0] = self.tok()
                            yield
                        self.wa(nc.vector)
                        self.V.tensor_scalar(ki[:].bitcast(F32), trig[:],
                                             INV_2PI, None, ALU.mult)
                        self.V.tensor_copy(ki[:], ki[:].bitcast(F32))
                        self.V.scalar_tensor_tensor(
                            trig[:], ki[:], -TWO_PI, trig[:], ALU.mult,
                            ALU.add)
                        yield
                        self.wv(nc.gpsimd)
                        self.G.tensor_scalar(
                            trig[:], trig[:], CLAMP, -CLAMP, ALU.min, ALU.max)
                        yield
                        self.wg(nc.scalar)
                        self.A.activation(
                            cs[:], trig[:], AF.Sin, bias=0.0, scale=1.0)
                        yield
                        self.wa(nc.vector)
                        self.V.tensor_mul(srcd[:, pr, hi, :],
                                          srcd[:, pr, hi, :], cs[:])
                        yield
                for hi in range(2):
                    self.wv(nc.tensor)
                    for kq in range(2):
                        pz = grp1()
                        v = pz[:].bitcast(BF16)
                        for j in range(8):
                            kt = 8 * kq + j
                            ins = nc.tensor.transpose(
                                v[:, 1024 * kq + 128 * j:
                                  1024 * kq + 128 * (j + 1)],
                                rd[:, pr, hi, 128 * kt:128 * (kt + 1)],
                                self.identb[:])
                            self.tse.inc(ins)
                        self.wt(nc.vector)
                        self.V.tensor_copy(
                            rctm[:, 1024 * kq:1024 * (kq + 1)],
                            v[:, 1024 * kq:1024 * (kq + 1)])
                        gtk1[0] = self.tok()
                        yield
                    pzM = grp1()
                    for kt in range(16):
                        ins = nc.tensor.matmul(
                            pzM[:, 0:128], rctm[:, 128 * kt:128 * (kt + 1)],
                            rctm[:, 128 * kt:128 * (kt + 1)],
                            start=(kt == 0), stop=(kt == 15))
                    self.tse.inc(ins)
                    self.wt(nc.vector)
                    self.V.tensor_copy(msb[:], pzM[:, 0:128])
                    gtk1[0] = self.tok()
                    yield
                    for g2 in range(2):
                        pz = grp1()
                        for half in range(2):
                            qc = 2 * g2 + half
                            ins = nc.tensor.matmul(
                                pz[:, 512 * half:512 * (half + 1)], msb[:],
                                qd[:, pr, hi, 512 * qc:512 * (qc + 1)],
                                start=True, stop=True)
                            self.tse.inc(ins)
                        eng, E = evac_eng(g2)
                        self.wt(eng)
                        if g2 % 2 == 0:
                            E.tensor_copy(mqc[:, 1024 * g2:1024 * (g2 + 1)],
                                          pz[:])
                        else:
                            E.activation(mqc[:, 1024 * g2:1024 * (g2 + 1)],
                                         pz[:], AF.Identity, bias=0.0,
                                         scale=1.0)
                        gtk1[0] = self.tok()
                        yield
                    self.wa(nc.vector)
                    self.V.tensor_mul(mqc[:], mqc[:], qd[:, pr, hi, :])
                    gtk1[0] = self.tok()
                    yield
                    pzY = grp1()
                    for qt in range(16):
                        ins = nc.tensor.matmul(
                            pzY[:, qt:qt + 1],
                            mqc[:, 128 * qt:128 * (qt + 1)],
                            self.ones_cb[:], start=True, stop=True)
                        self.tse.inc(ins)
                    self.wt(nc.vector)
                    self.V.tensor_scalar(
                        ysb[:], pzY[:, 0:16], 1e-24, None, ALU.max)
                    gtk1[0] = self.tok()
                    yield
                    self.wv(nc.scalar)
                    self.A.activation(ysb[:], ysb[:], AF.Sqrt,
                                      bias=0.0, scale=1.0)
                    self.wa(nc.vector)
                    self.V.reciprocal(ysb[:], ysb[:])
                    yield
                    pzT = grp1()
                    self.wv(nc.tensor)
                    ins = nc.tensor.transpose(pzT[0:16, 0:128], ysb[:],
                                              self.identf[:])
                    self.tse.inc(ins)
                    self.wt(nc.vector)
                    self.wgld(nc.vector)
                    self.V.tensor_copy(itp[:], pzT[0:16, 0:128])
                    gtk1[0] = self.tok()
                    self.wv(nc.gpsimd)
                    self.gld.dinc(nc.gpsimd.dma_start(out=invr[:, hi, :],
                                                      in_=itp[:]))
                    yield
                self.wgld(nc.tensor)
                for g2 in range(2):
                    pz = grp1()
                    for half in range(2):
                        qc = 2 * g2 + half
                        ins = nc.tensor.matmul(
                            pz[0:64, 512 * half:512 * (half + 1)],
                            self.ones_rb[:, 0:64],
                            invr[:, 0, 512 * qc:512 * (qc + 1)],
                            start=True, stop=True)
                        self.tse.inc(ins)
                        ins = nc.tensor.matmul(
                            pz[64:128, 512 * half:512 * (half + 1)],
                            self.ones_rb[:, 0:64],
                            invr[:, 1, 512 * qc:512 * (qc + 1)],
                            start=True, stop=True, tile_position=(0, 64))
                        self.tse.inc(ins)
                    eng, E = evac_eng(g2)
                    self.wt(eng)
                    if g2 % 2 == 0:
                        E.tensor_copy(invbc[:, pr, 1024 * g2:1024 * (g2 + 1)],
                                      pz[:])
                    else:
                        E.activation(invbc[:, pr, 1024 * g2:1024 * (g2 + 1)],
                                     pz[:], AF.Identity, bias=0.0, scale=1.0)
                    gtk1[0] = self.tok()
                    yield

            def scores_gen(p, otc):
                pr = p % 2
                hA, hB = 2 * p, 2 * p + 1
                for hi, h in enumerate((hA, hB)):
                    for kt in range(16):
                        stok = [None, None]
                        for g2 in range(2):
                            wet(nc.tensor, gtk0[0])
                            for half in range(2):
                                qc = 2 * g2 + half
                                ins = nc.tensor.matmul(
                                    pG[0][:, 512 * half:512 * (half + 1)],
                                    rd[:, pr, hi, 128 * kt:128 * (kt + 1)],
                                    qd[:, pr, hi, 512 * qc:512 * (qc + 1)],
                                    start=True, stop=True)
                                self.tse.inc(ins)
                            eng, E = evac_eng(g2 + kt)
                            self.wt(eng)
                            if (g2 + kt) % 2 == 0:
                                E.tensor_copy(
                                    sct[:, 1024 * g2:1024 * (g2 + 1)],
                                    pG[0][:])
                            else:
                                E.activation(
                                    sct[:, 1024 * g2:1024 * (g2 + 1)],
                                    pG[0][:], AF.Identity, bias=0.0,
                                    scale=1.0)
                            gtk0[0] = et(E)
                            stok[g2] = gtk0[0]
                        for qc in range(4):
                            wet(nc.tensor, stok[qc // 2])
                            if kt == 0:
                                wet(nc.tensor, otc[0])
                            ins = nc.tensor.matmul(
                                pOut[64 * hi:64 * (hi + 1),
                                     512 * qc:512 * (qc + 1)],
                                self.hv[:, kt, HD * h:HD * (h + 1)],
                                sct[:, 512 * qc:512 * (qc + 1)],
                                start=(kt == 0), stop=(kt == 15),
                                tile_position=(0, 64 * hi))
                            self.tse.inc(ins)
                        yield
                self.wt(nc.vector)
                self.wa(nc.vector)
                self.V.tensor_mul(self.outfm[:, p, :], pOut[:],
                                  invbc[:, pr, :])
                otc[0] = et(self.V)

            _SENT = object()

            def advance(gen, n):
                if gen is None:
                    return False
                for _ in range(n):
                    if next(gen, _SENT) is _SENT:
                        return False
                return True

            gtk0[0] = et(self.V)
            cg = chain_gen(0)
            while advance(cg, 8):
                pass
            otc = [None]
            for p in range(NPAIR):
                sg = scores_gen(p, otc)
                cg = chain_gen(p + 1) if p + 1 < NPAIR else None
                s_alive, c_alive = True, True
                while s_alive or c_alive:
                    s_alive = advance(sg, 1)
                    c_alive = advance(cg, 2)
            if "Qc" in self.taps and b == 0:
                self.barrier()
                self._tap("Qc", qd[:, (NPAIR - 1) % 2, 0, :], [128, 2048], BF16)
                self._tap("Rc", rd[:, (NPAIR - 1) % 2, 0, :], [128, 2048], BF16)
                self._tap("invbc", invbc[:, (NPAIR - 1) % 2, :], [128, 2048], BF16)
            if "outfm" in self.taps and b == 0:
                self.barrier()
                self._tap("outfm", self.outfm[:].rearrange("p c q -> p (c q)"),
                          [128, DC * 2048], BF16)


    def wo_epilogue(self, b):
        nc = self.nc
        ctx = ExitStack()
        sb = lambda name, shape, dt=F32: ctx.enter_context(
            nc.sbuf_tensor(name, shape, dt))
        ps = lambda name, shape, dt=F32: ctx.enter_context(
            nc.psum_tensor(name, shape, dt))
        with ctx:
            wslot = sb(f"wo_wslot{b}", [128, DC, D], BF16)
            self.load_weight(self.d_Wo, wslot)
            pA = ps(f"wo_pA{b}", [128, 512])
            pB = ps(f"wo_pB{b}", [128, 512])
            pR = ps(f"wo_pR{b}", [1, 512])
            pBc = ps(f"wo_pBc{b}", [128, 1024])
            pTr = ps(f"wo_pTr{b}", [128, 512])
            x2c = sb(f"wo_x2c{b}", [128, DC, 512])
            x2b = sb(f"wo_x2b{b}", [128, DC, 512], BF16)
            x2sq = sb(f"wo_x2sq{b}", [128, DC, 512], BF16)
            xfc = sb(f"wo_xfc{b}", [128, DC, 512])
            h2c = sb(f"wo_h2c{b}", [128, DC, 512], BF16)
            d1 = sb(f"wo_d1{b}", [128, DC, 512], BF16)
            d2 = sb(f"wo_d2{b}", [128, 512])
            x2t = sb(f"wo_x2t{b}", [128, 4, 128])
            for qc in range(4):
                t0 = 512 * qc
                gt0 = b * S + t0
                self.guard_load(nc.sync)
                self.sld.dinc(nc.sync.dma_start(
                    out=xfc[:],
                    in_=self.d_x.ap()[b].rearrange("(k p) t -> p k t", p=128)
                    [:, :, t0:t0 + 512]))
                wtok2 = [self.tok(), self.tok()]
                for e in range(DC):
                    ep = e % 2
                    pz = pA if ep == 0 else pB
                    self.wtok(nc.tensor, wtok2[ep])
                    for k in range(DC):
                        ins = nc.tensor.matmul(
                            pz[:], wslot[:, k, 128 * e:128 * (e + 1)],
                            self.outfm[:, k, t0:t0 + 512],
                            start=(k == 0), stop=(k == DC - 1))
                    self.tse.inc(ins)
                    self.wt(nc.scalar)
                    self.A.activation(
                        x2c[:, e, :], pz[:], AF.Identity,
                        bias=self.gabo_t[:, e, b:b + 1],
                        scale=self.mod_t[:, e, 2, b:b + 1])
                    wtok2[ep] = self.tok()
                    self.wsld(nc.vector)
                    self.wa(nc.vector)
                    self.V.tensor_add(x2c[:, e, :], x2c[:, e, :],
                                         xfc[:, e, :])
                    self.V.tensor_copy(x2b[:, e, :],
                                                       x2c[:, e, :])
                    self.wv(nc.sync)
                    self.sld.dinc(nc.sync.dma_start(
                        out=self.d_x2f[128 * e:128 * (e + 1), gt0:gt0 + 512],
                        in_=x2c[:, e, :]))
                    self.wv(nc.tensor)
                    self.wgld(nc.tensor)
                    for j in range(4):
                        ins = nc.tensor.transpose(
                            pTr[:, 128 * j:128 * (j + 1)],
                            x2c[:, e, 128 * j:128 * (j + 1)], self.identf[:])
                    self.tse.inc(ins)
                    self.wt(nc.vector)
                    self.V.tensor_copy(
                        x2t[:].rearrange("p j f -> p (j f)"), pTr[:])
                    self.wv(nc.gpsimd)
                    self.gld.dinc(nc.gpsimd.dma_start(
                        out=self.d_x2t.ap()
                        [gt0:gt0 + 512, 128 * e:128 * (e + 1)]
                        .rearrange("(j p) f -> p j f", p=128),
                        in_=x2t[:]))
                self.wv(nc.scalar)
                self.A.activation(x2sq[:], x2b[:], AF.Square,
                                                  bias=0.0, scale=1.0)
                self._ln_stats(x2b, x2sq, pR, pBc)
                self.wt(nc.vector)
                self.wgld(nc.vector)
                self.V.tensor_sub(
                    d1[:], x2b[:],
                    pBc[:, 0:512][:, None, :].broadcast_to([128, DC, 512]))
                for k in range(DC):
                    self.wa(nc.vector)
                    self.V.scalar_tensor_tensor(
                        d2[:], d1[:, k, :], self.mod_t[:, k, 4, b:b + 1],
                        pBc[:, 512:1024], ALU.mult, ALU.mult)
                    self.wv(nc.scalar)
                    self.A.activation(
                        h2c[:, k, :], d2[:], AF.Identity,
                        bias=self.mod_t[:, k, 3, b:b + 1], scale=1.0)
                self.wa(nc.vector)
                self.V.tensor_reduce(
                    d2[:, 0:1], h2c[:].rearrange("p k t -> p (k t)"), AX.X,
                    ALU.max, apply_absolute_value=True)
                self.V.tensor_max(self.h2max[:],
                                                  self.h2max[:], d2[:, 0:1])
                self.wv(nc.gpsimd)
                self.gld.dinc(nc.gpsimd.dma_start(
                    out=self.d_h2[:, gt0:gt0 + 512]
                    .rearrange("(k p) t -> p k t", p=128),
                    in_=h2c[:]))
                self.wgld(nc.scalar)


    _gs_n = 0

    def global_scale(self, idx, locmax_col, nvals=1):
        nc = self.nc
        ctx = ExitStack()
        with ctx:
            ps = lambda name, shape, dt=F32: ctx.enter_context(
                nc.psum_tensor(name, shape, dt))
            sbl = lambda name, shape, dt=F32: ctx.enter_context(
                nc.sbuf_tensor(name, shape, dt))
            Prog._gs_n += 1
            u = Prog._gs_n
            pT = ps(f"gs_p{idx}_{u}", [128, 128])
            mrow = sbl(f"gs_m{idx}_{u}", [1, 64])
            m2 = sbl(f"gs_m2{idx}_{u}", [2, 1])
            srow = sbl(f"gs_s{idx}_{u}", [1, 16])
            self.wv(nc.tensor)
            self.wa(nc.tensor)
            ins = nc.tensor.transpose(pT[0:nvals, 0:128],
                                      locmax_col[:, 0:nvals], self.identf[:])
            self.tse.inc(ins)
            self.wt(nc.vector)
            self.V.memset(mrow[:], 0.0)
            self.V.tensor_reduce(m2[0:nvals, :], pT[0:nvals, 0:128], AX.X,
                                 ALU.max)
            self.wv(nc.gpsimd)
            self.gld.dinc(nc.gpsimd.dma_start(out=mrow[0:1, 0:nvals],
                                              in_=m2[0:nvals, :]))
            nc.gpsimd.wait_ge(self.gld.h, self.gld.v)
            self.gld.dinc(nc.gpsimd.dma_start(out=self.d_arin[idx].ap(),
                                              in_=mrow[:]))
            nc.gpsimd.wait_ge(self.gld.h, self.gld.v)
            if SIM_LOCAL:
                self.gld.dinc(nc.gpsimd.dma_start(
                    out=self.d_arout[idx].ap(), in_=self.d_arin[idx].ap()))
                nc.gpsimd.wait_ge(self.gld.h, self.gld.v)
            else:
                self.cc.inc(nc.gpsimd.collective_compute(
                    "AllReduce", ALU.max, replica_groups=[list(range(NCORES))],
                    ins=[self.d_arin[idx].ap().opt()],
                    outs=[self.d_arout[idx].ap().opt()]))
                nc.gpsimd.wait_ge(self.cc.h, self.cc.v)
            self.gld.dinc(nc.gpsimd.dma_start(out=mrow[:],
                                              in_=self.d_arout[idx].ap()))
            self.wgld(nc.vector)
            self.V.memset(srow[:], 1.0)
            self.V.tensor_scalar(srow[:, 0:nvals], mrow[:, 0:nvals],
                                    1.0 / QMAX, 1e-12, ALU.mult, ALU.max)
            self.V.reciprocal(srow[:, 8:8 + nvals],
                                              srow[:, 0:nvals])
            self.wv(nc.tensor)
            ins = nc.tensor.matmul(pT[:, 0:16], self.ones_r[:], srow[:],
                                   start=True, stop=True)
            self.tse.inc(ins)
            self.wt(nc.vector)
            for i in range(nvals):
                self.V.tensor_copy(
                    self.sc_col[:, 2 * (idx + i):2 * (idx + i) + 1],
                    pT[:, i:i + 1])
                self.V.tensor_copy(
                    self.sc_col[:, 2 * (idx + i) + 1:2 * (idx + i) + 2],
                    pT[:, 8 + i:8 + i + 1])


    def ffn(self):
        nc = self.nc
        ctx = ExitStack()
        sb = lambda name, shape, dt=F32: ctx.enter_context(
            nc.sbuf_tensor(name, shape, dt))
        ps = lambda name, shape, dt=F32: ctx.enter_context(
            nc.psum_tensor(name, shape, dt))
        mx2 = self.ctx.enter_context(nc.sbuf_tensor("ffn_mx2", [128, 2], F32))
        with ctx:
            hq = sb("ffn_hq", [128, DC, T], BF16)
            t1 = sb("ffn_t1", [128, 2048])
            ki = sb("ffn_ki", [128, 2048], I32)
            hb = sb("ffn_hb", [128, DC, 512], BF16)
            wu = [sb(f"ffn_wu{i}", [128, DC, 128], BF16) for i in range(2)]
            wg = [sb(f"ffn_wg{i}", [128, DC, 128], BF16) for i in range(2)]

            def load_slices(j, par):
                nc.sync.wait_ge(self.tse.h, self.tse.v)
                self.sld.dinc(nc.sync.dma_start(
                    out=wu[par][:],
                    in_=self.d_upT.ap().rearrange("(k p) i -> p k i", p=128)
                    [:, :, 128 * j:128 * (j + 1)]))
                self.sld.dinc(nc.sync.dma_start(
                    out=wg[par][:],
                    in_=self.d_gtT.ap().rearrange("(k p) i -> p k i", p=128)
                    [:, :, 128 * j:128 * (j + 1)]))

            for ch in range(8):
                t0 = 512 * ch
                self.guard_load(nc.gpsimd)
                self.gld.dinc(nc.gpsimd.dma_start(
                    out=hb[:], in_=self.d_h2[:, t0:t0 + 512]
                    .rearrange("(k p) t -> p k t", p=128)))
                self.wgld(nc.vector)
                for kg in range(2):
                    sl = slice(kg * 4, kg * 4 + 4)
                    hv_ = hb[:, sl, :].rearrange("p k t -> p (k t)")
                    t1v = t1[:].rearrange("p (k t) -> p k t", t=512)
                    kiv = ki[:].rearrange("p (k t) -> p k t", t=512)
                    self.V.tensor_scalar(t1[:], hv_, self.sc_col[:, 1:2],
                                            None, ALU.mult)
                    self.V.tensor_copy(ki[:], t1[:])
                    self.V.tensor_scalar(
                        hq[:, sl, t0:t0 + 512],
                        kiv, self.sc_col[:, 0:1], None, ALU.mult)
            if "hq" in self.taps:
                self.barrier()
                self._tap("hq", hq[:].rearrange("p k t -> p (k t)"),
                          [128, DC * T], BF16)
            if self.upto == "hq":
                self.barrier()
                return
            pU0 = ps("ffn_pU0", [128, 512])
            pU1 = ps("ffn_pU1", [128, 512])
            pG0 = ps("ffn_pG0", [128, 512])
            pG1 = ps("ffn_pG1", [128, 512])
            upc = sb("ffn_upc", [128, 4, 512], BF16)
            gtc = sb("ffn_gtc", [128, 4, 512], BF16)
            red = sb("ffn_red", [128, 1])
            self.V.memset(mx2[:], 0.0)
            load_slices(0, 0)
            utok = [self.tok(), self.tok()]
            gtok = [self.tok(), self.tok()]
            sttok = [self.tok()] * 4
            for ic in range(IC):
                wpar = ic % 2
                self.wsld(nc.tensor)
                if ic + 1 < IC:
                    load_slices(ic + 1, 1 - wpar)
                for tc in range(8):
                    par = tc % 2
                    pu = pU0 if par == 0 else pU1
                    pg = pG0 if par == 0 else pG1
                    self.wtok(nc.tensor, utok[par])
                    for k in range(DC):
                        ins = nc.tensor.matmul(
                            pu[:], wu[wpar][:, k, :],
                            hq[:, k, 512 * tc:512 * (tc + 1)],
                            start=(k == 0), stop=(k == DC - 1))
                    self.tse.inc(ins)
                    self.wtok(nc.tensor, gtok[par])
                    for k in range(DC):
                        ins = nc.tensor.matmul(
                            pg[:], wg[wpar][:, k, :],
                            hq[:, k, 512 * tc:512 * (tc + 1)],
                            start=(k == 0), stop=(k == DC - 1))
                    self.tse.inc(ins)
                    slot = (ic * 8 + tc) % 4
                    self.wt(nc.vector)
                    self.wtok(nc.vector, sttok[slot])
                    self.V.tensor_copy(upc[:, slot, :], pu[:])
                    utok[par] = self.tok()
                    self.wt(nc.scalar)
                    self.wtok(nc.scalar, sttok[slot])
                    self.A.activation(
                        gtc[:, slot, :], pg[:], AF.Identity, bias=0.0,
                        scale=1.0)
                    gtok[par] = self.tok()
                    self.wa(nc.vector)
                    self.V.tensor_reduce(red[:], upc[:, slot, :], AX.X,
                                            ALU.max, apply_absolute_value=True)
                    self.V.tensor_max(mx2[:, 0:1], mx2[:, 0:1], red[:])
                    self.V.tensor_reduce(red[:], gtc[:, slot, :], AX.X,
                                            ALU.max, apply_absolute_value=True)
                    self.V.tensor_max(mx2[:, 1:2],
                                                      mx2[:, 1:2], red[:])
                    self.wv(nc.sync)
                    self.wa(nc.sync)
                    self.sld.dinc(nc.sync.dma_start(
                        out=self.d_up[128 * ic:128 * (ic + 1),
                                      512 * tc:512 * (tc + 1)],
                        in_=upc[:, slot, :]))
                    self.sld.dinc(nc.sync.dma_start(
                        out=self.d_gt[128 * ic:128 * (ic + 1),
                                      512 * tc:512 * (tc + 1)],
                        in_=gtc[:, slot, :]))
                    if slot == 3:
                        self.scommit()
                        for s_ in range(4):
                            sttok[s_] = (0, 0, 0, self.scm.v)
        self.barrier()
        if self.upto == "s1":
            if "up" in self.taps:
                self._tap("up", self.d_up.ap(), [I3, T], BF16)
            if "gt" in self.taps:
                self._tap("gt", self.d_gt.ap(), [I3, T], BF16)
            return
        self.global_scale(1, mx2[:], nvals=2)
        self.barrier()
        ctx = ExitStack()
        sb = lambda name, shape, dt=F32: ctx.enter_context(
            nc.sbuf_tensor(name, shape, dt))
        with ctx:
            bufA = sb("ffn_bA", [128, 3, T], BF16)
            bufB = sb("ffn_bB", [128, 3, T], BF16)
            kk = sb("ffn_kk", [128, 2, T], I32)
            kb = sb("ffn_kb", [128, 2, T], I32)
            mxc = sb("ffn_mxc", [128, IC])
            ltok = [None, None, None]
            ktok = [None, None]
            for ic in range(IC):
                p3 = ic % 3
                p2 = ic % 2
                if ltok[p3]:
                    self.wtok(nc.sync, ltok[p3])
                self.sld.dinc(nc.sync.dma_start(
                    out=bufA[:, p3, :],
                    in_=self.d_gt[128 * ic:128 * (ic + 1), :]))
                self.wsld(nc.scalar)
                if ktok[p2]:
                    nc.scalar.wait_ge(self.vse.h, ktok[p2])
                self.A.activation(kk[:, p2, :].bitcast(F32), bufA[:, p3, :],
                                  AF.Identity, bias=0.0,
                                  scale=self.sc_col[:, 5:6])
                self.wa(nc.vector)
                self.V.tensor_copy(kk[:, p2, :], kk[:, p2, :].bitcast(F32))
                self.V.tensor_scalar(bufB[:, p3, :], kk[:, p2, :],
                                     self.sc_col[:, 4:5], None, ALU.mult)
                self.wv(nc.scalar)
                self.A.activation(kb[:, p2, :].bitcast(F32), bufB[:, p3, :],
                                  AF.Sigmoid, bias=0.0, scale=1.0)
                self.A.activation(kb[:, p2, :].bitcast(F32),
                                  kb[:, p2, :].bitcast(F32),
                                  AF.Identity, bias=0.0, scale=65536.0)
                self.wa(nc.vector)
                self.V.tensor_copy(kb[:, p2, :], kb[:, p2, :].bitcast(F32))
                self.V.scalar_tensor_tensor(
                    bufA[:, p3, :], kb[:, p2, :], 65535.0, bufB[:, p3, :],
                    ALU.min, ALU.mult)
                self.V.tensor_reduce(mxc[:, ic:ic + 1], bufA[:, p3, :], AX.X,
                                     ALU.max, apply_absolute_value=True)
                ktok[p2] = self.vse.v
                self.wv(nc.sync)
                self.sld.dinc(nc.sync.dma_start(
                    out=self.d_g2[128 * ic:128 * (ic + 1), :],
                    in_=bufA[:, p3, :]))
                self.scommit()
                ltok[p3] = (0, 0, 0, self.scm.v)
            self.V.tensor_reduce(mx2[:, 0:1], mxc[:], AX.X, ALU.max)
            self.barrier()
            if self.upto == "s2":
                if "g2" in self.taps:
                    self._tap("g2", self.d_g2.ap(), [I3, T], BF16)
                return
            self.global_scale(3, mx2[:, 0:1], nvals=1)
            self.barrier()
            ltok = [None, None, None]
            ktok = [None, None]
            for ic in range(IC):
                p3 = ic % 3
                p2 = ic % 2
                if ltok[p3]:
                    self.wtok(nc.sync, ltok[p3])
                self.sld.dinc(nc.sync.dma_start(
                    out=bufA[:, p3, :],
                    in_=self.d_g2[128 * ic:128 * (ic + 1), :]))
                self.sld.dinc(nc.sync.dma_start(
                    out=bufB[:, p3, :],
                    in_=self.d_up[128 * ic:128 * (ic + 1), :]))
                self.wsld(nc.scalar)
                if ktok[p2]:
                    nc.scalar.wait_ge(self.vse.h, ktok[p2])
                self.A.activation(kk[:, p2, :].bitcast(F32), bufA[:, p3, :],
                                  AF.Identity, bias=0.0,
                                  scale=self.sc_col[:, 7:8])
                self.A.activation(kb[:, p2, :].bitcast(F32), bufB[:, p3, :],
                                  AF.Identity, bias=0.0,
                                  scale=self.sc_col[:, 3:4])
                self.wa(nc.vector)
                self.V.tensor_copy(kk[:, p2, :], kk[:, p2, :].bitcast(F32))
                self.V.tensor_copy(kb[:, p2, :], kb[:, p2, :].bitcast(F32))
                self.V.tensor_mul(bufA[:, p3, :], kk[:, p2, :], kb[:, p2, :])
                self.V.tensor_reduce(mxc[:, ic:ic + 1], bufA[:, p3, :], AX.X,
                                     ALU.max, apply_absolute_value=True)
                ktok[p2] = self.vse.v
                self.wv(nc.sync)
                self.sld.dinc(nc.sync.dma_start(
                    out=self.d_o[128 * ic:128 * (ic + 1), :],
                    in_=bufA[:, p3, :]))
                self.scommit()
                ltok[p3] = (0, 0, 0, self.scm.v)
            self.V.tensor_reduce(mx2[:, 0:1], mxc[:], AX.X, ALU.max)
            self.V.tensor_scalar(mx2[:, 0:1], mx2[:, 0:1],
                                 self.sc_col[:, 6:7], None, ALU.mult)
            self.V.tensor_scalar(mx2[:, 0:1], mx2[:, 0:1],
                                 self.sc_col[:, 2:3], None, ALU.mult)
            self.V.tensor_scalar(mx2[:, 0:1], mx2[:, 0:1], 1.0 / 65536.0,
                                 None, ALU.mult)
        self.barrier()
        if self.upto == "s3":
            if "o" in self.taps:
                self._tap("o", self.d_o.ap(), [I3, T], BF16)
            return
        self.global_scale(0, mx2[:, 0:1], nvals=1)
        self.V.tensor_mul(self.sc_col[:, 9:10], self.sc_col[:, 6:7],
                          self.sc_col[:, 2:3])
        self.V.tensor_scalar(self.sc_col[:, 9:10], self.sc_col[:, 9:10],
                             self.sc_col[:, 1:2], None, ALU.mult)
        self.V.tensor_scalar(self.sc_col[:, 9:10], self.sc_col[:, 9:10],
                             1.0 / 65536.0, None, ALU.mult)
        self.barrier()
        ctx = ExitStack()
        sb = lambda name, shape, dt=F32: ctx.enter_context(
            nc.sbuf_tensor(name, shape, dt))
        ps = lambda name, shape, dt=F32: ctx.enter_context(
            nc.psum_tensor(name, shape, dt))
        with ctx:
            wdnT = sb("ffn_wdnT", [128, IC, D], BF16)
            ot = sb("ffn_ot", [128, 2, IC, 128], BF16)
            otf = sb("ffn_otf", [128, 2, IC * 128])
            oki = sb("ffn_oki", [128, 2, IC * 128], I32)
            x2tc = sb("ffn_x2tc", [128, 2, 2, 512])
            osb = sb("ffn_osb", [128, 2, 512])
            pD0 = ps("ffn_pD0", [128, 512])
            pD1 = ps("ffn_pD1", [128, 512])
            for c0 in range(0, IC, 8):
                self.sld.dinc(nc.sync.dma_start(
                    out=wdnT[:, c0:c0 + 8, :],
                    in_=self.d_dnT.ap().rearrange("(c p) d -> p c d", p=128)
                    [:, c0:c0 + 8, :]))
            self.wsld(nc.tensor)
            dtok = [self.tok(), self.tok()]
            ostok = [self.tok(), self.tok()]
            sotok = [self.tok(), self.tok()]
            qtoks = [None] * 33

            def s4_load(tt):
                par = tt % 2
                t0 = 128 * tt
                self.wtok(nc.gpsimd, dtok[par])
                nc.gpsimd.wait_ge(self.gld.h, self.gld.v)
                for c0 in range(0, IC, 8):
                    self.gld.dinc(nc.gpsimd.dma_start(
                        out=ot[:, par, c0:c0 + 8, :], in_=self.d_o.ap()
                        .rearrange("(c p) t -> p c t", p=128)
                        [:, c0:c0 + 8, t0:t0 + 128]))
                self.wtok(nc.sync, dtok[par])
                self.sld.dinc(nc.sync.dma_start(
                    out=x2tc[:, par, :, :],
                    in_=self.d_x2t.ap()[t0:t0 + 128, :]
                    .rearrange("p (a f) -> p a f", f=512)))

            def s4_qdq(tt):
                par = tt % 2
                self.wgld(nc.vector)
                self.wsld(nc.vector)
                otv = ot[:, par, :, :].rearrange("p c t -> p (c t)")
                self.V.tensor_scalar(otf[:, par, :], otv,
                                     self.sc_col[:, 9:10], None, ALU.mult)
                self.V.tensor_copy(oki[:, par, :], otf[:, par, :])
                self.V.tensor_scalar(
                    otv, oki[:, par, :], self.sc_col[:, 0:1], None, ALU.mult)
                qtoks[tt] = self.tok()

            s4_load(0)
            s4_qdq(0)
            s4_load(1)
            for tt in range(32):
                b = tt // 16
                t0 = 128 * tt
                par = tt % 2
                if tt + 1 < 32:
                    s4_qdq(tt + 1)
                for a in range(2):
                    pz = pD0 if a == 0 else pD1
                    self.wtok(nc.tensor, qtoks[tt])
                    self.wtok(nc.tensor, ostok[a])
                    for c in range(IC):
                        ins = nc.tensor.matmul(
                            pz[:], ot[:, par, c, :],
                            wdnT[:, c, 512 * a:512 * (a + 1)],
                            start=(c == 0), stop=(c == IC - 1))
                    self.tse.inc(ins)
                    self.wt(nc.vector)
                    self.wtok(nc.vector, sotok[a])
                    self.V.tensor_mul(
                        osb[:, a, :], pz[:],
                        self.gbbc[:, b, 512 * a:512 * (a + 1)])
                    self.V.tensor_add(
                        osb[:, a, :], osb[:, a, :], x2tc[:, par, a, :])
                    ostok[a] = self.tok()
                dtok[par] = (self.vse.v, 0, 0, 0, self.tse.v)
                if tt + 2 < 32:
                    s4_load(tt + 2)
                self.wv(nc.sync)
                self.sld.dinc(nc.sync.dma_start(
                    out=self.d_out.ap()[b, (t0 - b * S):(t0 - b * S) + 128, :],
                    in_=osb[:].rearrange("p a f -> p (a f)")))
                self.scommit()
                sotok[0] = (0, 0, 0, self.scm.v)
                sotok[1] = (0, 0, 0, self.scm.v)

    def build(self):
        nc = self.nc
        self.setup()
        if self.upto == "setup":
            self._finish()
            return nc
        actx = ExitStack()
        sb = lambda name, shape, dt=F32: actx.enter_context(
            nc.sbuf_tensor(name, shape, dt))
        with actx:
            self.zln = sb("zln", [128, DC, S], BF16)
            self.hv = sb("hv", [128, 16, D], BF16)
            self.outfm = sb("outfm", [128, DC, S], BF16)
            for b in range(BL):
                self.a1(b)
                self.barrier()
                if self.upto == "a1" and b == 0:
                    if "zln" in self.taps:
                        self._tap("zln",
                                  self.zln[:].rearrange("p k t -> p (k t)"),
                                  [128, DC * S], BF16)
                    self._finish()
                    return nc
                self.a2(b)
                self.barrier()
                if self.upto == "a2" and b == 0:
                    self._finish()
                    return nc
                self.wo_epilogue(b)
                self.barrier()
        if self.upto == "attn":
            if "x2f" in self.taps:
                self._tap("x2f", self.d_x2f.ap(), [D, T], F32)
            if "x2t" in self.taps:
                self._tap("x2t", self.d_x2t.ap(), [T, D], F32)
            if "h2" in self.taps:
                self._tap("h2", self.d_h2.ap(), [D, T], BF16)
            self._finish()
            return nc
        self.global_scale(0, self.h2max[:], nvals=1)
        self.barrier()
        self.ffn()
        self.barrier()
        self._finish()
        return nc

    def _finish(self):
        nc = self.nc
        self.gcommit()
        self.scommit()
        nc.sync.wait_ge(self.sld.h, self.sld.v)
        nc.gpsimd.wait_ge(self.gld.h, self.gld.v)
        nc.gpsimd.wait_ge(self.cc.h, self.cc.v)


def build_program(upto="all", taps=()):
    p = Prog(upto=upto, taps=taps)
    nc = p.build()
    return nc, p


def make_feeds(inputs):
    x = np.asarray(inputs["L0_attn_query"])
    c = np.asarray(inputs["L0_c"])
    base = {
        "wmod": np.ascontiguousarray(inputs["L0_wmod"]),
        "Wz": np.ascontiguousarray(inputs["L0_attn_Wz"]),
        "Wq": np.ascontiguousarray(inputs["L0_attn_Wq"]),
        "Wr": np.ascontiguousarray(inputs["L0_attn_Wr"]),
        "Wh": np.ascontiguousarray(inputs["L0_attn_Wh"]),
        "Wo": np.ascontiguousarray(inputs["L0_attn_Wo"]),
        "Wphi": np.ascontiguousarray(inputs["L0_attn_Wphi"]),
        "bphi": np.ascontiguousarray(
            np.asarray(inputs["L0_attn_bphi"]).reshape(1, HD)),
        "vecs": np.ascontiguousarray(np.stack([
            np.asarray(inputs["L0_attn_bz"]),
            np.asarray(inputs["L0_attn_ln_g"]),
            np.asarray(inputs["L0_attn_ln_b"]),
            np.asarray(inputs["L0_attn_bo"]),
            np.asarray(inputs["L0_attn_bq"]),
            np.asarray(inputs["L0_attn_br"]),
            np.asarray(inputs["L0_attn_bh"]),
        ], axis=0)),
    }
    wup = np.asarray(inputs["L0_ffn_w_up"])
    wgt = np.asarray(inputs["L0_ffn_w_gate"])
    wdn = np.asarray(inputs["L0_ffn_w_down"])
    feeds = []
    for core in range(NCORES):
        f = dict(base)
        xb = x[2 * core:2 * core + 2]
        f["x_fm"] = np.ascontiguousarray(xb.transpose(0, 2, 1))
        f["c2"] = np.ascontiguousarray(c[2 * core:2 * core + 2])
        f["wup_sl"] = np.ascontiguousarray(wup[:, 128 * core:128 * (core + 1)])
        f["wgt_sl"] = np.ascontiguousarray(wgt[:, 128 * core:128 * (core + 1)])
        f["wdn_sl"] = np.ascontiguousarray(wdn[:, 384 * core:384 * (core + 1)])
        feeds.append(f)
    return feeds


def kernel(**inputs):
    feeds = make_feeds(inputs)
    nc, _ = build_program()
    res = run_bass_kernel_spmd(nc, feeds, core_ids=list(range(NCORES)))
    out = np.concatenate([np.asarray(r["out"]) for r in res.results], axis=0)
    return out.astype(np.float32, copy=False)


# version 10 (final)
# speedup vs baseline: 1.1430x; 1.0964x over previous
from contextlib import ExitStack

import numpy as np
import concourse.bass as bass
import concourse.mybir as mybir
from concourse.bass_utils import run_bass_kernel_spmd

F32 = mybir.dt.float32
BF16 = mybir.dt.bfloat16
I32 = mybir.dt.int32
AF = mybir.ActivationFunctionType
ALU = mybir.AluOpType
AX = mybir.AxisListType

NCORES = 8
BL = 2
S = 2048
T = BL * S
D = 1024
DC = D // 128
H = 16
HD = 64
NPAIR = H // 2
I3 = 3072
IC = I3 // 128
QMAX = 32767.0
PI = float(np.pi)
TWO_PI = 2.0 * PI
INV_2PI = 1.0 / TWO_PI
CLAMP = 3.1415925
LN_EPS = 1e-5
SIM_LOCAL = False
SETUP_UPTO = "all"
DEQ_UPTO = 9


class Sem:

    def __init__(self, nc, ctx, name):
        self.h = ctx.enter_context(nc.semaphore(name))
        self.v = 0

    def inc(self, instr, n=1):
        instr.then_inc(self.h, n)
        self.v += n
        return instr

    def dinc(self, instr):
        return self.inc(instr, 16)


class ChainEng:

    _PASS = {"wait_ge", "wait_op", "sem_inc", "dma_start", "engine_nop", "nop"}

    def __init__(self, eng, sem):
        self._eng = eng
        self._sem = sem

    def __getattr__(self, name):
        f = getattr(self._eng, name)
        if name in self._PASS or name.startswith("_"):
            return f

        def g(*a, **k):
            self._eng.wait_ge(self._sem.h, self._sem.v)
            ins = f(*a, **k)
            self._sem.inc(ins)
            return ins
        return g


class Prog:
    def __init__(self, upto="all", taps=()):
        self.upto = upto
        self.taps = set(taps)
        nc = bass.Bass("TRN2", target_bir_lowering=False)
        self.nc = nc
        self.ctx = ExitStack()
        self.tap_handles = {}
        self._declare_io()
        self._alloc_persistent()


    def _declare_io(self):
        nc = self.nc
        dr = lambda name, shape, dt=F32, kind="ExternalInput": nc.dram_tensor(
            name, shape, dt, kind=kind)
        self.d_x = dr("x_fm", [BL, D, S])
        self.d_c2 = dr("c2", [BL, D])
        self.d_wmod = dr("wmod", [D, 6 * D])
        self.d_Wz = dr("Wz", [D, D])
        self.d_Wq = dr("Wq", [D, D])
        self.d_Wr = dr("Wr", [D, D])
        self.d_Wh = dr("Wh", [D, D])
        self.d_Wo = dr("Wo", [D, D])
        self.d_Wphi = dr("Wphi", [HD, HD])
        self.d_vec = dr("vecs", [7, D])
        self.d_bphi = dr("bphi", [1, HD])
        self.d_wup = dr("wup_sl", [I3, 128])
        self.d_wgt = dr("wgt_sl", [I3, 128])
        self.d_wdn = dr("wdn_sl", [D, 384])
        self.d_out = dr("out", [BL, S, D], kind="ExternalOutput")
        di = lambda name, shape, dt: nc.dram_tensor(name, shape, dt, kind="Internal")
        self.d_x2f = di("x2f_hbm", [D, T], F32)
        self.d_x2t = di("x2t_hbm", [T, D], F32)
        self.d_h2 = di("h2_hbm", [D, T], BF16)
        self.d_up = di("up_hbm", [I3, T], BF16)
        self.d_gt = di("gt_hbm", [I3, T], BF16)
        self.d_g2 = di("g2_hbm", [I3, T], BF16)
        self.d_o = di("o_hbm", [I3, T], BF16)
        self.d_upT_my = di("upT_my", [128, I3], BF16)
        self.d_gtT_my = di("gtT_my", [128, I3], BF16)
        self.d_dnT_my = di("dnT_my", [384, D], BF16)
        self.d_upT = di("upT_all", [D, I3], BF16)
        self.d_gtT = di("gtT_all", [D, I3], BF16)
        self.d_dnT = di("dnT_all", [I3, D], BF16)
        self.d_arin = [di(f"arin{i}", [1, 64], F32) for i in range(4)]
        self.d_arout = [di(f"arout{i}", [1, 64], F32) for i in range(4)]
        self.t_identf = nc.inline_tensor(np.eye(128, dtype=np.float32), "identf")
        import ml_dtypes
        self.t_identb = nc.inline_tensor(
            np.eye(128, dtype=ml_dtypes.bfloat16), "identb")

    def _alloc_persistent(self):
        nc, ctx = self.nc, self.ctx
        sb = lambda name, shape, dt=F32: ctx.enter_context(
            nc.sbuf_tensor(name, shape, dt))
        self.identf = sb("identf_sb", [128, 128])
        self.identb = sb("identb_sb", [128, 128], BF16)
        self.ones_r = sb("ones_r", [1, 128])
        self.ones_rb = sb("ones_rb", [1, 128], BF16)
        self.cst_b = sb("cst_b", [128, 1], BF16)
        self.ones_cb = sb("ones_cb", [128, 1], BF16)
        self.bz_t = sb("bz_t", [128, DC])
        self.g_t = sb("g_t", [128, DC])
        self.lb_t = sb("lb_t", [128, DC])
        self.bo_t = sb("bo_t", [128, DC])
        self.bq_dup = sb("bq_dup", [128, H])
        self.br_dup = sb("br_dup", [128, H])
        self.bh_row = sb("bh_row", [1, D])
        self.bphi_v = sb("bphi_v", [128, 1])
        self.mod_t = sb("mod_t", [128, DC, 6, BL])
        self.gabo_t = sb("gabo_t", [128, DC, BL])
        self.gbbc = sb("gbbc", [128, BL, D], BF16)
        self.mu_r = sb("mu_r", [1, 512])
        self.ms_r = sb("ms_r", [1, 512])
        self.a_r = sb("a_r", [1, 512])
        self.sc_col = sb("sc_col", [128, 16])
        self.h2max = sb("h2max", [128, 1])
        self.sld = Sem(nc, ctx, "sld")
        self.gld = Sem(nc, ctx, "gld")
        self.cc = Sem(nc, ctx, "cc")
        self.tse = Sem(nc, ctx, "tse")
        self.vse = Sem(nc, ctx, "vse")
        self.ase = Sem(nc, ctx, "ase")
        self.gse = Sem(nc, ctx, "gse")
        self.sse = Sem(nc, ctx, "sse")
        self.V = None
        self.A = None
        self.gcm = Sem(nc, ctx, "gcm")
        self.scm = Sem(nc, ctx, "scm")
        self._gcommitted = 0
        self._scommitted = 0
        self.V = ChainEng(nc.vector, self.vse)
        self.A = ChainEng(nc.scalar, self.ase)
        self.G = ChainEng(nc.gpsimd, self.gse)


    def barrier(self):
        nc = self.nc
        self.gcommit()
        self.scommit()
        nc.sync.wait_ge(self.sld.h, self.sld.v)
        nc.gpsimd.wait_ge(self.gld.h, self.gld.v)
        nc.gpsimd.wait_ge(self.cc.h, self.cc.v)
        nc.sync.wait_ge(self.sse.h, self.sse.v)
        nc.sync.sem_inc(self.sse.h, 1); self.sse.v += 1
        nc.tensor.wait_ge(self.tse.h, self.tse.v)
        nc.tensor.sem_inc(self.tse.h, 1); self.tse.v += 1
        nc.vector.wait_ge(self.vse.h, self.vse.v)
        nc.vector.sem_inc(self.vse.h, 1); self.vse.v += 1
        nc.scalar.wait_ge(self.ase.h, self.ase.v)
        nc.scalar.sem_inc(self.ase.h, 1); self.ase.v += 1
        nc.gpsimd.wait_ge(self.gse.h, self.gse.v)
        nc.gpsimd.sem_inc(self.gse.h, 1); self.gse.v += 1
        for eng in (nc.sync, nc.tensor, nc.vector, nc.scalar, nc.gpsimd):
            for s in (self.sse, self.tse, self.vse, self.ase, self.gse):
                eng.wait_ge(s.h, s.v)

    def gcommit(self):
        if self.gld.v > self._gcommitted:
            nc = self.nc
            nc.gpsimd.wait_ge(self.gld.h, self.gld.v)
            nc.gpsimd.wait_ge(self.gcm.h, self.gcm.v)
            nc.gpsimd.sem_inc(self.gcm.h, 1)
            self.gcm.v += 1
            self._gcommitted = self.gld.v

    def scommit(self):
        if self.sld.v > self._scommitted:
            nc = self.nc
            nc.sync.wait_ge(self.sld.h, self.sld.v)
            nc.sync.wait_ge(self.scm.h, self.scm.v)
            nc.sync.sem_inc(self.scm.h, 1)
            self.scm.v += 1
            self._scommitted = self.sld.v

    def tok(self):
        return (self.vse.v, self.ase.v, self.gse.v)

    def wtok(self, eng, t):
        if t[0]:
            eng.wait_ge(self.vse.h, t[0])
        if t[1]:
            eng.wait_ge(self.ase.h, t[1])
        if t[2]:
            eng.wait_ge(self.gse.h, t[2])
        if len(t) > 3 and t[3]:
            eng.wait_ge(self.scm.h, t[3])
        if len(t) > 4 and t[4]:
            eng.wait_ge(self.tse.h, t[4])

    def wv(self, eng):
        eng.wait_ge(self.vse.h, self.vse.v)

    def wt(self, eng):
        eng.wait_ge(self.tse.h, self.tse.v)

    def wa(self, eng):
        eng.wait_ge(self.ase.h, self.ase.v)

    def wsld(self, eng):
        self.scommit()
        eng.wait_ge(self.scm.h, self.scm.v)

    def wgld(self, eng):
        self.gcommit()
        eng.wait_ge(self.gcm.h, self.gcm.v)

    def wg(self, eng):
        eng.wait_ge(self.gse.h, self.gse.v)

    def guard_load(self, eng):
        self.wv(eng)
        self.wa(eng)
        self.wt(eng)
        if eng is self.nc.gpsimd:
            eng.wait_ge(self.gld.h, self.gld.v)
        elif eng is self.nc.sync:
            eng.wait_ge(self.sld.h, self.sld.v)

    def load_weight(self, dram, wslot):
        nc = self.nc
        self.guard_load(nc.gpsimd)
        self.gld.dinc(nc.gpsimd.dma_start(
            out=wslot[:],
            in_=dram.ap().rearrange("(k p) j -> p k j", p=128)))
        self.wgld(nc.tensor)


    def _tap(self, name, ap, shape, dt):
        nc = self.nc
        t = nc.dram_tensor(f"tap_{name}", shape, dt, kind="ExternalOutput")
        self.tap_handles[name] = t
        self.sld.dinc(nc.sync.dma_start(out=t.ap(), in_=ap))
        nc.sync.wait_ge(self.sld.h, self.sld.v)


    def setup(self):
        nc = self.nc
        ctx = ExitStack()
        sb = lambda name, shape, dt=F32: ctx.enter_context(
            nc.sbuf_tensor(name, shape, dt))
        ps = lambda name, shape, dt=F32: ctx.enter_context(
            nc.psum_tensor(name, shape, dt))
        with ctx:
            row7 = [sb(f"row7_{i}", [1, D]) for i in range(7)]
            bphi_row = sb("bphi_row", [1, HD])
            c_sb = sb("c_sb", [BL, D])
            c_si = sb("c_si", [BL, D])
            sc_fm = sb("sc_fm", [128, DC, BL])
            mod_rows = sb("mod_rows", [BL, 6 * D])
            wmod_c = sb("wmod_c", [128, DC, 512])
            scr = sb("setup_scr", [128, DC, BL])
            tmp = sb("setup_tmp", [128, DC, BL])
            gbrow = [sb(f"gbrow{b}", [1, D]) for b in range(BL)]
            pv = ps("pv", [128, 16])
            pm = ps("pm", [BL, 512])
            pgb = ps("pgb", [128, 512])
            self.sld.dinc(nc.sync.dma_start(out=self.identf[:],
                                            in_=self.t_identf.ap()))
            self.sld.dinc(nc.sync.dma_start(out=self.identb[:],
                                            in_=self.t_identb.ap()))
            for i in range(7):
                self.sld.dinc(nc.sync.dma_start(out=row7[i][:],
                                                in_=self.d_vec.ap()[i:i + 1, :]))
            self.sld.dinc(nc.sync.dma_start(out=c_sb[:], in_=self.d_c2.ap()))
            self.sld.dinc(nc.sync.dma_start(out=bphi_row[:], in_=self.d_bphi.ap()))
            self.wsld(nc.vector)
            self.V.memset(self.ones_r[:], 1.0)
            self.V.memset(self.ones_rb[:], 1.0)
            self.V.memset(self.cst_b[:], 1.0 / 1024.0)
            self.V.memset(self.ones_cb[:], 1.0)
            self.V.memset(self.h2max[:], 0.0)
            self.wsld(nc.tensor)
            for vi, dst in ((0, self.bz_t), (1, self.g_t), (2, self.lb_t),
                            (3, self.bo_t)):
                self.wv(nc.tensor)
                for k in range(DC):
                    ins = nc.tensor.transpose(
                        pv[:, k:k + 1], row7[vi][0:1, 128 * k:128 * (k + 1)],
                        self.identf[0:1, 0:1])
                self.tse.inc(ins)
                self.wt(nc.vector)
                self.V.tensor_copy(dst[:], pv[:, 0:DC])
            for vi, dst in ((4, self.bq_dup), (5, self.br_dup)):
                self.wv(nc.tensor)
                for h in range(H):
                    ins = nc.tensor.transpose(
                        pv[0:64, h:h + 1], row7[vi][0:1, 64 * h:64 * (h + 1)],
                        self.identf[0:1, 0:1])
                self.tse.inc(ins)
                self.wt(nc.vector)
                self.V.tensor_copy(dst[0:64, :], pv[0:64, 0:H])
                self.wv(nc.gpsimd)
                self.gld.dinc(nc.gpsimd.dma_start(out=dst[64:128, :],
                                                  in_=dst[0:64, :]))
            self.V.tensor_copy(self.bh_row[:], row7[6][:])
            self.wv(nc.tensor)
            ins = nc.tensor.transpose(pv[0:64, 0:1], bphi_row[0:1, :],
                                      self.identf[0:1, 0:1])
            self.tse.inc(ins)
            self.wt(nc.vector)
            self.V.tensor_copy(self.bphi_v[0:64, 0:1],
                                               pv[0:64, 0:1])
            self.wv(nc.gpsimd)
            self.gld.dinc(nc.gpsimd.dma_start(out=self.bphi_v[64:128, 0:1],
                                              in_=self.bphi_v[0:64, 0:1]))
            self.wgld(nc.vector)
            self.V.tensor_scalar(
                self.bphi_v[0:64, 0:1], self.bphi_v[0:64, 0:1], PI / 2.0,
                None, ALU.add)
            if SETUP_UPTO == "vec":
                self.barrier()
                return
            self.wsld(nc.scalar)
            self.A.activation(c_si[:], c_sb[:], AF.Silu,
                                              bias=0.0, scale=1.0)
            self.wa(nc.tensor)
            self.wv(nc.tensor)
            for k in range(DC):
                ins = nc.tensor.transpose(
                    pv[:, 2 * k:2 * k + BL], c_si[0:BL, 128 * k:128 * (k + 1)],
                    self.identf[0:BL, 0:BL])
            self.tse.inc(ins)
            self.wt(nc.vector)
            self.V.tensor_copy(
                sc_fm[:].rearrange("p k b -> p (k b)"), pv[:, 0:2 * DC])
            for j in range(12):
                self.guard_load(nc.gpsimd)
                self.gld.dinc(nc.gpsimd.dma_start(
                    out=wmod_c[:],
                    in_=self.d_wmod.ap().rearrange("(k p) j -> p k j", p=128)
                    [:, :, 512 * j:512 * (j + 1)]))
                self.wgld(nc.tensor)
                self.wv(nc.tensor)
                for k in range(DC):
                    ins = nc.tensor.matmul(pm[:], sc_fm[:, k, :], wmod_c[:, k, :],
                                           start=(k == 0), stop=(k == DC - 1))
                self.tse.inc(ins)
                self.wt(nc.vector)
                self.V.tensor_copy(
                    mod_rows[:, 512 * j:512 * (j + 1)], pm[:])
            for param in range(6):
                for k in range(DC):
                    off = param * D + 128 * k
                    self.wv(nc.tensor)
                    ins = nc.tensor.transpose(
                        pv[:, 0:BL], mod_rows[0:BL, off:off + 128],
                        self.identf[0:BL, 0:BL])
                    self.tse.inc(ins)
                    self.wt(nc.vector)
                    self.V.tensor_copy(
                        self.mod_t[:, k, param, :], pv[:, 0:BL])
            for param in (1, 4):
                self.V.tensor_scalar(
                    self.mod_t[:, :, param, :], self.mod_t[:, :, param, :],
                    1.0, None, ALU.add)
            for param in (2, 5):
                g = self.mod_t[:, :, param, :]
                self.wv(nc.scalar)
                self.A.activation(scr[:], g, AF.Sign, bias=0.0, scale=1.0)
                self.A.activation(tmp[:], g, AF.Abs,
                                                  bias=0.0, scale=1.0)
                self.wa(nc.vector)
                self.V.tensor_scalar(tmp[:], tmp[:], 0.25, None, ALU.max)
                self.V.tensor_mul(g, tmp[:], scr[:])
            self.V.tensor_mul(
                self.gabo_t[:], self.mod_t[:, :, 2, :],
                self.bo_t[:, :, None].broadcast_to([128, DC, BL]))
            for k in range(DC):
                for b in range(BL):
                    self.wv(nc.tensor)
                    ins = nc.tensor.transpose(pm[0:1, 0:128],
                                              self.mod_t[:, k, 5, b:b + 1],
                                              self.identf[:])
                    self.tse.inc(ins)
                    self.wt(nc.vector)
                    self.V.tensor_copy(
                        gbrow[b][0:1, 128 * k:128 * (k + 1)],
                        pm[0:1, 0:128])
            for b in range(BL):
                for jc in range(2):
                    self.wv(nc.tensor)
                    ins = nc.tensor.matmul(
                        pgb[:], self.ones_r[:],
                        gbrow[b][0:1, 512 * jc:512 * (jc + 1)],
                        start=True, stop=True)
                    self.tse.inc(ins)
                    self.wt(nc.vector)
                    self.V.tensor_copy(
                        self.gbbc[:, b, 512 * jc:512 * (jc + 1)], pgb[:])
            if SETUP_UPTO != "mod":
                self._dequant_ffn_weights(ctx)
        self.barrier()
        if "mod" in self.taps:
            self._tap("mod", self.mod_t[:].rearrange("p k q b -> p (k q b)"),
                      [128, DC * 6 * BL], F32)
        if "gabo" in self.taps:
            self._tap("gabo", self.gabo_t[:].rearrange("p k b -> p (k b)"),
                      [128, DC * BL], F32)

    def _dequant_ffn_weights(self, ctx):
        nc = self.nc
        ps = lambda name, shape, dt=F32: ctx.enter_context(
            nc.psum_tensor(name, shape, dt))
        ptr = ps("ptr_dq", [128, 512], BF16)

        def dequant(dsrc, nblk, nchunk, name):
            s2 = ExitStack()
            cols = nblk * 32
            w = s2.enter_context(nc.sbuf_tensor(f"wdq_{name}",
                                                [128, nchunk, cols], F32))
            m = s2.enter_context(nc.sbuf_tensor(f"wdm_{name}",
                                                [128, nchunk * nblk], F32))
            si = s2.enter_context(nc.sbuf_tensor(f"wds_{name}",
                                                 [128, nchunk * nblk], F32))
            ki = s2.enter_context(nc.sbuf_tensor(f"wdk_{name}",
                                                 [128, nchunk, cols], I32))
            kf = s2.enter_context(nc.sbuf_tensor(f"wdf_{name}",
                                                 [128, nchunk, cols], F32))
            wb = s2.enter_context(nc.sbuf_tensor(f"wdb_{name}",
                                                 [128, nchunk, cols], BF16))
            nch = w.shape[1]
            third = (nch + 2) // 3
            for c0 in range(0, nch, third):
                c1 = min(c0 + third, nch)
                self.gld.dinc(nc.gpsimd.dma_start(
                    out=w[:, c0:c1, :],
                    in_=dsrc.ap().rearrange("(c p) i -> p c i", p=128)
                    [:, c0:c1, :]))
            self.wgld(nc.vector)
            wv4 = w[:].rearrange("p c (b j) -> p c b j", j=32)
            if DEQ_UPTO >= 2:
                self.V.tensor_reduce(
                    m[:].rearrange("p (c b) -> p c b", b=nblk), wv4, AX.X,
                    ALU.max, apply_absolute_value=True)
                self.V.tensor_scalar(m[:], m[:], 1.0 / 7.0, 1e-12,
                                        ALU.mult, ALU.max)
                self.V.reciprocal(si[:], m[:])
            else:
                self.V.memset(m[:], 1.0)
                self.V.memset(si[:], 1.0)
            if DEQ_UPTO >= 3:
                self.V.tensor_mul(
                    wv4, wv4,
                    si[:].rearrange("p (c b) -> p c b", b=nblk)
                    [:, :, :, None].broadcast_to([128, nchunk, nblk, 32]))
            if DEQ_UPTO >= 4:
                self.V.tensor_copy(ki[:], w[:])
                self.V.tensor_copy(kf[:], ki[:])
            else:
                self.V.tensor_copy(kf[:], w[:])
            if DEQ_UPTO >= 5:
                self.V.tensor_mul(
                    wb[:].rearrange("p c (b j) -> p c b j", j=32),
                    kf[:].rearrange("p c (b j) -> p c b j", j=32),
                    m[:].rearrange("p (c b) -> p c b", b=nblk)
                    [:, :, :, None].broadcast_to([128, nchunk, nblk, 32]))
            else:
                self.V.tensor_copy(wb[:], kf[:])
            return s2, wb

        for dsrc, dmy, nm in ((self.d_wup, self.d_upT_my, "up"),
                              (self.d_wgt, self.d_gtT_my, "gt")):
            s2, wb = dequant(dsrc, 4, IC, nm)
            if DEQ_UPTO < 6:
                s2.close()
                continue
            with s2:
                wt_ = s2.enter_context(nc.sbuf_tensor(f"wdt_{nm}",
                                                      [128, IC, 128], BF16))
                for ic in range(IC):
                    g = ic % 4
                    self.wv(nc.tensor)
                    ins = nc.tensor.transpose(ptr[:, 128 * g:128 * (g + 1)],
                                              wb[:, ic, :], self.identb[:])
                    self.tse.inc(ins)
                    self.wt(nc.vector)
                    self.V.tensor_copy(
                        wt_[:, ic, :], ptr[:, 128 * g:128 * (g + 1)])
                self.wv(nc.gpsimd)
                self.gld.dinc(nc.gpsimd.dma_start(
                    out=dmy.ap().rearrange("p (c j) -> p c j", j=128),
                    in_=wt_[:]))
                nc.gpsimd.wait_ge(self.gld.h, self.gld.v)
        s2, wb = dequant(self.d_wdn, 12, DC, "dn")
        if DEQ_UPTO < 6:
            s2.close()
            return
        with s2:
            wt_ = s2.enter_context(nc.sbuf_tensor("wdt_dn",
                                                  [128, 3, DC, 128], BF16))
            for it in range(3):
                for oc in range(DC):
                    g = (it * DC + oc) % 4
                    self.wv(nc.tensor)
                    ins = nc.tensor.transpose(ptr[:, 128 * g:128 * (g + 1)],
                                              wb[:, oc, 128 * it:128 * (it + 1)],
                                              self.identb[:])
                    self.tse.inc(ins)
                    self.wt(nc.vector)
                    self.V.tensor_copy(
                        wt_[:, it, oc, :], ptr[:, 128 * g:128 * (g + 1)])
            self.wv(nc.gpsimd)
            for it in range(3):
                self.gld.dinc(nc.gpsimd.dma_start(
                    out=self.d_dnT_my.ap()[128 * it:128 * (it + 1), :]
                    .rearrange("p (oc j) -> p oc j", j=128),
                    in_=wt_[:, it, :, :]))
            nc.gpsimd.wait_ge(self.gld.h, self.gld.v)
        if SETUP_UPTO == "deq":
            return
        for my, full in ((self.d_upT_my, self.d_upT),
                         (self.d_gtT_my, self.d_gtT),
                         (self.d_dnT_my, self.d_dnT)):
            if SIM_LOCAL:
                rows = my.shape[0]
                for rr in range(NCORES):
                    self.gld.dinc(nc.gpsimd.dma_start(
                        out=full.ap()[rr * rows:(rr + 1) * rows, :],
                        in_=my.ap()))
            else:
                self.cc.inc(nc.gpsimd.collective_compute(
                    "AllGather", ALU.bypass,
                    replica_groups=[list(range(NCORES))],
                    ins=[my.ap().opt()], outs=[full.ap().opt()]))
        nc.gpsimd.wait_ge(self.gld.h, self.gld.v)
        nc.gpsimd.wait_ge(self.cc.h, self.cc.v)


    def _ln_stats(self, xt, xsqt, pR, pBc):
        nc = self.nc
        self.wa(nc.tensor)
        self.wv(nc.tensor)
        for k in range(DC):
            ins = nc.tensor.matmul(pR[:], self.cst_b[:], xt[:, k, :],
                                   start=(k == 0), stop=(k == DC - 1))
        self.tse.inc(ins)
        self.wt(nc.vector)
        self.V.tensor_copy(self.mu_r[:], pR[:])
        self.wv(nc.tensor)
        for k in range(DC):
            ins = nc.tensor.matmul(pR[:], self.cst_b[:], xsqt[:, k, :],
                                   start=(k == 0), stop=(k == DC - 1))
        self.tse.inc(ins)
        self.wt(nc.vector)
        self.V.tensor_copy(self.ms_r[:], pR[:])
        self.V.tensor_mul(self.a_r[:], self.mu_r[:], self.mu_r[:])
        self.V.tensor_sub(self.a_r[:], self.ms_r[:], self.a_r[:])
        self.V.tensor_scalar(self.a_r[:], self.a_r[:], LN_EPS, None, ALU.add)
        self.wv(nc.scalar)
        self.A.activation(self.a_r[:], self.a_r[:], AF.Sqrt,
                                          bias=0.0, scale=1.0)
        self.wa(nc.vector)
        self.V.reciprocal(self.a_r[:], self.a_r[:])
        self.wv(nc.tensor)
        ins = nc.tensor.matmul(pBc[:, 0:512], self.ones_r[:], self.mu_r[:],
                               start=True, stop=True)
        self.tse.inc(ins)
        ins = nc.tensor.matmul(pBc[:, 512:1024], self.ones_r[:], self.a_r[:],
                               start=True, stop=True)
        self.tse.inc(ins)


    def a1(self, b):
        nc = self.nc
        ctx = ExitStack()
        sb = lambda name, shape, dt=F32: ctx.enter_context(
            nc.sbuf_tensor(name, shape, dt))
        ps = lambda name, shape, dt=F32: ctx.enter_context(
            nc.psum_tensor(name, shape, dt))
        with ctx:
            wslot = sb(f"a1_wslot{b}", [128, DC, D], BF16)
            self.load_weight(self.d_Wz, wslot)
            xc = sb(f"a1_xc{b}", [128, DC, 512], BF16)
            xsq = sb(f"a1_xsq{b}", [128, DC, 512], BF16)
            hc = sb(f"a1_hc{b}", [128, DC, 512], BF16)
            d1 = sb(f"a1_d1{b}", [128, DC, 512], BF16)
            d2 = sb(f"a1_d2{b}", [128, 512])
            zc = sb(f"a1_zc{b}", [128, DC, 512], BF16)
            pR = ps(f"a1_pR{b}", [1, 512])
            pBc = ps(f"a1_pBc{b}", [128, 1024])
            pZ0 = ps(f"a1_pZ0{b}", [128, 512])
            pZ1 = ps(f"a1_pZ1{b}", [128, 512])
            for ch in range(4):
                t0 = 512 * ch
                self.guard_load(nc.gpsimd)
                self.gld.dinc(nc.gpsimd.dma_start(
                    out=xc[:],
                    in_=self.d_x.ap()[b].rearrange("(k p) t -> p k t", p=128)
                    [:, :, t0:t0 + 512]))
                self.wgld(nc.scalar)
                self.A.activation(xsq[:], xc[:], AF.Square,
                                                  bias=0.0, scale=1.0)
                self._ln_stats(xc, xsq, pR, pBc)
                self.wt(nc.vector)
                self.V.tensor_sub(
                    d1[:], xc[:],
                    pBc[:, 0:512][:, None, :].broadcast_to([128, DC, 512]))
                for k in range(DC):
                    self.wa(nc.vector)
                    self.V.scalar_tensor_tensor(
                        d2[:], d1[:, k, :], self.mod_t[:, k, 1, b:b + 1],
                        pBc[:, 512:1024], ALU.mult, ALU.mult)
                    self.wv(nc.scalar)
                    self.A.activation(
                        hc[:, k, :], d2[:], AF.Identity,
                        bias=self.mod_t[:, k, 0, b:b + 1], scale=1.0)
                self.wa(nc.tensor)
                ztok = [self.tok(), self.tok()]
                for j in range(DC):
                    jp = j % 2
                    pz = pZ0 if jp == 0 else pZ1
                    self.wtok(nc.tensor, ztok[jp])
                    for k in range(DC):
                        ins = nc.tensor.matmul(
                            pz[:], wslot[:, k, 128 * j:128 * (j + 1)],
                            hc[:, k, :], start=(k == 0), stop=(k == DC - 1))
                    self.tse.inc(ins)
                    eng, E = ((nc.vector, self.V) if jp == 0
                              else (nc.scalar, self.A))
                    self.wt(eng)
                    if jp == 0:
                        E.tensor_scalar(zc[:, j, :], pz[:],
                                        self.bz_t[:, j:j + 1], None, ALU.add)
                    else:
                        E.activation(zc[:, j, :], pz[:], AF.Identity,
                                     bias=self.bz_t[:, j:j + 1], scale=1.0)
                    ztok[jp] = self.tok()
                self.wa(nc.scalar)
                self.wv(nc.scalar)
                self.A.activation(xsq[:], zc[:], AF.Square,
                                                  bias=0.0, scale=1.0)
                self._ln_stats(zc, xsq, pR, pBc)
                self.wt(nc.vector)
                self.V.tensor_sub(
                    d1[:], zc[:],
                    pBc[:, 0:512][:, None, :].broadcast_to([128, DC, 512]))
                for k in range(DC):
                    self.wa(nc.vector)
                    self.V.scalar_tensor_tensor(
                        d2[:], d1[:, k, :], self.g_t[:, k:k + 1],
                        pBc[:, 512:1024], ALU.mult, ALU.mult)
                    self.wv(nc.scalar)
                    self.A.activation(
                        self.zln[:, k, t0:t0 + 512], d2[:], AF.Identity,
                        bias=self.lb_t[:, k:k + 1], scale=1.0)


    def a2(self, b):
        nc = self.nc
        ctx = ExitStack()
        sb = lambda name, shape, dt=F32: ctx.enter_context(
            nc.sbuf_tensor(name, shape, dt))
        ps = lambda name, shape, dt=F32: ctx.enter_context(
            nc.psum_tensor(name, shape, dt))
        with ctx:
            hvx = ExitStack()
            wslot = hvx.enter_context(
                nc.sbuf_tensor(f"a2_wslot{b}", [128, DC, D], BF16))
            self.load_weight(self.d_Wh, wslot)
            pG = [ps(f"a2_pG{i}{b}", [128, 1024]) for i in range(2)]
            gtokv = [self.tok(), self.tok()]
            pOut = ps(f"a2_pOut{b}", [128, 2048])
            otok = self.tok()

            def grp(i):
                self.wtok(nc.tensor, gtokv[i])
                return pG[i]

            def evac_eng(i):
                return (nc.vector, self.V) if i % 2 == 0 else (nc.scalar, self.A)

            for tt in range(16):
                g = tt % 2
                pz = grp(g)
                for fc in range(2):
                    ins = nc.tensor.matmul(
                        pz[:, 512 * fc:512 * (fc + 1)], self.ones_r[:],
                        self.bh_row[:, 512 * fc:512 * (fc + 1)],
                        start=True, stop=False)
                    for k in range(DC):
                        ins = nc.tensor.matmul(
                            pz[:, 512 * fc:512 * (fc + 1)],
                            self.zln[:, k, 128 * tt:128 * (tt + 1)],
                            wslot[:, k, 512 * fc:512 * (fc + 1)],
                            start=False, stop=(k == DC - 1))
                self.tse.inc(ins)
                eng, E = evac_eng(tt)
                self.wt(eng)
                if tt % 2 == 0:
                    E.tensor_copy(self.hv[:, tt, :], pz[:])
                else:
                    E.activation(self.hv[:, tt, :], pz[:], AF.Identity,
                                 bias=0.0, scale=1.0)
                gtokv[g] = self.tok()
            if "Hv" in self.taps and b == 0:
                self.barrier()
                self._tap("Hv", self.hv[:].rearrange("p t f -> p (t f)"),
                          [128, 16 * D], BF16)
            hvx.close()
            wqd = sb(f"a2_wqd{b}", [128, 2, DC, 128], BF16)
            wrd = sb(f"a2_wrd{b}", [128, 2, DC, 128], BF16)
            wphid = sb(f"a2_wphid{b}", [64, 128], BF16)
            qd = sb(f"a2_qd{b}", [128, 2, 2, 2048], BF16)
            rd = sb(f"a2_rd{b}", [128, 2, 2, 2048], BF16)
            trig = sb(f"a2_trig{b}", [128, 2048])
            ki = sb(f"a2_ki{b}", [128, 2048], I32)
            cs = sb(f"a2_cs{b}", [128, 2048], BF16)
            rctm = sb(f"a2_rctm{b}", [128, 2048], BF16)
            msb = sb(f"a2_msb{b}", [128, 128], BF16)
            mqc = sb(f"a2_mqc{b}", [128, 2048], BF16)
            ysb = sb(f"a2_ysb{b}", [128, 16])
            itp = sb(f"a2_itp{b}", [16, 128])
            invr = sb(f"a2_invr{b}", [1, 2, 2048], BF16)
            invbc = sb(f"a2_invbc{b}", [128, 2, 2048], BF16)
            sct = sb(f"a2_sct{b}", [128, 2048], BF16)
            self.guard_load(nc.gpsimd)
            self.gld.dinc(nc.gpsimd.dma_start(out=wphid[:, 0:64],
                                              in_=self.d_Wphi.ap()))
            self.gld.dinc(nc.gpsimd.dma_start(out=wphid[:, 64:128],
                                              in_=self.d_Wphi.ap()))

            gtk0 = [None]
            gtk1 = [self.tok()]

            def et(E):
                s = self.vse if E is self.V else self.ase
                return (s.h, s.v)

            def wet(eng, t):
                if t is not None:
                    eng.wait_ge(t[0], t[1])

            def grp1():
                self.wtok(nc.tensor, gtk1[0])
                return pG[1]

            def chain_gen(p):
                pr = p % 2
                hA, hB = 2 * p, 2 * p + 1
                self.wt(nc.gpsimd)
                nc.gpsimd.wait_ge(self.gld.h, self.gld.v)
                for (dst, dram) in ((wqd, self.d_Wq), (wrd, self.d_Wr)):
                    for hi, h in enumerate((hA, hB)):
                        srcw = dram.ap().rearrange("(k p) j -> p k j", p=128)
                        srcw = srcw[:, :, HD * h:HD * (h + 1)]
                        self.gld.dinc(nc.gpsimd.dma_start(
                            out=dst[:, hi, :, 0:64], in_=srcw))
                        self.gld.dinc(nc.gpsimd.dma_start(
                            out=dst[:, hi, :, 64:128], in_=srcw))
                yield
                self.wgld(nc.tensor)
                for ti, (dst, wsl, bias) in enumerate(
                        ((qd, wqd, self.bq_dup), (rd, wrd, self.br_dup))):
                    for hi, h in enumerate((hA, hB)):
                        for g2 in range(2):
                            pz = grp1()
                            for half in range(2):
                                qc = 2 * g2 + half
                                for k in range(DC):
                                    ins = nc.tensor.matmul(
                                        pz[:, 512 * half:512 * (half + 1)],
                                        wsl[:, hi, k, :],
                                        self.zln[:, k,
                                                 512 * qc:512 * (qc + 1)],
                                        start=(k == 0), stop=(k == DC - 1))
                            self.tse.inc(ins)
                            eng, E = evac_eng(2 * ti + hi + g2)
                            self.wt(eng)
                            if (2 * ti + hi + g2) % 2 == 0:
                                E.tensor_scalar(
                                    dst[:, pr, hi, 1024 * g2:1024 * (g2 + 1)],
                                    pz[:], bias[:, h:h + 1], None, ALU.add)
                            else:
                                E.activation(
                                    dst[:, pr, hi, 1024 * g2:1024 * (g2 + 1)],
                                    pz[:], AF.Identity,
                                    bias=bias[:, h:h + 1], scale=1.0)
                            gtk1[0] = self.tok()
                            yield
                for hi in range(2):
                    for ti, srcd in enumerate((qd, rd)):
                        for g2 in range(2):
                            pz = grp1()
                            for half in range(2):
                                qc = 2 * g2 + half
                                ins = nc.tensor.matmul(
                                    pz[:, 512 * half:512 * (half + 1)],
                                    wphid[:, :],
                                    srcd[0:64, pr, hi,
                                         512 * qc:512 * (qc + 1)],
                                    start=True, stop=True)
                                self.tse.inc(ins)
                            self.wt(nc.scalar)
                            self.A.activation(
                                trig[:, 1024 * g2:1024 * (g2 + 1)], pz[:],
                                AF.Identity, bias=self.bphi_v[:], scale=1.0)
                            gtk1[0] = self.tok()
                            yield
                        self.wa(nc.vector)
                        self.V.tensor_scalar(ki[:].bitcast(F32), trig[:],
                                             INV_2PI, None, ALU.mult)
                        self.V.tensor_copy(ki[:], ki[:].bitcast(F32))
                        self.V.scalar_tensor_tensor(
                            trig[:], ki[:], -TWO_PI, trig[:], ALU.mult,
                            ALU.add)
                        yield
                        self.wv(nc.gpsimd)
                        self.G.tensor_scalar(
                            trig[:], trig[:], CLAMP, -CLAMP, ALU.min, ALU.max)
                        yield
                        self.wg(nc.scalar)
                        self.A.activation(
                            cs[:], trig[:], AF.Sin, bias=0.0, scale=1.0)
                        yield
                        self.wa(nc.vector)
                        self.V.tensor_mul(srcd[:, pr, hi, :],
                                          srcd[:, pr, hi, :], cs[:])
                        yield
                for hi in range(2):
                    self.wv(nc.tensor)
                    for kq in range(2):
                        pz = grp1()
                        v = pz[:].bitcast(BF16)
                        for j in range(8):
                            kt = 8 * kq + j
                            ins = nc.tensor.transpose(
                                v[:, 1024 * kq + 128 * j:
                                  1024 * kq + 128 * (j + 1)],
                                rd[:, pr, hi, 128 * kt:128 * (kt + 1)],
                                self.identb[:])
                            self.tse.inc(ins)
                        self.wt(nc.vector)
                        self.V.tensor_copy(
                            rctm[:, 1024 * kq:1024 * (kq + 1)],
                            v[:, 1024 * kq:1024 * (kq + 1)])
                        gtk1[0] = self.tok()
                        yield
                    pzM = grp1()
                    for kt in range(16):
                        ins = nc.tensor.matmul(
                            pzM[:, 0:128], rctm[:, 128 * kt:128 * (kt + 1)],
                            rctm[:, 128 * kt:128 * (kt + 1)],
                            start=(kt == 0), stop=(kt == 15))
                    self.tse.inc(ins)
                    self.wt(nc.vector)
                    self.V.tensor_copy(msb[:], pzM[:, 0:128])
                    gtk1[0] = self.tok()
                    yield
                    for g2 in range(2):
                        pz = grp1()
                        for half in range(2):
                            qc = 2 * g2 + half
                            ins = nc.tensor.matmul(
                                pz[:, 512 * half:512 * (half + 1)], msb[:],
                                qd[:, pr, hi, 512 * qc:512 * (qc + 1)],
                                start=True, stop=True)
                            self.tse.inc(ins)
                        eng, E = evac_eng(g2)
                        self.wt(eng)
                        if g2 % 2 == 0:
                            E.tensor_copy(mqc[:, 1024 * g2:1024 * (g2 + 1)],
                                          pz[:])
                        else:
                            E.activation(mqc[:, 1024 * g2:1024 * (g2 + 1)],
                                         pz[:], AF.Identity, bias=0.0,
                                         scale=1.0)
                        gtk1[0] = self.tok()
                        yield
                    self.wa(nc.vector)
                    self.V.tensor_mul(mqc[:], mqc[:], qd[:, pr, hi, :])
                    gtk1[0] = self.tok()
                    yield
                    pzY = grp1()
                    for qt in range(16):
                        ins = nc.tensor.matmul(
                            pzY[:, qt:qt + 1],
                            mqc[:, 128 * qt:128 * (qt + 1)],
                            self.ones_cb[:], start=True, stop=True)
                        self.tse.inc(ins)
                    self.wt(nc.vector)
                    self.V.tensor_scalar(
                        ysb[:], pzY[:, 0:16], 1e-24, None, ALU.max)
                    gtk1[0] = self.tok()
                    yield
                    self.wv(nc.scalar)
                    self.A.activation(ysb[:], ysb[:], AF.Sqrt,
                                      bias=0.0, scale=1.0)
                    self.wa(nc.vector)
                    self.V.reciprocal(ysb[:], ysb[:])
                    yield
                    pzT = grp1()
                    self.wv(nc.tensor)
                    ins = nc.tensor.transpose(pzT[0:16, 0:128], ysb[:],
                                              self.identf[:])
                    self.tse.inc(ins)
                    self.wt(nc.vector)
                    self.wgld(nc.vector)
                    self.V.tensor_copy(itp[:], pzT[0:16, 0:128])
                    gtk1[0] = self.tok()
                    self.wv(nc.gpsimd)
                    self.gld.dinc(nc.gpsimd.dma_start(out=invr[:, hi, :],
                                                      in_=itp[:]))
                    yield
                self.wgld(nc.tensor)
                for g2 in range(2):
                    pz = grp1()
                    for half in range(2):
                        qc = 2 * g2 + half
                        ins = nc.tensor.matmul(
                            pz[0:64, 512 * half:512 * (half + 1)],
                            self.ones_rb[:, 0:64],
                            invr[:, 0, 512 * qc:512 * (qc + 1)],
                            start=True, stop=True)
                        self.tse.inc(ins)
                        ins = nc.tensor.matmul(
                            pz[64:128, 512 * half:512 * (half + 1)],
                            self.ones_rb[:, 0:64],
                            invr[:, 1, 512 * qc:512 * (qc + 1)],
                            start=True, stop=True, tile_position=(0, 64))
                        self.tse.inc(ins)
                    eng, E = evac_eng(g2)
                    self.wt(eng)
                    if g2 % 2 == 0:
                        E.tensor_copy(invbc[:, pr, 1024 * g2:1024 * (g2 + 1)],
                                      pz[:])
                    else:
                        E.activation(invbc[:, pr, 1024 * g2:1024 * (g2 + 1)],
                                     pz[:], AF.Identity, bias=0.0, scale=1.0)
                    gtk1[0] = self.tok()
                    yield

            def scores_gen(p, otc):
                pr = p % 2
                hA, hB = 2 * p, 2 * p + 1
                for hi, h in enumerate((hA, hB)):
                    for kt in range(16):
                        stok = [None, None]
                        for g2 in range(2):
                            wet(nc.tensor, gtk0[0])
                            for half in range(2):
                                qc = 2 * g2 + half
                                ins = nc.tensor.matmul(
                                    pG[0][:, 512 * half:512 * (half + 1)],
                                    rd[:, pr, hi, 128 * kt:128 * (kt + 1)],
                                    qd[:, pr, hi, 512 * qc:512 * (qc + 1)],
                                    start=True, stop=True)
                                self.tse.inc(ins)
                            eng, E = evac_eng(g2 + kt)
                            self.wt(eng)
                            if (g2 + kt) % 2 == 0:
                                E.tensor_copy(
                                    sct[:, 1024 * g2:1024 * (g2 + 1)],
                                    pG[0][:])
                            else:
                                E.activation(
                                    sct[:, 1024 * g2:1024 * (g2 + 1)],
                                    pG[0][:], AF.Identity, bias=0.0,
                                    scale=1.0)
                            gtk0[0] = et(E)
                            stok[g2] = gtk0[0]
                        for qc in range(4):
                            wet(nc.tensor, stok[qc // 2])
                            if kt == 0:
                                wet(nc.tensor, otc[0])
                            ins = nc.tensor.matmul(
                                pOut[64 * hi:64 * (hi + 1),
                                     512 * qc:512 * (qc + 1)],
                                self.hv[:, kt, HD * h:HD * (h + 1)],
                                sct[:, 512 * qc:512 * (qc + 1)],
                                start=(kt == 0), stop=(kt == 15),
                                tile_position=(0, 64 * hi))
                            self.tse.inc(ins)
                        yield
                self.wt(nc.vector)
                self.wa(nc.vector)
                self.V.tensor_mul(self.outfm[:, p, :], pOut[:],
                                  invbc[:, pr, :])
                otc[0] = et(self.V)

            _SENT = object()

            def advance(gen, n):
                if gen is None:
                    return False
                for _ in range(n):
                    if next(gen, _SENT) is _SENT:
                        return False
                return True

            gtk0[0] = et(self.V)
            cg = chain_gen(0)
            while advance(cg, 8):
                pass
            otc = [None]
            for p in range(NPAIR):
                sg = scores_gen(p, otc)
                cg = chain_gen(p + 1) if p + 1 < NPAIR else None
                s_alive, c_alive = True, True
                while s_alive or c_alive:
                    s_alive = advance(sg, 1)
                    c_alive = advance(cg, 2)
            if "Qc" in self.taps and b == 0:
                self.barrier()
                self._tap("Qc", qd[:, (NPAIR - 1) % 2, 0, :], [128, 2048], BF16)
                self._tap("Rc", rd[:, (NPAIR - 1) % 2, 0, :], [128, 2048], BF16)
                self._tap("invbc", invbc[:, (NPAIR - 1) % 2, :], [128, 2048], BF16)
            if "outfm" in self.taps and b == 0:
                self.barrier()
                self._tap("outfm", self.outfm[:].rearrange("p c q -> p (c q)"),
                          [128, DC * 2048], BF16)


    def wo_epilogue(self, b):
        nc = self.nc
        ctx = ExitStack()
        sb = lambda name, shape, dt=F32: ctx.enter_context(
            nc.sbuf_tensor(name, shape, dt))
        ps = lambda name, shape, dt=F32: ctx.enter_context(
            nc.psum_tensor(name, shape, dt))
        with ctx:
            wslot = sb(f"wo_wslot{b}", [128, DC, D], BF16)
            self.load_weight(self.d_Wo, wslot)
            pA = ps(f"wo_pA{b}", [128, 512])
            pB = ps(f"wo_pB{b}", [128, 512])
            pR = ps(f"wo_pR{b}", [1, 512])
            pBc = ps(f"wo_pBc{b}", [128, 1024])
            pTr = ps(f"wo_pTr{b}", [128, 512])
            x2c = sb(f"wo_x2c{b}", [128, DC, 512])
            x2b = sb(f"wo_x2b{b}", [128, DC, 512], BF16)
            x2sq = sb(f"wo_x2sq{b}", [128, DC, 512], BF16)
            xfc = sb(f"wo_xfc{b}", [128, DC, 512])
            h2c = sb(f"wo_h2c{b}", [128, DC, 512], BF16)
            d1 = sb(f"wo_d1{b}", [128, DC, 512], BF16)
            d2 = sb(f"wo_d2{b}", [128, 512])
            x2t = sb(f"wo_x2t{b}", [128, 4, 128])
            for qc in range(4):
                t0 = 512 * qc
                gt0 = b * S + t0
                self.guard_load(nc.sync)
                self.sld.dinc(nc.sync.dma_start(
                    out=xfc[:],
                    in_=self.d_x.ap()[b].rearrange("(k p) t -> p k t", p=128)
                    [:, :, t0:t0 + 512]))
                wtok2 = [self.tok(), self.tok()]
                for e in range(DC):
                    ep = e % 2
                    pz = pA if ep == 0 else pB
                    self.wtok(nc.tensor, wtok2[ep])
                    for k in range(DC):
                        ins = nc.tensor.matmul(
                            pz[:], wslot[:, k, 128 * e:128 * (e + 1)],
                            self.outfm[:, k, t0:t0 + 512],
                            start=(k == 0), stop=(k == DC - 1))
                    self.tse.inc(ins)
                    self.wt(nc.scalar)
                    self.A.activation(
                        x2c[:, e, :], pz[:], AF.Identity,
                        bias=self.gabo_t[:, e, b:b + 1],
                        scale=self.mod_t[:, e, 2, b:b + 1])
                    wtok2[ep] = self.tok()
                    self.wsld(nc.vector)
                    self.wa(nc.vector)
                    self.V.tensor_add(x2c[:, e, :], x2c[:, e, :],
                                         xfc[:, e, :])
                    self.V.tensor_copy(x2b[:, e, :],
                                                       x2c[:, e, :])
                    self.wv(nc.sync)
                    self.sld.dinc(nc.sync.dma_start(
                        out=self.d_x2f[128 * e:128 * (e + 1), gt0:gt0 + 512],
                        in_=x2c[:, e, :]))
                    self.wv(nc.tensor)
                    self.wgld(nc.tensor)
                    for j in range(4):
                        ins = nc.tensor.transpose(
                            pTr[:, 128 * j:128 * (j + 1)],
                            x2c[:, e, 128 * j:128 * (j + 1)], self.identf[:])
                    self.tse.inc(ins)
                    self.wt(nc.vector)
                    self.V.tensor_copy(
                        x2t[:].rearrange("p j f -> p (j f)"), pTr[:])
                    self.wv(nc.gpsimd)
                    self.gld.dinc(nc.gpsimd.dma_start(
                        out=self.d_x2t.ap()
                        [gt0:gt0 + 512, 128 * e:128 * (e + 1)]
                        .rearrange("(j p) f -> p j f", p=128),
                        in_=x2t[:]))
                self.wv(nc.scalar)
                self.A.activation(x2sq[:], x2b[:], AF.Square,
                                                  bias=0.0, scale=1.0)
                self._ln_stats(x2b, x2sq, pR, pBc)
                self.wt(nc.vector)
                self.wgld(nc.vector)
                self.V.tensor_sub(
                    d1[:], x2b[:],
                    pBc[:, 0:512][:, None, :].broadcast_to([128, DC, 512]))
                for k in range(DC):
                    self.wa(nc.vector)
                    self.V.scalar_tensor_tensor(
                        d2[:], d1[:, k, :], self.mod_t[:, k, 4, b:b + 1],
                        pBc[:, 512:1024], ALU.mult, ALU.mult)
                    self.wv(nc.scalar)
                    self.A.activation(
                        h2c[:, k, :], d2[:], AF.Identity,
                        bias=self.mod_t[:, k, 3, b:b + 1], scale=1.0)
                self.wa(nc.vector)
                self.V.tensor_reduce(
                    d2[:, 0:1], h2c[:].rearrange("p k t -> p (k t)"), AX.X,
                    ALU.max, apply_absolute_value=True)
                self.V.tensor_max(self.h2max[:],
                                                  self.h2max[:], d2[:, 0:1])
                self.wv(nc.gpsimd)
                self.gld.dinc(nc.gpsimd.dma_start(
                    out=self.d_h2[:, gt0:gt0 + 512]
                    .rearrange("(k p) t -> p k t", p=128),
                    in_=h2c[:]))
                self.wgld(nc.scalar)


    _gs_n = 0

    def global_scale(self, idx, locmax_col, nvals=1):
        nc = self.nc
        ctx = ExitStack()
        with ctx:
            ps = lambda name, shape, dt=F32: ctx.enter_context(
                nc.psum_tensor(name, shape, dt))
            sbl = lambda name, shape, dt=F32: ctx.enter_context(
                nc.sbuf_tensor(name, shape, dt))
            Prog._gs_n += 1
            u = Prog._gs_n
            pT = ps(f"gs_p{idx}_{u}", [128, 128])
            mrow = sbl(f"gs_m{idx}_{u}", [1, 64])
            m2 = sbl(f"gs_m2{idx}_{u}", [2, 1])
            srow = sbl(f"gs_s{idx}_{u}", [1, 16])
            self.wv(nc.tensor)
            self.wa(nc.tensor)
            ins = nc.tensor.transpose(pT[0:nvals, 0:128],
                                      locmax_col[:, 0:nvals], self.identf[:])
            self.tse.inc(ins)
            self.wt(nc.vector)
            self.V.memset(mrow[:], 0.0)
            self.V.tensor_reduce(m2[0:nvals, :], pT[0:nvals, 0:128], AX.X,
                                 ALU.max)
            self.wv(nc.gpsimd)
            self.gld.dinc(nc.gpsimd.dma_start(out=mrow[0:1, 0:nvals],
                                              in_=m2[0:nvals, :]))
            nc.gpsimd.wait_ge(self.gld.h, self.gld.v)
            self.gld.dinc(nc.gpsimd.dma_start(out=self.d_arin[idx].ap(),
                                              in_=mrow[:]))
            nc.gpsimd.wait_ge(self.gld.h, self.gld.v)
            if SIM_LOCAL:
                self.gld.dinc(nc.gpsimd.dma_start(
                    out=self.d_arout[idx].ap(), in_=self.d_arin[idx].ap()))
                nc.gpsimd.wait_ge(self.gld.h, self.gld.v)
            else:
                self.cc.inc(nc.gpsimd.collective_compute(
                    "AllReduce", ALU.max, replica_groups=[list(range(NCORES))],
                    ins=[self.d_arin[idx].ap().opt()],
                    outs=[self.d_arout[idx].ap().opt()]))
                nc.gpsimd.wait_ge(self.cc.h, self.cc.v)
            self.gld.dinc(nc.gpsimd.dma_start(out=mrow[:],
                                              in_=self.d_arout[idx].ap()))
            self.wgld(nc.vector)
            self.V.memset(srow[:], 1.0)
            self.V.tensor_scalar(srow[:, 0:nvals], mrow[:, 0:nvals],
                                    1.0 / QMAX, 1e-12, ALU.mult, ALU.max)
            self.V.reciprocal(srow[:, 8:8 + nvals],
                                              srow[:, 0:nvals])
            self.wv(nc.tensor)
            ins = nc.tensor.matmul(pT[:, 0:16], self.ones_r[:], srow[:],
                                   start=True, stop=True)
            self.tse.inc(ins)
            self.wt(nc.vector)
            for i in range(nvals):
                self.V.tensor_copy(
                    self.sc_col[:, 2 * (idx + i):2 * (idx + i) + 1],
                    pT[:, i:i + 1])
                self.V.tensor_copy(
                    self.sc_col[:, 2 * (idx + i) + 1:2 * (idx + i) + 2],
                    pT[:, 8 + i:8 + i + 1])


    def ffn(self):
        nc = self.nc
        ctx = ExitStack()
        sb = lambda name, shape, dt=F32: ctx.enter_context(
            nc.sbuf_tensor(name, shape, dt))
        ps = lambda name, shape, dt=F32: ctx.enter_context(
            nc.psum_tensor(name, shape, dt))
        mx2 = self.ctx.enter_context(nc.sbuf_tensor("ffn_mx2", [128, 2], F32))
        with ctx:
            hq = sb("ffn_hq", [128, DC, T], BF16)
            t1 = sb("ffn_t1", [128, 2048])
            ki = sb("ffn_ki", [128, 2048], I32)
            hb = sb("ffn_hb", [128, DC, 512], BF16)
            wu = [sb(f"ffn_wu{i}", [128, DC, 128], BF16) for i in range(2)]
            wg = [sb(f"ffn_wg{i}", [128, DC, 128], BF16) for i in range(2)]

            def load_slices(j, par):
                nc.sync.wait_ge(self.tse.h, self.tse.v)
                self.sld.dinc(nc.sync.dma_start(
                    out=wu[par][:],
                    in_=self.d_upT.ap().rearrange("(k p) i -> p k i", p=128)
                    [:, :, 128 * j:128 * (j + 1)]))
                self.sld.dinc(nc.sync.dma_start(
                    out=wg[par][:],
                    in_=self.d_gtT.ap().rearrange("(k p) i -> p k i", p=128)
                    [:, :, 128 * j:128 * (j + 1)]))

            for ch in range(8):
                t0 = 512 * ch
                self.guard_load(nc.gpsimd)
                self.gld.dinc(nc.gpsimd.dma_start(
                    out=hb[:], in_=self.d_h2[:, t0:t0 + 512]
                    .rearrange("(k p) t -> p k t", p=128)))
                self.wgld(nc.vector)
                for kg in range(2):
                    sl = slice(kg * 4, kg * 4 + 4)
                    hv_ = hb[:, sl, :].rearrange("p k t -> p (k t)")
                    t1v = t1[:].rearrange("p (k t) -> p k t", t=512)
                    kiv = ki[:].rearrange("p (k t) -> p k t", t=512)
                    self.V.tensor_scalar(t1[:], hv_, self.sc_col[:, 1:2],
                                            None, ALU.mult)
                    self.V.tensor_copy(ki[:], t1[:])
                    self.V.tensor_scalar(
                        hq[:, sl, t0:t0 + 512],
                        kiv, self.sc_col[:, 0:1], None, ALU.mult)
            if "hq" in self.taps:
                self.barrier()
                self._tap("hq", hq[:].rearrange("p k t -> p (k t)"),
                          [128, DC * T], BF16)
            if self.upto == "hq":
                self.barrier()
                return
            pU0 = ps("ffn_pU0", [128, 512])
            pU1 = ps("ffn_pU1", [128, 512])
            pG0 = ps("ffn_pG0", [128, 512])
            pG1 = ps("ffn_pG1", [128, 512])
            upc = sb("ffn_upc", [128, 4, 512], BF16)
            gtc = sb("ffn_gtc", [128, 4, 512], BF16)
            red = sb("ffn_red", [128, 1])
            self.V.memset(mx2[:], 0.0)
            load_slices(0, 0)
            utok = [self.tok(), self.tok()]
            gtok = [self.tok(), self.tok()]
            sttok = [self.tok()] * 4
            for ic in range(IC):
                wpar = ic % 2
                self.wsld(nc.tensor)
                if ic + 1 < IC:
                    load_slices(ic + 1, 1 - wpar)
                for tc in range(8):
                    par = tc % 2
                    pu = pU0 if par == 0 else pU1
                    pg = pG0 if par == 0 else pG1
                    self.wtok(nc.tensor, utok[par])
                    for k in range(DC):
                        ins = nc.tensor.matmul(
                            pu[:], wu[wpar][:, k, :],
                            hq[:, k, 512 * tc:512 * (tc + 1)],
                            start=(k == 0), stop=(k == DC - 1))
                    self.tse.inc(ins)
                    self.wtok(nc.tensor, gtok[par])
                    for k in range(DC):
                        ins = nc.tensor.matmul(
                            pg[:], wg[wpar][:, k, :],
                            hq[:, k, 512 * tc:512 * (tc + 1)],
                            start=(k == 0), stop=(k == DC - 1))
                    self.tse.inc(ins)
                    slot = (ic * 8 + tc) % 4
                    self.wt(nc.vector)
                    self.wtok(nc.vector, sttok[slot])
                    self.V.tensor_copy(upc[:, slot, :], pu[:])
                    utok[par] = self.tok()
                    self.wt(nc.scalar)
                    self.wtok(nc.scalar, sttok[slot])
                    self.A.activation(
                        gtc[:, slot, :], pg[:], AF.Identity, bias=0.0,
                        scale=1.0)
                    gtok[par] = self.tok()
                    self.wa(nc.vector)
                    self.V.tensor_reduce(red[:], upc[:, slot, :], AX.X,
                                            ALU.max, apply_absolute_value=True)
                    self.V.tensor_max(mx2[:, 0:1], mx2[:, 0:1], red[:])
                    self.V.tensor_reduce(red[:], gtc[:, slot, :], AX.X,
                                            ALU.max, apply_absolute_value=True)
                    self.V.tensor_max(mx2[:, 1:2],
                                                      mx2[:, 1:2], red[:])
                    self.wv(nc.sync)
                    self.wa(nc.sync)
                    self.sld.dinc(nc.sync.dma_start(
                        out=self.d_up[128 * ic:128 * (ic + 1),
                                      512 * tc:512 * (tc + 1)],
                        in_=upc[:, slot, :]))
                    self.sld.dinc(nc.sync.dma_start(
                        out=self.d_gt[128 * ic:128 * (ic + 1),
                                      512 * tc:512 * (tc + 1)],
                        in_=gtc[:, slot, :]))
                    if slot == 3:
                        self.scommit()
                        for s_ in range(4):
                            sttok[s_] = (0, 0, 0, self.scm.v)
        self.barrier()
        if self.upto == "s1":
            if "up" in self.taps:
                self._tap("up", self.d_up.ap(), [I3, T], BF16)
            if "gt" in self.taps:
                self._tap("gt", self.d_gt.ap(), [I3, T], BF16)
            return
        self.global_scale(1, mx2[:], nvals=2)
        self.barrier()
        ctx = ExitStack()
        sb = lambda name, shape, dt=F32: ctx.enter_context(
            nc.sbuf_tensor(name, shape, dt))
        with ctx:
            bufA = sb("ffn_bA", [128, 3, T], BF16)
            bufB = sb("ffn_bB", [128, 3, T], BF16)
            kk = sb("ffn_kk", [128, 2, T], I32)
            kb = sb("ffn_kb", [128, 2, T], I32)
            mxc = sb("ffn_mxc", [128, IC])
            ltok = [None, None, None]
            kfree = [None, None]
            tA1 = [None] * IC
            tV2 = [None] * IC
            tA3 = [None] * IC

            def s2_front(j):
                p3, p2 = j % 3, j % 2
                if ltok[p3] is not None:
                    nc.sync.wait_ge(self.gcm.h, ltok[p3])
                self.sld.dinc(nc.sync.dma_start(
                    out=bufA[:, p3, :],
                    in_=self.d_gt[128 * j:128 * (j + 1), :]))
                self.wsld(nc.scalar)
                if kfree[p2] is not None:
                    nc.scalar.wait_ge(self.vse.h, kfree[p2])
                self.A.activation(kk[:, p2, :].bitcast(F32), bufA[:, p3, :],
                                  AF.Identity, bias=0.0,
                                  scale=self.sc_col[:, 5:6])
                tA1[j] = self.ase.v
                nc.vector.wait_ge(self.ase.h, tA1[j])
                self.V.tensor_copy(kk[:, p2, :], kk[:, p2, :].bitcast(F32))
                self.V.tensor_scalar(bufB[:, p3, :], kk[:, p2, :],
                                     self.sc_col[:, 4:5], None, ALU.mult)
                kfree[p2] = self.vse.v
                tV2[j] = self.vse.v

            def s2_back(j):
                p3, p2 = j % 3, j % 2
                nc.scalar.wait_ge(self.vse.h, tV2[j])
                self.A.activation(kb[:, p2, :].bitcast(F32), bufB[:, p3, :],
                                  AF.Sigmoid, bias=0.0, scale=1.0)
                self.A.activation(kb[:, p2, :].bitcast(F32),
                                  kb[:, p2, :].bitcast(F32),
                                  AF.Identity, bias=0.0, scale=65536.0)
                tA3[j] = self.ase.v
                nc.vector.wait_ge(self.ase.h, tA3[j])
                self.V.tensor_copy(kb[:, p2, :], kb[:, p2, :].bitcast(F32))
                self.V.scalar_tensor_tensor(
                    bufA[:, p3, :], kb[:, p2, :], 65535.0, bufB[:, p3, :],
                    ALU.min, ALU.mult)
                self.V.tensor_reduce(mxc[:, j:j + 1], bufA[:, p3, :], AX.X,
                                     ALU.max, apply_absolute_value=True)
                nc.gpsimd.wait_ge(self.vse.h, self.vse.v)
                self.gld.dinc(nc.gpsimd.dma_start(
                    out=self.d_g2[128 * j:128 * (j + 1), :],
                    in_=bufA[:, p3, :]))
                self.gcommit()
                ltok[p3] = self.gcm.v

            s2_front(0)
            for ic in range(IC):
                if ic + 1 < IC:
                    s2_front(ic + 1)
                s2_back(ic)
            self.V.tensor_reduce(mx2[:, 0:1], mxc[:], AX.X, ALU.max)
            self.barrier()
            if self.upto == "s2":
                if "g2" in self.taps:
                    self._tap("g2", self.d_g2.ap(), [I3, T], BF16)
                return
            self.global_scale(3, mx2[:, 0:1], nvals=1)
            self.barrier()
            ltok = [None, None, None]
            kfree = [None, None]
            tA3 = [None] * IC

            def s3_front(j):
                p3, p2 = j % 3, j % 2
                if ltok[p3] is not None:
                    nc.sync.wait_ge(self.gcm.h, ltok[p3])
                self.sld.dinc(nc.sync.dma_start(
                    out=bufA[:, p3, :],
                    in_=self.d_g2[128 * j:128 * (j + 1), :]))
                self.sld.dinc(nc.sync.dma_start(
                    out=bufB[:, p3, :],
                    in_=self.d_up[128 * j:128 * (j + 1), :]))
                self.wsld(nc.scalar)
                if kfree[p2] is not None:
                    nc.scalar.wait_ge(self.vse.h, kfree[p2])
                self.A.activation(kk[:, p2, :].bitcast(F32), bufA[:, p3, :],
                                  AF.Identity, bias=0.0,
                                  scale=self.sc_col[:, 7:8])
                self.A.activation(kb[:, p2, :].bitcast(F32), bufB[:, p3, :],
                                  AF.Identity, bias=0.0,
                                  scale=self.sc_col[:, 3:4])
                tA3[j] = self.ase.v

            def s3_back(j):
                p3, p2 = j % 3, j % 2
                nc.vector.wait_ge(self.ase.h, tA3[j])
                self.V.tensor_copy(kk[:, p2, :], kk[:, p2, :].bitcast(F32))
                self.V.tensor_copy(kb[:, p2, :], kb[:, p2, :].bitcast(F32))
                self.V.tensor_mul(bufA[:, p3, :], kk[:, p2, :], kb[:, p2, :])
                self.V.tensor_reduce(mxc[:, j:j + 1], bufA[:, p3, :], AX.X,
                                     ALU.max, apply_absolute_value=True)
                kfree[p2] = self.vse.v
                nc.gpsimd.wait_ge(self.vse.h, self.vse.v)
                self.gld.dinc(nc.gpsimd.dma_start(
                    out=self.d_o[128 * j:128 * (j + 1), :],
                    in_=bufA[:, p3, :]))
                self.gcommit()
                ltok[p3] = self.gcm.v

            s3_front(0)
            for ic in range(IC):
                if ic + 1 < IC:
                    s3_front(ic + 1)
                s3_back(ic)
            self.V.tensor_reduce(mx2[:, 0:1], mxc[:], AX.X, ALU.max)
            self.V.tensor_scalar(mx2[:, 0:1], mx2[:, 0:1],
                                 self.sc_col[:, 6:7], None, ALU.mult)
            self.V.tensor_scalar(mx2[:, 0:1], mx2[:, 0:1],
                                 self.sc_col[:, 2:3], None, ALU.mult)
            self.V.tensor_scalar(mx2[:, 0:1], mx2[:, 0:1], 1.0 / 65536.0,
                                 None, ALU.mult)
        self.barrier()
        if self.upto == "s3":
            if "o" in self.taps:
                self._tap("o", self.d_o.ap(), [I3, T], BF16)
            return
        self.global_scale(0, mx2[:, 0:1], nvals=1)
        self.V.tensor_mul(self.sc_col[:, 9:10], self.sc_col[:, 6:7],
                          self.sc_col[:, 2:3])
        self.V.tensor_scalar(self.sc_col[:, 9:10], self.sc_col[:, 9:10],
                             self.sc_col[:, 1:2], None, ALU.mult)
        self.V.tensor_scalar(self.sc_col[:, 9:10], self.sc_col[:, 9:10],
                             1.0 / 65536.0, None, ALU.mult)
        self.barrier()
        ctx = ExitStack()
        sb = lambda name, shape, dt=F32: ctx.enter_context(
            nc.sbuf_tensor(name, shape, dt))
        ps = lambda name, shape, dt=F32: ctx.enter_context(
            nc.psum_tensor(name, shape, dt))
        with ctx:
            wdnT = sb("ffn_wdnT", [128, IC, D], BF16)
            ot = sb("ffn_ot", [128, 2, IC, 128], BF16)
            otf = sb("ffn_otf", [128, 2, IC * 128])
            oki = sb("ffn_oki", [128, 2, IC * 128], I32)
            x2tc = sb("ffn_x2tc", [128, 2, 2, 512])
            osb = sb("ffn_osb", [128, 2, 512])
            pD0 = ps("ffn_pD0", [128, 512])
            pD1 = ps("ffn_pD1", [128, 512])
            for c0 in range(0, IC, 8):
                self.sld.dinc(nc.sync.dma_start(
                    out=wdnT[:, c0:c0 + 8, :],
                    in_=self.d_dnT.ap().rearrange("(c p) d -> p c d", p=128)
                    [:, c0:c0 + 8, :]))
            self.wsld(nc.tensor)
            dtok = [self.tok(), self.tok()]
            ostok = [self.tok(), self.tok()]
            sotok = [self.tok(), self.tok()]
            qtoks = [None] * 33

            def s4_load(tt):
                par = tt % 2
                t0 = 128 * tt
                self.wtok(nc.gpsimd, dtok[par])
                nc.gpsimd.wait_ge(self.gld.h, self.gld.v)
                for c0 in range(0, IC, 8):
                    self.gld.dinc(nc.gpsimd.dma_start(
                        out=ot[:, par, c0:c0 + 8, :], in_=self.d_o.ap()
                        .rearrange("(c p) t -> p c t", p=128)
                        [:, c0:c0 + 8, t0:t0 + 128]))
                self.wtok(nc.sync, dtok[par])
                self.sld.dinc(nc.sync.dma_start(
                    out=x2tc[:, par, :, :],
                    in_=self.d_x2t.ap()[t0:t0 + 128, :]
                    .rearrange("p (a f) -> p a f", f=512)))

            def s4_qdq(tt):
                par = tt % 2
                self.wgld(nc.vector)
                self.wsld(nc.vector)
                otv = ot[:, par, :, :].rearrange("p c t -> p (c t)")
                self.V.tensor_scalar(otf[:, par, :], otv,
                                     self.sc_col[:, 9:10], None, ALU.mult)
                self.V.tensor_copy(oki[:, par, :], otf[:, par, :])
                self.V.tensor_scalar(
                    otv, oki[:, par, :], self.sc_col[:, 0:1], None, ALU.mult)
                qtoks[tt] = self.tok()

            s4_load(0)
            s4_qdq(0)
            s4_load(1)
            for tt in range(32):
                b = tt // 16
                t0 = 128 * tt
                par = tt % 2
                if tt + 1 < 32:
                    s4_qdq(tt + 1)
                for a in range(2):
                    pz = pD0 if a == 0 else pD1
                    self.wtok(nc.tensor, qtoks[tt])
                    self.wtok(nc.tensor, ostok[a])
                    for c in range(IC):
                        ins = nc.tensor.matmul(
                            pz[:], ot[:, par, c, :],
                            wdnT[:, c, 512 * a:512 * (a + 1)],
                            start=(c == 0), stop=(c == IC - 1))
                    self.tse.inc(ins)
                    self.wt(nc.vector)
                    self.wtok(nc.vector, sotok[a])
                    self.V.tensor_mul(
                        osb[:, a, :], pz[:],
                        self.gbbc[:, b, 512 * a:512 * (a + 1)])
                    self.V.tensor_add(
                        osb[:, a, :], osb[:, a, :], x2tc[:, par, a, :])
                    ostok[a] = self.tok()
                dtok[par] = (self.vse.v, 0, 0, 0, self.tse.v)
                if tt + 2 < 32:
                    s4_load(tt + 2)
                self.wv(nc.sync)
                self.sld.dinc(nc.sync.dma_start(
                    out=self.d_out.ap()[b, (t0 - b * S):(t0 - b * S) + 128, :],
                    in_=osb[:].rearrange("p a f -> p (a f)")))
                self.scommit()
                sotok[0] = (0, 0, 0, self.scm.v)
                sotok[1] = (0, 0, 0, self.scm.v)

    def build(self):
        nc = self.nc
        self.setup()
        if self.upto == "setup":
            self._finish()
            return nc
        actx = ExitStack()
        sb = lambda name, shape, dt=F32: actx.enter_context(
            nc.sbuf_tensor(name, shape, dt))
        with actx:
            self.zln = sb("zln", [128, DC, S], BF16)
            self.hv = sb("hv", [128, 16, D], BF16)
            self.outfm = sb("outfm", [128, DC, S], BF16)
            for b in range(BL):
                self.a1(b)
                self.barrier()
                if self.upto == "a1" and b == 0:
                    if "zln" in self.taps:
                        self._tap("zln",
                                  self.zln[:].rearrange("p k t -> p (k t)"),
                                  [128, DC * S], BF16)
                    self._finish()
                    return nc
                self.a2(b)
                self.barrier()
                if self.upto == "a2" and b == 0:
                    self._finish()
                    return nc
                self.wo_epilogue(b)
                self.barrier()
        if self.upto == "attn":
            if "x2f" in self.taps:
                self._tap("x2f", self.d_x2f.ap(), [D, T], F32)
            if "x2t" in self.taps:
                self._tap("x2t", self.d_x2t.ap(), [T, D], F32)
            if "h2" in self.taps:
                self._tap("h2", self.d_h2.ap(), [D, T], BF16)
            self._finish()
            return nc
        self.global_scale(0, self.h2max[:], nvals=1)
        self.barrier()
        self.ffn()
        self.barrier()
        self._finish()
        return nc

    def _finish(self):
        nc = self.nc
        self.gcommit()
        self.scommit()
        nc.sync.wait_ge(self.sld.h, self.sld.v)
        nc.gpsimd.wait_ge(self.gld.h, self.gld.v)
        nc.gpsimd.wait_ge(self.cc.h, self.cc.v)


def build_program(upto="all", taps=()):
    p = Prog(upto=upto, taps=taps)
    nc = p.build()
    return nc, p


def make_feeds(inputs):
    x = np.asarray(inputs["L0_attn_query"])
    c = np.asarray(inputs["L0_c"])
    base = {
        "wmod": np.ascontiguousarray(inputs["L0_wmod"]),
        "Wz": np.ascontiguousarray(inputs["L0_attn_Wz"]),
        "Wq": np.ascontiguousarray(inputs["L0_attn_Wq"]),
        "Wr": np.ascontiguousarray(inputs["L0_attn_Wr"]),
        "Wh": np.ascontiguousarray(inputs["L0_attn_Wh"]),
        "Wo": np.ascontiguousarray(inputs["L0_attn_Wo"]),
        "Wphi": np.ascontiguousarray(inputs["L0_attn_Wphi"]),
        "bphi": np.ascontiguousarray(
            np.asarray(inputs["L0_attn_bphi"]).reshape(1, HD)),
        "vecs": np.ascontiguousarray(np.stack([
            np.asarray(inputs["L0_attn_bz"]),
            np.asarray(inputs["L0_attn_ln_g"]),
            np.asarray(inputs["L0_attn_ln_b"]),
            np.asarray(inputs["L0_attn_bo"]),
            np.asarray(inputs["L0_attn_bq"]),
            np.asarray(inputs["L0_attn_br"]),
            np.asarray(inputs["L0_attn_bh"]),
        ], axis=0)),
    }
    wup = np.asarray(inputs["L0_ffn_w_up"])
    wgt = np.asarray(inputs["L0_ffn_w_gate"])
    wdn = np.asarray(inputs["L0_ffn_w_down"])
    feeds = []
    for core in range(NCORES):
        f = dict(base)
        xb = x[2 * core:2 * core + 2]
        f["x_fm"] = np.ascontiguousarray(xb.transpose(0, 2, 1))
        f["c2"] = np.ascontiguousarray(c[2 * core:2 * core + 2])
        f["wup_sl"] = np.ascontiguousarray(wup[:, 128 * core:128 * (core + 1)])
        f["wgt_sl"] = np.ascontiguousarray(wgt[:, 128 * core:128 * (core + 1)])
        f["wdn_sl"] = np.ascontiguousarray(wdn[:, 384 * core:384 * (core + 1)])
        feeds.append(f)
    return feeds


def kernel(**inputs):
    feeds = make_feeds(inputs)
    nc, _ = build_program()
    res = run_bass_kernel_spmd(nc, feeds, core_ids=list(range(NCORES)))
    out = np.concatenate([np.asarray(r["out"]) for r in res.results], axis=0)
    return out.astype(np.float32, copy=False)


# version 11
# speedup vs baseline: 1.1442x; 1.0011x over previous
from contextlib import ExitStack

import numpy as np
import concourse.bass as bass
import concourse.mybir as mybir
from concourse.bass_utils import run_bass_kernel_spmd

F32 = mybir.dt.float32
BF16 = mybir.dt.bfloat16
I32 = mybir.dt.int32
AF = mybir.ActivationFunctionType
ALU = mybir.AluOpType
AX = mybir.AxisListType

NCORES = 8
BL = 2
S = 2048
T = BL * S
D = 1024
DC = D // 128
H = 16
HD = 64
NPAIR = H // 2
I3 = 3072
IC = I3 // 128
QMAX = 32767.0
PI = float(np.pi)
TWO_PI = 2.0 * PI
INV_2PI = 1.0 / TWO_PI
CLAMP = 3.1415925
LN_EPS = 1e-5
SIM_LOCAL = False
SETUP_UPTO = "all"
DEQ_UPTO = 9


class Sem:

    def __init__(self, nc, ctx, name):
        self.h = ctx.enter_context(nc.semaphore(name))
        self.v = 0

    def inc(self, instr, n=1):
        instr.then_inc(self.h, n)
        self.v += n
        return instr

    def dinc(self, instr):
        return self.inc(instr, 16)


class ChainEng:

    _PASS = {"wait_ge", "wait_op", "sem_inc", "dma_start", "engine_nop", "nop"}

    def __init__(self, eng, sem):
        self._eng = eng
        self._sem = sem

    def __getattr__(self, name):
        f = getattr(self._eng, name)
        if name in self._PASS or name.startswith("_"):
            return f

        def g(*a, **k):
            self._eng.wait_ge(self._sem.h, self._sem.v)
            ins = f(*a, **k)
            self._sem.inc(ins)
            return ins
        return g


class Prog:
    def __init__(self, upto="all", taps=()):
        self.upto = upto
        self.taps = set(taps)
        nc = bass.Bass("TRN2", target_bir_lowering=False)
        self.nc = nc
        self.ctx = ExitStack()
        self.tap_handles = {}
        self._declare_io()
        self._alloc_persistent()


    def _declare_io(self):
        nc = self.nc
        dr = lambda name, shape, dt=F32, kind="ExternalInput": nc.dram_tensor(
            name, shape, dt, kind=kind)
        self.d_x = dr("x_fm", [BL, D, S])
        self.d_c2 = dr("c2", [BL, D])
        self.d_wmod = dr("wmod", [D, 6 * D])
        self.d_Wz = dr("Wz", [D, D])
        self.d_Wq = dr("Wq", [D, D])
        self.d_Wr = dr("Wr", [D, D])
        self.d_Wh = dr("Wh", [D, D])
        self.d_Wo = dr("Wo", [D, D])
        self.d_Wphi = dr("Wphi", [HD, HD])
        self.d_vec = dr("vecs", [7, D])
        self.d_bphi = dr("bphi", [1, HD])
        self.d_wup = dr("wup_sl", [I3, 128])
        self.d_wgt = dr("wgt_sl", [I3, 128])
        self.d_wdn = dr("wdn_sl", [D, 384])
        self.d_out = dr("out", [BL, S, D], kind="ExternalOutput")
        di = lambda name, shape, dt: nc.dram_tensor(name, shape, dt, kind="Internal")
        self.d_x2f = di("x2f_hbm", [D, T], F32)
        self.d_x2t = di("x2t_hbm", [T, D], F32)
        self.d_h2 = di("h2_hbm", [D, T], BF16)
        self.d_up = di("up_hbm", [I3, T], BF16)
        self.d_gt = di("gt_hbm", [I3, T], BF16)
        self.d_g2 = di("g2_hbm", [I3, T], BF16)
        self.d_o = di("o_hbm", [I3, T], BF16)
        self.d_upT_my = di("upT_my", [128, I3], BF16)
        self.d_gtT_my = di("gtT_my", [128, I3], BF16)
        self.d_dnT_my = di("dnT_my", [384, D], BF16)
        self.d_upT = di("upT_all", [D, I3], BF16)
        self.d_gtT = di("gtT_all", [D, I3], BF16)
        self.d_dnT = di("dnT_all", [I3, D], BF16)
        self.d_arin = [di(f"arin{i}", [1, 64], F32) for i in range(4)]
        self.d_arout = [di(f"arout{i}", [1, 64], F32) for i in range(4)]
        self.t_identf = nc.inline_tensor(np.eye(128, dtype=np.float32), "identf")
        import ml_dtypes
        self.t_identb = nc.inline_tensor(
            np.eye(128, dtype=ml_dtypes.bfloat16), "identb")

    def _alloc_persistent(self):
        nc, ctx = self.nc, self.ctx
        sb = lambda name, shape, dt=F32: ctx.enter_context(
            nc.sbuf_tensor(name, shape, dt))
        self.identf = sb("identf_sb", [128, 128])
        self.identb = sb("identb_sb", [128, 128], BF16)
        self.ones_r = sb("ones_r", [1, 128])
        self.ones_rb = sb("ones_rb", [1, 128], BF16)
        self.cst_b = sb("cst_b", [128, 1], BF16)
        self.ones_cb = sb("ones_cb", [128, 1], BF16)
        self.bz_t = sb("bz_t", [128, DC])
        self.g_t = sb("g_t", [128, DC])
        self.lb_t = sb("lb_t", [128, DC])
        self.bo_t = sb("bo_t", [128, DC])
        self.bq_dup = sb("bq_dup", [128, H])
        self.br_dup = sb("br_dup", [128, H])
        self.bh_row = sb("bh_row", [1, D])
        self.bphi_v = sb("bphi_v", [128, 1])
        self.mod_t = sb("mod_t", [128, DC, 6, BL])
        self.gabo_t = sb("gabo_t", [128, DC, BL])
        self.gbbc = sb("gbbc", [128, BL, D], BF16)
        self.mu_r = sb("mu_r", [1, 512])
        self.ms_r = sb("ms_r", [1, 512])
        self.a_r = sb("a_r", [1, 512])
        self.sc_col = sb("sc_col", [128, 16])
        self.h2max = sb("h2max", [128, 1])
        self.sld = Sem(nc, ctx, "sld")
        self.gld = Sem(nc, ctx, "gld")
        self.cc = Sem(nc, ctx, "cc")
        self.tse = Sem(nc, ctx, "tse")
        self.vse = Sem(nc, ctx, "vse")
        self.ase = Sem(nc, ctx, "ase")
        self.gse = Sem(nc, ctx, "gse")
        self.sse = Sem(nc, ctx, "sse")
        self.V = None
        self.A = None
        self.gcm = Sem(nc, ctx, "gcm")
        self.scm = Sem(nc, ctx, "scm")
        self._gcommitted = 0
        self._scommitted = 0
        self.V = ChainEng(nc.vector, self.vse)
        self.A = ChainEng(nc.scalar, self.ase)
        self.G = ChainEng(nc.gpsimd, self.gse)


    def barrier(self):
        nc = self.nc
        self.gcommit()
        self.scommit()
        nc.sync.wait_ge(self.sld.h, self.sld.v)
        nc.gpsimd.wait_ge(self.gld.h, self.gld.v)
        nc.gpsimd.wait_ge(self.cc.h, self.cc.v)
        nc.sync.wait_ge(self.sse.h, self.sse.v)
        nc.sync.sem_inc(self.sse.h, 1); self.sse.v += 1
        nc.tensor.wait_ge(self.tse.h, self.tse.v)
        nc.tensor.sem_inc(self.tse.h, 1); self.tse.v += 1
        nc.vector.wait_ge(self.vse.h, self.vse.v)
        nc.vector.sem_inc(self.vse.h, 1); self.vse.v += 1
        nc.scalar.wait_ge(self.ase.h, self.ase.v)
        nc.scalar.sem_inc(self.ase.h, 1); self.ase.v += 1
        nc.gpsimd.wait_ge(self.gse.h, self.gse.v)
        nc.gpsimd.sem_inc(self.gse.h, 1); self.gse.v += 1
        for eng in (nc.sync, nc.tensor, nc.vector, nc.scalar, nc.gpsimd):
            for s in (self.sse, self.tse, self.vse, self.ase, self.gse):
                eng.wait_ge(s.h, s.v)

    def gcommit(self):
        if self.gld.v > self._gcommitted:
            nc = self.nc
            nc.gpsimd.wait_ge(self.gld.h, self.gld.v)
            nc.gpsimd.wait_ge(self.gcm.h, self.gcm.v)
            nc.gpsimd.sem_inc(self.gcm.h, 1)
            self.gcm.v += 1
            self._gcommitted = self.gld.v

    def scommit(self):
        if self.sld.v > self._scommitted:
            nc = self.nc
            nc.sync.wait_ge(self.sld.h, self.sld.v)
            nc.sync.wait_ge(self.scm.h, self.scm.v)
            nc.sync.sem_inc(self.scm.h, 1)
            self.scm.v += 1
            self._scommitted = self.sld.v

    def tok(self):
        return (self.vse.v, self.ase.v, self.gse.v)

    def wtok(self, eng, t):
        if t[0]:
            eng.wait_ge(self.vse.h, t[0])
        if t[1]:
            eng.wait_ge(self.ase.h, t[1])
        if t[2]:
            eng.wait_ge(self.gse.h, t[2])
        if len(t) > 3 and t[3]:
            eng.wait_ge(self.scm.h, t[3])
        if len(t) > 4 and t[4]:
            eng.wait_ge(self.tse.h, t[4])

    def wv(self, eng):
        eng.wait_ge(self.vse.h, self.vse.v)

    def wt(self, eng):
        eng.wait_ge(self.tse.h, self.tse.v)

    def wa(self, eng):
        eng.wait_ge(self.ase.h, self.ase.v)

    def wsld(self, eng):
        self.scommit()
        eng.wait_ge(self.scm.h, self.scm.v)

    def wgld(self, eng):
        self.gcommit()
        eng.wait_ge(self.gcm.h, self.gcm.v)

    def wg(self, eng):
        eng.wait_ge(self.gse.h, self.gse.v)

    def guard_load(self, eng):
        self.wv(eng)
        self.wa(eng)
        self.wt(eng)
        if eng is self.nc.gpsimd:
            eng.wait_ge(self.gld.h, self.gld.v)
        elif eng is self.nc.sync:
            eng.wait_ge(self.sld.h, self.sld.v)

    def load_weight(self, dram, wslot):
        nc = self.nc
        self.guard_load(nc.gpsimd)
        self.gld.dinc(nc.gpsimd.dma_start(
            out=wslot[:],
            in_=dram.ap().rearrange("(k p) j -> p k j", p=128)))
        self.wgld(nc.tensor)


    def _tap(self, name, ap, shape, dt):
        nc = self.nc
        t = nc.dram_tensor(f"tap_{name}", shape, dt, kind="ExternalOutput")
        self.tap_handles[name] = t
        self.sld.dinc(nc.sync.dma_start(out=t.ap(), in_=ap))
        nc.sync.wait_ge(self.sld.h, self.sld.v)


    def setup(self):
        nc = self.nc
        ctx = ExitStack()
        sb = lambda name, shape, dt=F32: ctx.enter_context(
            nc.sbuf_tensor(name, shape, dt))
        ps = lambda name, shape, dt=F32: ctx.enter_context(
            nc.psum_tensor(name, shape, dt))
        with ctx:
            row7 = [sb(f"row7_{i}", [1, D]) for i in range(7)]
            bphi_row = sb("bphi_row", [1, HD])
            c_sb = sb("c_sb", [BL, D])
            c_si = sb("c_si", [BL, D])
            sc_fm = sb("sc_fm", [128, DC, BL])
            mod_rows = sb("mod_rows", [BL, 6 * D])
            wmod_c = sb("wmod_c", [128, DC, 512])
            scr = sb("setup_scr", [128, DC, BL])
            tmp = sb("setup_tmp", [128, DC, BL])
            gbrow = [sb(f"gbrow{b}", [1, D]) for b in range(BL)]
            pv = ps("pv", [128, 16])
            pm = ps("pm", [BL, 512])
            pgb = ps("pgb", [128, 512])
            self.sld.dinc(nc.sync.dma_start(out=self.identf[:],
                                            in_=self.t_identf.ap()))
            self.sld.dinc(nc.sync.dma_start(out=self.identb[:],
                                            in_=self.t_identb.ap()))
            for i in range(7):
                self.sld.dinc(nc.sync.dma_start(out=row7[i][:],
                                                in_=self.d_vec.ap()[i:i + 1, :]))
            self.sld.dinc(nc.sync.dma_start(out=c_sb[:], in_=self.d_c2.ap()))
            self.sld.dinc(nc.sync.dma_start(out=bphi_row[:], in_=self.d_bphi.ap()))
            self.wsld(nc.vector)
            self.V.memset(self.ones_r[:], 1.0)
            self.V.memset(self.ones_rb[:], 1.0)
            self.V.memset(self.cst_b[:], 1.0 / 1024.0)
            self.V.memset(self.ones_cb[:], 1.0)
            self.V.memset(self.h2max[:], 0.0)
            self.wsld(nc.tensor)
            for vi, dst in ((0, self.bz_t), (1, self.g_t), (2, self.lb_t),
                            (3, self.bo_t)):
                self.wv(nc.tensor)
                for k in range(DC):
                    ins = nc.tensor.transpose(
                        pv[:, k:k + 1], row7[vi][0:1, 128 * k:128 * (k + 1)],
                        self.identf[0:1, 0:1])
                self.tse.inc(ins)
                self.wt(nc.vector)
                self.V.tensor_copy(dst[:], pv[:, 0:DC])
            for vi, dst in ((4, self.bq_dup), (5, self.br_dup)):
                self.wv(nc.tensor)
                for h in range(H):
                    ins = nc.tensor.transpose(
                        pv[0:64, h:h + 1], row7[vi][0:1, 64 * h:64 * (h + 1)],
                        self.identf[0:1, 0:1])
                self.tse.inc(ins)
                self.wt(nc.vector)
                self.V.tensor_copy(dst[0:64, :], pv[0:64, 0:H])
                self.wv(nc.gpsimd)
                self.gld.dinc(nc.gpsimd.dma_start(out=dst[64:128, :],
                                                  in_=dst[0:64, :]))
            self.V.tensor_copy(self.bh_row[:], row7[6][:])
            self.wv(nc.tensor)
            ins = nc.tensor.transpose(pv[0:64, 0:1], bphi_row[0:1, :],
                                      self.identf[0:1, 0:1])
            self.tse.inc(ins)
            self.wt(nc.vector)
            self.V.tensor_copy(self.bphi_v[0:64, 0:1],
                                               pv[0:64, 0:1])
            self.wv(nc.gpsimd)
            self.gld.dinc(nc.gpsimd.dma_start(out=self.bphi_v[64:128, 0:1],
                                              in_=self.bphi_v[0:64, 0:1]))
            self.wgld(nc.vector)
            self.V.tensor_scalar(
                self.bphi_v[0:64, 0:1], self.bphi_v[0:64, 0:1], PI / 2.0,
                None, ALU.add)
            if SETUP_UPTO == "vec":
                self.barrier()
                return
            self.wsld(nc.scalar)
            self.A.activation(c_si[:], c_sb[:], AF.Silu,
                                              bias=0.0, scale=1.0)
            self.wa(nc.tensor)
            self.wv(nc.tensor)
            for k in range(DC):
                ins = nc.tensor.transpose(
                    pv[:, 2 * k:2 * k + BL], c_si[0:BL, 128 * k:128 * (k + 1)],
                    self.identf[0:BL, 0:BL])
            self.tse.inc(ins)
            self.wt(nc.vector)
            self.V.tensor_copy(
                sc_fm[:].rearrange("p k b -> p (k b)"), pv[:, 0:2 * DC])
            for j in range(12):
                self.guard_load(nc.gpsimd)
                self.gld.dinc(nc.gpsimd.dma_start(
                    out=wmod_c[:],
                    in_=self.d_wmod.ap().rearrange("(k p) j -> p k j", p=128)
                    [:, :, 512 * j:512 * (j + 1)]))
                self.wgld(nc.tensor)
                self.wv(nc.tensor)
                for k in range(DC):
                    ins = nc.tensor.matmul(pm[:], sc_fm[:, k, :], wmod_c[:, k, :],
                                           start=(k == 0), stop=(k == DC - 1))
                self.tse.inc(ins)
                self.wt(nc.vector)
                self.V.tensor_copy(
                    mod_rows[:, 512 * j:512 * (j + 1)], pm[:])
            for param in range(6):
                for k in range(DC):
                    off = param * D + 128 * k
                    self.wv(nc.tensor)
                    ins = nc.tensor.transpose(
                        pv[:, 0:BL], mod_rows[0:BL, off:off + 128],
                        self.identf[0:BL, 0:BL])
                    self.tse.inc(ins)
                    self.wt(nc.vector)
                    self.V.tensor_copy(
                        self.mod_t[:, k, param, :], pv[:, 0:BL])
            for param in (1, 4):
                self.V.tensor_scalar(
                    self.mod_t[:, :, param, :], self.mod_t[:, :, param, :],
                    1.0, None, ALU.add)
            for param in (2, 5):
                g = self.mod_t[:, :, param, :]
                self.wv(nc.scalar)
                self.A.activation(scr[:], g, AF.Sign, bias=0.0, scale=1.0)
                self.A.activation(tmp[:], g, AF.Abs,
                                                  bias=0.0, scale=1.0)
                self.wa(nc.vector)
                self.V.tensor_scalar(tmp[:], tmp[:], 0.25, None, ALU.max)
                self.V.tensor_mul(g, tmp[:], scr[:])
            self.V.tensor_mul(
                self.gabo_t[:], self.mod_t[:, :, 2, :],
                self.bo_t[:, :, None].broadcast_to([128, DC, BL]))
            for k in range(DC):
                for b in range(BL):
                    self.wv(nc.tensor)
                    ins = nc.tensor.transpose(pm[0:1, 0:128],
                                              self.mod_t[:, k, 5, b:b + 1],
                                              self.identf[:])
                    self.tse.inc(ins)
                    self.wt(nc.vector)
                    self.V.tensor_copy(
                        gbrow[b][0:1, 128 * k:128 * (k + 1)],
                        pm[0:1, 0:128])
            for b in range(BL):
                for jc in range(2):
                    self.wv(nc.tensor)
                    ins = nc.tensor.matmul(
                        pgb[:], self.ones_r[:],
                        gbrow[b][0:1, 512 * jc:512 * (jc + 1)],
                        start=True, stop=True)
                    self.tse.inc(ins)
                    self.wt(nc.vector)
                    self.V.tensor_copy(
                        self.gbbc[:, b, 512 * jc:512 * (jc + 1)], pgb[:])
            if SETUP_UPTO != "mod":
                self._dequant_ffn_weights(ctx)
        self.barrier()
        if "mod" in self.taps:
            self._tap("mod", self.mod_t[:].rearrange("p k q b -> p (k q b)"),
                      [128, DC * 6 * BL], F32)
        if "gabo" in self.taps:
            self._tap("gabo", self.gabo_t[:].rearrange("p k b -> p (k b)"),
                      [128, DC * BL], F32)

    def _dequant_ffn_weights(self, ctx):
        nc = self.nc
        ps = lambda name, shape, dt=F32: ctx.enter_context(
            nc.psum_tensor(name, shape, dt))
        ptr = ps("ptr_dq", [128, 512], BF16)

        def dequant(dsrc, nblk, nchunk, name):
            s2 = ExitStack()
            cols = nblk * 32
            w = s2.enter_context(nc.sbuf_tensor(f"wdq_{name}",
                                                [128, nchunk, cols], F32))
            m = s2.enter_context(nc.sbuf_tensor(f"wdm_{name}",
                                                [128, nchunk * nblk], F32))
            si = s2.enter_context(nc.sbuf_tensor(f"wds_{name}",
                                                 [128, nchunk * nblk], F32))
            ki = s2.enter_context(nc.sbuf_tensor(f"wdk_{name}",
                                                 [128, nchunk, cols], I32))
            kf = s2.enter_context(nc.sbuf_tensor(f"wdf_{name}",
                                                 [128, nchunk, cols], F32))
            wb = s2.enter_context(nc.sbuf_tensor(f"wdb_{name}",
                                                 [128, nchunk, cols], BF16))
            nch = w.shape[1]
            third = (nch + 2) // 3
            for c0 in range(0, nch, third):
                c1 = min(c0 + third, nch)
                self.gld.dinc(nc.gpsimd.dma_start(
                    out=w[:, c0:c1, :],
                    in_=dsrc.ap().rearrange("(c p) i -> p c i", p=128)
                    [:, c0:c1, :]))
            self.wgld(nc.vector)
            wv4 = w[:].rearrange("p c (b j) -> p c b j", j=32)
            if DEQ_UPTO >= 2:
                self.V.tensor_reduce(
                    m[:].rearrange("p (c b) -> p c b", b=nblk), wv4, AX.X,
                    ALU.max, apply_absolute_value=True)
                self.V.tensor_scalar(m[:], m[:], 1.0 / 7.0, 1e-12,
                                        ALU.mult, ALU.max)
                self.V.reciprocal(si[:], m[:])
            else:
                self.V.memset(m[:], 1.0)
                self.V.memset(si[:], 1.0)
            if DEQ_UPTO >= 3:
                self.V.tensor_mul(
                    wv4, wv4,
                    si[:].rearrange("p (c b) -> p c b", b=nblk)
                    [:, :, :, None].broadcast_to([128, nchunk, nblk, 32]))
            if DEQ_UPTO >= 4:
                self.V.tensor_copy(ki[:], w[:])
                self.V.tensor_copy(kf[:], ki[:])
            else:
                self.V.tensor_copy(kf[:], w[:])
            if DEQ_UPTO >= 5:
                self.V.tensor_mul(
                    wb[:].rearrange("p c (b j) -> p c b j", j=32),
                    kf[:].rearrange("p c (b j) -> p c b j", j=32),
                    m[:].rearrange("p (c b) -> p c b", b=nblk)
                    [:, :, :, None].broadcast_to([128, nchunk, nblk, 32]))
            else:
                self.V.tensor_copy(wb[:], kf[:])
            return s2, wb

        for dsrc, dmy, nm in ((self.d_wup, self.d_upT_my, "up"),
                              (self.d_wgt, self.d_gtT_my, "gt")):
            s2, wb = dequant(dsrc, 4, IC, nm)
            if DEQ_UPTO < 6:
                s2.close()
                continue
            with s2:
                wt_ = s2.enter_context(nc.sbuf_tensor(f"wdt_{nm}",
                                                      [128, IC, 128], BF16))
                for ic in range(IC):
                    g = ic % 4
                    self.wv(nc.tensor)
                    ins = nc.tensor.transpose(ptr[:, 128 * g:128 * (g + 1)],
                                              wb[:, ic, :], self.identb[:])
                    self.tse.inc(ins)
                    self.wt(nc.vector)
                    self.V.tensor_copy(
                        wt_[:, ic, :], ptr[:, 128 * g:128 * (g + 1)])
                self.wv(nc.gpsimd)
                self.gld.dinc(nc.gpsimd.dma_start(
                    out=dmy.ap().rearrange("p (c j) -> p c j", j=128),
                    in_=wt_[:]))
                nc.gpsimd.wait_ge(self.gld.h, self.gld.v)
        s2, wb = dequant(self.d_wdn, 12, DC, "dn")
        if DEQ_UPTO < 6:
            s2.close()
            return
        with s2:
            wt_ = s2.enter_context(nc.sbuf_tensor("wdt_dn",
                                                  [128, 3, DC, 128], BF16))
            for it in range(3):
                for oc in range(DC):
                    g = (it * DC + oc) % 4
                    self.wv(nc.tensor)
                    ins = nc.tensor.transpose(ptr[:, 128 * g:128 * (g + 1)],
                                              wb[:, oc, 128 * it:128 * (it + 1)],
                                              self.identb[:])
                    self.tse.inc(ins)
                    self.wt(nc.vector)
                    self.V.tensor_copy(
                        wt_[:, it, oc, :], ptr[:, 128 * g:128 * (g + 1)])
            self.wv(nc.gpsimd)
            for it in range(3):
                self.gld.dinc(nc.gpsimd.dma_start(
                    out=self.d_dnT_my.ap()[128 * it:128 * (it + 1), :]
                    .rearrange("p (oc j) -> p oc j", j=128),
                    in_=wt_[:, it, :, :]))
            nc.gpsimd.wait_ge(self.gld.h, self.gld.v)
        if SETUP_UPTO == "deq":
            return
        for my, full in ((self.d_upT_my, self.d_upT),
                         (self.d_gtT_my, self.d_gtT),
                         (self.d_dnT_my, self.d_dnT)):
            if SIM_LOCAL:
                rows = my.shape[0]
                for rr in range(NCORES):
                    self.gld.dinc(nc.gpsimd.dma_start(
                        out=full.ap()[rr * rows:(rr + 1) * rows, :],
                        in_=my.ap()))
            else:
                self.cc.inc(nc.gpsimd.collective_compute(
                    "AllGather", ALU.bypass,
                    replica_groups=[list(range(NCORES))],
                    ins=[my.ap().opt()], outs=[full.ap().opt()]))
        nc.gpsimd.wait_ge(self.gld.h, self.gld.v)
        nc.gpsimd.wait_ge(self.cc.h, self.cc.v)


    def _ln_stats(self, xt, xsqt, pR, pBc):
        nc = self.nc
        self.wa(nc.tensor)
        self.wv(nc.tensor)
        for k in range(DC):
            ins = nc.tensor.matmul(pR[:], self.cst_b[:], xt[:, k, :],
                                   start=(k == 0), stop=(k == DC - 1))
        self.tse.inc(ins)
        self.wt(nc.vector)
        self.V.tensor_copy(self.mu_r[:], pR[:])
        self.wv(nc.tensor)
        for k in range(DC):
            ins = nc.tensor.matmul(pR[:], self.cst_b[:], xsqt[:, k, :],
                                   start=(k == 0), stop=(k == DC - 1))
        self.tse.inc(ins)
        self.wt(nc.vector)
        self.V.tensor_copy(self.ms_r[:], pR[:])
        self.V.tensor_mul(self.a_r[:], self.mu_r[:], self.mu_r[:])
        self.V.tensor_sub(self.a_r[:], self.ms_r[:], self.a_r[:])
        self.V.tensor_scalar(self.a_r[:], self.a_r[:], LN_EPS, None, ALU.add)
        self.wv(nc.scalar)
        self.A.activation(self.a_r[:], self.a_r[:], AF.Sqrt,
                                          bias=0.0, scale=1.0)
        self.wa(nc.vector)
        self.V.reciprocal(self.a_r[:], self.a_r[:])
        self.wv(nc.tensor)
        ins = nc.tensor.matmul(pBc[:, 0:512], self.ones_r[:], self.mu_r[:],
                               start=True, stop=True)
        self.tse.inc(ins)
        ins = nc.tensor.matmul(pBc[:, 512:1024], self.ones_r[:], self.a_r[:],
                               start=True, stop=True)
        self.tse.inc(ins)


    def a1(self, b):
        nc = self.nc
        ctx = ExitStack()
        sb = lambda name, shape, dt=F32: ctx.enter_context(
            nc.sbuf_tensor(name, shape, dt))
        ps = lambda name, shape, dt=F32: ctx.enter_context(
            nc.psum_tensor(name, shape, dt))
        with ctx:
            wslot = sb(f"a1_wslot{b}", [128, DC, D], BF16)
            self.load_weight(self.d_Wz, wslot)
            xc = sb(f"a1_xc{b}", [128, DC, 512], BF16)
            xsq = sb(f"a1_xsq{b}", [128, DC, 512], BF16)
            hc = sb(f"a1_hc{b}", [128, DC, 512], BF16)
            d1 = sb(f"a1_d1{b}", [128, DC, 512], BF16)
            d2 = sb(f"a1_d2{b}", [128, 512])
            d2big = sb(f"a1_d2b{b}", [128, DC, 512])
            zc = sb(f"a1_zc{b}", [128, DC, 512], BF16)
            pR = ps(f"a1_pR{b}", [1, 512])
            pBc = ps(f"a1_pBc{b}", [128, 1024])
            pZ0 = ps(f"a1_pZ0{b}", [128, 512])
            pZ1 = ps(f"a1_pZ1{b}", [128, 512])
            for ch in range(4):
                t0 = 512 * ch
                self.guard_load(nc.gpsimd)
                self.gld.dinc(nc.gpsimd.dma_start(
                    out=xc[:],
                    in_=self.d_x.ap()[b].rearrange("(k p) t -> p k t", p=128)
                    [:, :, t0:t0 + 512]))
                self.wgld(nc.scalar)
                self.A.activation(xsq[:], xc[:], AF.Square,
                                                  bias=0.0, scale=1.0)
                self._ln_stats(xc, xsq, pR, pBc)
                self.wt(nc.vector)
                self.V.tensor_sub(
                    d1[:], xc[:],
                    pBc[:, 0:512][:, None, :].broadcast_to([128, DC, 512]))
                self.V.tensor_mul(
                    d2big[:], d1[:],
                    self.mod_t[:, :, 1, b:b + 1]
                    .broadcast_to([128, DC, 512]))
                self.V.tensor_mul(
                    d2big[:], d2big[:],
                    pBc[:, 512:1024][:, None, :]
                    .broadcast_to([128, DC, 512]))
                self.V.tensor_add(
                    hc[:], d2big[:],
                    self.mod_t[:, :, 0, b:b + 1]
                    .broadcast_to([128, DC, 512]))
                self.wv(nc.tensor)
                ztok = [self.tok(), self.tok()]
                for j in range(DC):
                    jp = j % 2
                    pz = pZ0 if jp == 0 else pZ1
                    self.wtok(nc.tensor, ztok[jp])
                    for k in range(DC):
                        ins = nc.tensor.matmul(
                            pz[:], wslot[:, k, 128 * j:128 * (j + 1)],
                            hc[:, k, :], start=(k == 0), stop=(k == DC - 1))
                    self.tse.inc(ins)
                    eng, E = ((nc.vector, self.V) if jp == 0
                              else (nc.scalar, self.A))
                    self.wt(eng)
                    if jp == 0:
                        E.tensor_scalar(zc[:, j, :], pz[:],
                                        self.bz_t[:, j:j + 1], None, ALU.add)
                    else:
                        E.activation(zc[:, j, :], pz[:], AF.Identity,
                                     bias=self.bz_t[:, j:j + 1], scale=1.0)
                    ztok[jp] = self.tok()
                self.wa(nc.scalar)
                self.wv(nc.scalar)
                self.A.activation(xsq[:], zc[:], AF.Square,
                                                  bias=0.0, scale=1.0)
                self._ln_stats(zc, xsq, pR, pBc)
                self.wt(nc.vector)
                self.V.tensor_sub(
                    d1[:], zc[:],
                    pBc[:, 0:512][:, None, :].broadcast_to([128, DC, 512]))
                self.V.tensor_mul(
                    d2big[:], d1[:],
                    self.g_t[:, :, None].broadcast_to([128, DC, 512]))
                self.V.tensor_mul(
                    d2big[:], d2big[:],
                    pBc[:, 512:1024][:, None, :]
                    .broadcast_to([128, DC, 512]))
                self.V.tensor_add(
                    self.zln[:, :, t0:t0 + 512], d2big[:],
                    self.lb_t[:, :, None].broadcast_to([128, DC, 512]))


    def a2(self, b):
        nc = self.nc
        ctx = ExitStack()
        sb = lambda name, shape, dt=F32: ctx.enter_context(
            nc.sbuf_tensor(name, shape, dt))
        ps = lambda name, shape, dt=F32: ctx.enter_context(
            nc.psum_tensor(name, shape, dt))
        with ctx:
            hvx = ExitStack()
            wslot = hvx.enter_context(
                nc.sbuf_tensor(f"a2_wslot{b}", [128, DC, D], BF16))
            self.load_weight(self.d_Wh, wslot)
            pG = [ps(f"a2_pG{i}{b}", [128, 1024]) for i in range(2)]
            gtokv = [self.tok(), self.tok()]
            pOut = ps(f"a2_pOut{b}", [128, 2048])
            otok = self.tok()

            def grp(i):
                self.wtok(nc.tensor, gtokv[i])
                return pG[i]

            def evac_eng(i):
                return (nc.vector, self.V) if i % 2 == 0 else (nc.scalar, self.A)

            for tt in range(16):
                g = tt % 2
                pz = grp(g)
                for fc in range(2):
                    ins = nc.tensor.matmul(
                        pz[:, 512 * fc:512 * (fc + 1)], self.ones_r[:],
                        self.bh_row[:, 512 * fc:512 * (fc + 1)],
                        start=True, stop=False)
                    for k in range(DC):
                        ins = nc.tensor.matmul(
                            pz[:, 512 * fc:512 * (fc + 1)],
                            self.zln[:, k, 128 * tt:128 * (tt + 1)],
                            wslot[:, k, 512 * fc:512 * (fc + 1)],
                            start=False, stop=(k == DC - 1))
                self.tse.inc(ins)
                eng, E = evac_eng(tt)
                self.wt(eng)
                if tt % 2 == 0:
                    E.tensor_copy(self.hv[:, tt, :], pz[:])
                else:
                    E.activation(self.hv[:, tt, :], pz[:], AF.Identity,
                                 bias=0.0, scale=1.0)
                gtokv[g] = self.tok()
            if "Hv" in self.taps and b == 0:
                self.barrier()
                self._tap("Hv", self.hv[:].rearrange("p t f -> p (t f)"),
                          [128, 16 * D], BF16)
            hvx.close()
            wqd = sb(f"a2_wqd{b}", [128, 2, DC, 128], BF16)
            wrd = sb(f"a2_wrd{b}", [128, 2, DC, 128], BF16)
            wphid = sb(f"a2_wphid{b}", [64, 128], BF16)
            qd = sb(f"a2_qd{b}", [128, 2, 2, 2048], BF16)
            rd = sb(f"a2_rd{b}", [128, 2, 2, 2048], BF16)
            trig = sb(f"a2_trig{b}", [128, 2048])
            ki = sb(f"a2_ki{b}", [128, 2048], I32)
            cs = sb(f"a2_cs{b}", [128, 2048], BF16)
            rctm = sb(f"a2_rctm{b}", [128, 2048], BF16)
            msb = sb(f"a2_msb{b}", [128, 128], BF16)
            mqc = sb(f"a2_mqc{b}", [128, 2048], BF16)
            ysb = sb(f"a2_ysb{b}", [128, 16])
            itp = sb(f"a2_itp{b}", [16, 128])
            invr = sb(f"a2_invr{b}", [1, 2, 2048], BF16)
            invbc = sb(f"a2_invbc{b}", [128, 2, 2048], BF16)
            sct = sb(f"a2_sct{b}", [128, 2048], BF16)
            self.guard_load(nc.gpsimd)
            self.gld.dinc(nc.gpsimd.dma_start(out=wphid[:, 0:64],
                                              in_=self.d_Wphi.ap()))
            self.gld.dinc(nc.gpsimd.dma_start(out=wphid[:, 64:128],
                                              in_=self.d_Wphi.ap()))

            gtk0 = [None]
            gtk1 = [self.tok()]

            def et(E):
                s = self.vse if E is self.V else self.ase
                return (s.h, s.v)

            def wet(eng, t):
                if t is not None:
                    eng.wait_ge(t[0], t[1])

            def grp1():
                self.wtok(nc.tensor, gtk1[0])
                return pG[1]

            def chain_gen(p):
                pr = p % 2
                hA, hB = 2 * p, 2 * p + 1
                self.wt(nc.gpsimd)
                nc.gpsimd.wait_ge(self.gld.h, self.gld.v)
                for (dst, dram) in ((wqd, self.d_Wq), (wrd, self.d_Wr)):
                    for hi, h in enumerate((hA, hB)):
                        srcw = dram.ap().rearrange("(k p) j -> p k j", p=128)
                        srcw = srcw[:, :, HD * h:HD * (h + 1)]
                        self.gld.dinc(nc.gpsimd.dma_start(
                            out=dst[:, hi, :, 0:64], in_=srcw))
                        self.gld.dinc(nc.gpsimd.dma_start(
                            out=dst[:, hi, :, 64:128], in_=srcw))
                yield
                self.wgld(nc.tensor)
                for ti, (dst, wsl, bias) in enumerate(
                        ((qd, wqd, self.bq_dup), (rd, wrd, self.br_dup))):
                    for hi, h in enumerate((hA, hB)):
                        for g2 in range(2):
                            pz = grp1()
                            for half in range(2):
                                qc = 2 * g2 + half
                                for k in range(DC):
                                    ins = nc.tensor.matmul(
                                        pz[:, 512 * half:512 * (half + 1)],
                                        wsl[:, hi, k, :],
                                        self.zln[:, k,
                                                 512 * qc:512 * (qc + 1)],
                                        start=(k == 0), stop=(k == DC - 1))
                            self.tse.inc(ins)
                            eng, E = evac_eng(2 * ti + hi + g2)
                            self.wt(eng)
                            if (2 * ti + hi + g2) % 2 == 0:
                                E.tensor_scalar(
                                    dst[:, pr, hi, 1024 * g2:1024 * (g2 + 1)],
                                    pz[:], bias[:, h:h + 1], None, ALU.add)
                            else:
                                E.activation(
                                    dst[:, pr, hi, 1024 * g2:1024 * (g2 + 1)],
                                    pz[:], AF.Identity,
                                    bias=bias[:, h:h + 1], scale=1.0)
                            gtk1[0] = self.tok()
                            yield
                for hi in range(2):
                    for ti, srcd in enumerate((qd, rd)):
                        for g2 in range(2):
                            pz = grp1()
                            for half in range(2):
                                qc = 2 * g2 + half
                                ins = nc.tensor.matmul(
                                    pz[:, 512 * half:512 * (half + 1)],
                                    wphid[:, :],
                                    srcd[0:64, pr, hi,
                                         512 * qc:512 * (qc + 1)],
                                    start=True, stop=True)
                                self.tse.inc(ins)
                            self.wt(nc.scalar)
                            self.A.activation(
                                trig[:, 1024 * g2:1024 * (g2 + 1)], pz[:],
                                AF.Identity, bias=self.bphi_v[:], scale=1.0)
                            gtk1[0] = self.tok()
                            yield
                        self.wa(nc.vector)
                        self.V.tensor_scalar(ki[:].bitcast(F32), trig[:],
                                             INV_2PI, None, ALU.mult)
                        self.V.tensor_copy(ki[:], ki[:].bitcast(F32))
                        self.V.scalar_tensor_tensor(
                            trig[:], ki[:], -TWO_PI, trig[:], ALU.mult,
                            ALU.add)
                        yield
                        self.wv(nc.gpsimd)
                        self.G.tensor_scalar(
                            trig[:], trig[:], CLAMP, -CLAMP, ALU.min, ALU.max)
                        yield
                        self.wg(nc.scalar)
                        self.A.activation(
                            cs[:], trig[:], AF.Sin, bias=0.0, scale=1.0)
                        yield
                        self.wa(nc.vector)
                        self.V.tensor_mul(srcd[:, pr, hi, :],
                                          srcd[:, pr, hi, :], cs[:])
                        yield
                for hi in range(2):
                    self.wv(nc.tensor)
                    for kq in range(2):
                        pz = grp1()
                        v = pz[:].bitcast(BF16)
                        for j in range(8):
                            kt = 8 * kq + j
                            ins = nc.tensor.transpose(
                                v[:, 1024 * kq + 128 * j:
                                  1024 * kq + 128 * (j + 1)],
                                rd[:, pr, hi, 128 * kt:128 * (kt + 1)],
                                self.identb[:])
                            self.tse.inc(ins)
                        self.wt(nc.vector)
                        self.V.tensor_copy(
                            rctm[:, 1024 * kq:1024 * (kq + 1)],
                            v[:, 1024 * kq:1024 * (kq + 1)])
                        gtk1[0] = self.tok()
                        yield
                    pzM = grp1()
                    for kt in range(16):
                        ins = nc.tensor.matmul(
                            pzM[:, 0:128], rctm[:, 128 * kt:128 * (kt + 1)],
                            rctm[:, 128 * kt:128 * (kt + 1)],
                            start=(kt == 0), stop=(kt == 15))
                    self.tse.inc(ins)
                    self.wt(nc.vector)
                    self.V.tensor_copy(msb[:], pzM[:, 0:128])
                    gtk1[0] = self.tok()
                    yield
                    for g2 in range(2):
                        pz = grp1()
                        for half in range(2):
                            qc = 2 * g2 + half
                            ins = nc.tensor.matmul(
                                pz[:, 512 * half:512 * (half + 1)], msb[:],
                                qd[:, pr, hi, 512 * qc:512 * (qc + 1)],
                                start=True, stop=True)
                            self.tse.inc(ins)
                        eng, E = evac_eng(g2)
                        self.wt(eng)
                        if g2 % 2 == 0:
                            E.tensor_copy(mqc[:, 1024 * g2:1024 * (g2 + 1)],
                                          pz[:])
                        else:
                            E.activation(mqc[:, 1024 * g2:1024 * (g2 + 1)],
                                         pz[:], AF.Identity, bias=0.0,
                                         scale=1.0)
                        gtk1[0] = self.tok()
                        yield
                    self.wa(nc.vector)
                    self.V.tensor_mul(mqc[:], mqc[:], qd[:, pr, hi, :])
                    gtk1[0] = self.tok()
                    yield
                    pzY = grp1()
                    for qt in range(16):
                        ins = nc.tensor.matmul(
                            pzY[:, qt:qt + 1],
                            mqc[:, 128 * qt:128 * (qt + 1)],
                            self.ones_cb[:], start=True, stop=True)
                        self.tse.inc(ins)
                    self.wt(nc.vector)
                    self.V.tensor_scalar(
                        ysb[:], pzY[:, 0:16], 1e-24, None, ALU.max)
                    gtk1[0] = self.tok()
                    yield
                    self.wv(nc.scalar)
                    self.A.activation(ysb[:], ysb[:], AF.Sqrt,
                                      bias=0.0, scale=1.0)
                    self.wa(nc.vector)
                    self.V.reciprocal(ysb[:], ysb[:])
                    yield
                    pzT = grp1()
                    self.wv(nc.tensor)
                    ins = nc.tensor.transpose(pzT[0:16, 0:128], ysb[:],
                                              self.identf[:])
                    self.tse.inc(ins)
                    self.wt(nc.vector)
                    self.wgld(nc.vector)
                    self.V.tensor_copy(itp[:], pzT[0:16, 0:128])
                    gtk1[0] = self.tok()
                    self.wv(nc.gpsimd)
                    self.gld.dinc(nc.gpsimd.dma_start(out=invr[:, hi, :],
                                                      in_=itp[:]))
                    yield
                self.wgld(nc.tensor)
                for g2 in range(2):
                    pz = grp1()
                    for half in range(2):
                        qc = 2 * g2 + half
                        ins = nc.tensor.matmul(
                            pz[0:64, 512 * half:512 * (half + 1)],
                            self.ones_rb[:, 0:64],
                            invr[:, 0, 512 * qc:512 * (qc + 1)],
                            start=True, stop=True)
                        self.tse.inc(ins)
                        ins = nc.tensor.matmul(
                            pz[64:128, 512 * half:512 * (half + 1)],
                            self.ones_rb[:, 0:64],
                            invr[:, 1, 512 * qc:512 * (qc + 1)],
                            start=True, stop=True, tile_position=(0, 64))
                        self.tse.inc(ins)
                    eng, E = evac_eng(g2)
                    self.wt(eng)
                    if g2 % 2 == 0:
                        E.tensor_copy(invbc[:, pr, 1024 * g2:1024 * (g2 + 1)],
                                      pz[:])
                    else:
                        E.activation(invbc[:, pr, 1024 * g2:1024 * (g2 + 1)],
                                     pz[:], AF.Identity, bias=0.0, scale=1.0)
                    gtk1[0] = self.tok()
                    yield

            def scores_gen(p, otc):
                pr = p % 2
                hA, hB = 2 * p, 2 * p + 1
                for hi, h in enumerate((hA, hB)):
                    for kt in range(16):
                        stok = [None, None]
                        for g2 in range(2):
                            wet(nc.tensor, gtk0[0])
                            for half in range(2):
                                qc = 2 * g2 + half
                                ins = nc.tensor.matmul(
                                    pG[0][:, 512 * half:512 * (half + 1)],
                                    rd[:, pr, hi, 128 * kt:128 * (kt + 1)],
                                    qd[:, pr, hi, 512 * qc:512 * (qc + 1)],
                                    start=True, stop=True)
                                self.tse.inc(ins)
                            eng, E = evac_eng(g2 + kt)
                            self.wt(eng)
                            if (g2 + kt) % 2 == 0:
                                E.tensor_copy(
                                    sct[:, 1024 * g2:1024 * (g2 + 1)],
                                    pG[0][:])
                            else:
                                E.activation(
                                    sct[:, 1024 * g2:1024 * (g2 + 1)],
                                    pG[0][:], AF.Identity, bias=0.0,
                                    scale=1.0)
                            gtk0[0] = et(E)
                            stok[g2] = gtk0[0]
                        for qc in range(4):
                            wet(nc.tensor, stok[qc // 2])
                            if kt == 0:
                                wet(nc.tensor, otc[0])
                            ins = nc.tensor.matmul(
                                pOut[64 * hi:64 * (hi + 1),
                                     512 * qc:512 * (qc + 1)],
                                self.hv[:, kt, HD * h:HD * (h + 1)],
                                sct[:, 512 * qc:512 * (qc + 1)],
                                start=(kt == 0), stop=(kt == 15),
                                tile_position=(0, 64 * hi))
                            self.tse.inc(ins)
                        yield
                self.wt(nc.vector)
                self.wa(nc.vector)
                self.V.tensor_mul(self.outfm[:, p, :], pOut[:],
                                  invbc[:, pr, :])
                otc[0] = et(self.V)

            _SENT = object()

            def advance(gen, n):
                if gen is None:
                    return False
                for _ in range(n):
                    if next(gen, _SENT) is _SENT:
                        return False
                return True

            gtk0[0] = et(self.V)
            cg = chain_gen(0)
            while advance(cg, 8):
                pass
            otc = [None]
            for p in range(NPAIR):
                sg = scores_gen(p, otc)
                cg = chain_gen(p + 1) if p + 1 < NPAIR else None
                s_alive, c_alive = True, True
                while s_alive or c_alive:
                    s_alive = advance(sg, 1)
                    c_alive = advance(cg, 2)
            if "Qc" in self.taps and b == 0:
                self.barrier()
                self._tap("Qc", qd[:, (NPAIR - 1) % 2, 0, :], [128, 2048], BF16)
                self._tap("Rc", rd[:, (NPAIR - 1) % 2, 0, :], [128, 2048], BF16)
                self._tap("invbc", invbc[:, (NPAIR - 1) % 2, :], [128, 2048], BF16)
            if "outfm" in self.taps and b == 0:
                self.barrier()
                self._tap("outfm", self.outfm[:].rearrange("p c q -> p (c q)"),
                          [128, DC * 2048], BF16)


    def wo_epilogue(self, b):
        nc = self.nc
        ctx = ExitStack()
        sb = lambda name, shape, dt=F32: ctx.enter_context(
            nc.sbuf_tensor(name, shape, dt))
        ps = lambda name, shape, dt=F32: ctx.enter_context(
            nc.psum_tensor(name, shape, dt))
        with ctx:
            wslot = sb(f"wo_wslot{b}", [128, DC, D], BF16)
            self.load_weight(self.d_Wo, wslot)
            pA = ps(f"wo_pA{b}", [128, 512])
            pB = ps(f"wo_pB{b}", [128, 512])
            pR = ps(f"wo_pR{b}", [1, 512])
            pBc = ps(f"wo_pBc{b}", [128, 1024])
            pTr = ps(f"wo_pTr{b}", [128, 512])
            x2c = sb(f"wo_x2c{b}", [128, DC, 512])
            x2b = sb(f"wo_x2b{b}", [128, DC, 512], BF16)
            x2sq = sb(f"wo_x2sq{b}", [128, DC, 512], BF16)
            xfc = sb(f"wo_xfc{b}", [128, DC, 512])
            h2c = sb(f"wo_h2c{b}", [128, DC, 512], BF16)
            d1 = sb(f"wo_d1{b}", [128, DC, 512], BF16)
            d2 = sb(f"wo_d2{b}", [128, 512])
            x2t = sb(f"wo_x2t{b}", [128, 4, 128])
            for qc in range(4):
                t0 = 512 * qc
                gt0 = b * S + t0
                self.guard_load(nc.sync)
                self.sld.dinc(nc.sync.dma_start(
                    out=xfc[:],
                    in_=self.d_x.ap()[b].rearrange("(k p) t -> p k t", p=128)
                    [:, :, t0:t0 + 512]))
                wtok2 = [self.tok(), self.tok()]
                for e in range(DC):
                    ep = e % 2
                    pz = pA if ep == 0 else pB
                    self.wtok(nc.tensor, wtok2[ep])
                    for k in range(DC):
                        ins = nc.tensor.matmul(
                            pz[:], wslot[:, k, 128 * e:128 * (e + 1)],
                            self.outfm[:, k, t0:t0 + 512],
                            start=(k == 0), stop=(k == DC - 1))
                    self.tse.inc(ins)
                    self.wt(nc.scalar)
                    self.A.activation(
                        x2c[:, e, :], pz[:], AF.Identity,
                        bias=self.gabo_t[:, e, b:b + 1],
                        scale=self.mod_t[:, e, 2, b:b + 1])
                    wtok2[ep] = self.tok()
                    self.wsld(nc.vector)
                    self.wa(nc.vector)
                    self.V.tensor_add(x2c[:, e, :], x2c[:, e, :],
                                         xfc[:, e, :])
                    self.V.tensor_copy(x2b[:, e, :],
                                                       x2c[:, e, :])
                    self.wv(nc.sync)
                    self.sld.dinc(nc.sync.dma_start(
                        out=self.d_x2f[128 * e:128 * (e + 1), gt0:gt0 + 512],
                        in_=x2c[:, e, :]))
                    self.wv(nc.tensor)
                    self.wgld(nc.tensor)
                    for j in range(4):
                        ins = nc.tensor.transpose(
                            pTr[:, 128 * j:128 * (j + 1)],
                            x2c[:, e, 128 * j:128 * (j + 1)], self.identf[:])
                    self.tse.inc(ins)
                    self.wt(nc.vector)
                    self.V.tensor_copy(
                        x2t[:].rearrange("p j f -> p (j f)"), pTr[:])
                    self.wv(nc.gpsimd)
                    self.gld.dinc(nc.gpsimd.dma_start(
                        out=self.d_x2t.ap()
                        [gt0:gt0 + 512, 128 * e:128 * (e + 1)]
                        .rearrange("(j p) f -> p j f", p=128),
                        in_=x2t[:]))
                self.wv(nc.scalar)
                self.A.activation(x2sq[:], x2b[:], AF.Square,
                                                  bias=0.0, scale=1.0)
                self._ln_stats(x2b, x2sq, pR, pBc)
                self.wt(nc.vector)
                self.wgld(nc.vector)
                self.V.tensor_sub(
                    d1[:], x2b[:],
                    pBc[:, 0:512][:, None, :].broadcast_to([128, DC, 512]))
                self.wgld(nc.vector)
                self.wsld(nc.vector)
                self.V.tensor_mul(
                    x2c[:], d1[:],
                    self.mod_t[:, :, 4, b:b + 1]
                    .broadcast_to([128, DC, 512]))
                self.V.tensor_mul(
                    x2c[:], x2c[:],
                    pBc[:, 512:1024][:, None, :]
                    .broadcast_to([128, DC, 512]))
                self.V.tensor_add(
                    h2c[:], x2c[:],
                    self.mod_t[:, :, 3, b:b + 1]
                    .broadcast_to([128, DC, 512]))
                self.V.tensor_reduce(
                    d2[:, 0:1], h2c[:].rearrange("p k t -> p (k t)"), AX.X,
                    ALU.max, apply_absolute_value=True)
                self.V.tensor_max(self.h2max[:],
                                                  self.h2max[:], d2[:, 0:1])
                self.wv(nc.gpsimd)
                self.gld.dinc(nc.gpsimd.dma_start(
                    out=self.d_h2[:, gt0:gt0 + 512]
                    .rearrange("(k p) t -> p k t", p=128),
                    in_=h2c[:]))
                self.wgld(nc.scalar)


    _gs_n = 0

    def global_scale(self, idx, locmax_col, nvals=1):
        nc = self.nc
        ctx = ExitStack()
        with ctx:
            ps = lambda name, shape, dt=F32: ctx.enter_context(
                nc.psum_tensor(name, shape, dt))
            sbl = lambda name, shape, dt=F32: ctx.enter_context(
                nc.sbuf_tensor(name, shape, dt))
            Prog._gs_n += 1
            u = Prog._gs_n
            pT = ps(f"gs_p{idx}_{u}", [128, 128])
            mrow = sbl(f"gs_m{idx}_{u}", [1, 64])
            m2 = sbl(f"gs_m2{idx}_{u}", [2, 1])
            srow = sbl(f"gs_s{idx}_{u}", [1, 16])
            self.wv(nc.tensor)
            self.wa(nc.tensor)
            ins = nc.tensor.transpose(pT[0:nvals, 0:128],
                                      locmax_col[:, 0:nvals], self.identf[:])
            self.tse.inc(ins)
            self.wt(nc.vector)
            self.V.memset(mrow[:], 0.0)
            self.V.tensor_reduce(m2[0:nvals, :], pT[0:nvals, 0:128], AX.X,
                                 ALU.max)
            self.wv(nc.gpsimd)
            self.gld.dinc(nc.gpsimd.dma_start(out=mrow[0:1, 0:nvals],
                                              in_=m2[0:nvals, :]))
            nc.gpsimd.wait_ge(self.gld.h, self.gld.v)
            self.gld.dinc(nc.gpsimd.dma_start(out=self.d_arin[idx].ap(),
                                              in_=mrow[:]))
            nc.gpsimd.wait_ge(self.gld.h, self.gld.v)
            if SIM_LOCAL:
                self.gld.dinc(nc.gpsimd.dma_start(
                    out=self.d_arout[idx].ap(), in_=self.d_arin[idx].ap()))
                nc.gpsimd.wait_ge(self.gld.h, self.gld.v)
            else:
                self.cc.inc(nc.gpsimd.collective_compute(
                    "AllReduce", ALU.max, replica_groups=[list(range(NCORES))],
                    ins=[self.d_arin[idx].ap().opt()],
                    outs=[self.d_arout[idx].ap().opt()]))
                nc.gpsimd.wait_ge(self.cc.h, self.cc.v)
            self.gld.dinc(nc.gpsimd.dma_start(out=mrow[:],
                                              in_=self.d_arout[idx].ap()))
            self.wgld(nc.vector)
            self.V.memset(srow[:], 1.0)
            self.V.tensor_scalar(srow[:, 0:nvals], mrow[:, 0:nvals],
                                    1.0 / QMAX, 1e-12, ALU.mult, ALU.max)
            self.V.reciprocal(srow[:, 8:8 + nvals],
                                              srow[:, 0:nvals])
            self.wv(nc.tensor)
            ins = nc.tensor.matmul(pT[:, 0:16], self.ones_r[:], srow[:],
                                   start=True, stop=True)
            self.tse.inc(ins)
            self.wt(nc.vector)
            for i in range(nvals):
                self.V.tensor_copy(
                    self.sc_col[:, 2 * (idx + i):2 * (idx + i) + 1],
                    pT[:, i:i + 1])
                self.V.tensor_copy(
                    self.sc_col[:, 2 * (idx + i) + 1:2 * (idx + i) + 2],
                    pT[:, 8 + i:8 + i + 1])


    def ffn(self):
        nc = self.nc
        ctx = ExitStack()
        sb = lambda name, shape, dt=F32: ctx.enter_context(
            nc.sbuf_tensor(name, shape, dt))
        ps = lambda name, shape, dt=F32: ctx.enter_context(
            nc.psum_tensor(name, shape, dt))
        mx2 = self.ctx.enter_context(nc.sbuf_tensor("ffn_mx2", [128, 2], F32))
        with ctx:
            hq = sb("ffn_hq", [128, DC, T], BF16)
            t1 = sb("ffn_t1", [128, 2048])
            ki = sb("ffn_ki", [128, 2048], I32)
            hb = sb("ffn_hb", [128, DC, 512], BF16)
            wu = [sb(f"ffn_wu{i}", [128, DC, 128], BF16) for i in range(2)]
            wg = [sb(f"ffn_wg{i}", [128, DC, 128], BF16) for i in range(2)]

            def load_slices(j, par):
                nc.sync.wait_ge(self.tse.h, self.tse.v)
                self.sld.dinc(nc.sync.dma_start(
                    out=wu[par][:],
                    in_=self.d_upT.ap().rearrange("(k p) i -> p k i", p=128)
                    [:, :, 128 * j:128 * (j + 1)]))
                self.sld.dinc(nc.sync.dma_start(
                    out=wg[par][:],
                    in_=self.d_gtT.ap().rearrange("(k p) i -> p k i", p=128)
                    [:, :, 128 * j:128 * (j + 1)]))

            for ch in range(8):
                t0 = 512 * ch
                self.guard_load(nc.gpsimd)
                self.gld.dinc(nc.gpsimd.dma_start(
                    out=hb[:], in_=self.d_h2[:, t0:t0 + 512]
                    .rearrange("(k p) t -> p k t", p=128)))
                self.wgld(nc.vector)
                for kg in range(2):
                    sl = slice(kg * 4, kg * 4 + 4)
                    hv_ = hb[:, sl, :].rearrange("p k t -> p (k t)")
                    t1v = t1[:].rearrange("p (k t) -> p k t", t=512)
                    kiv = ki[:].rearrange("p (k t) -> p k t", t=512)
                    self.V.tensor_scalar(t1[:], hv_, self.sc_col[:, 1:2],
                                            None, ALU.mult)
                    self.V.tensor_copy(ki[:], t1[:])
                    self.V.tensor_scalar(
                        hq[:, sl, t0:t0 + 512],
                        kiv, self.sc_col[:, 0:1], None, ALU.mult)
            if "hq" in self.taps:
                self.barrier()
                self._tap("hq", hq[:].rearrange("p k t -> p (k t)"),
                          [128, DC * T], BF16)
            if self.upto == "hq":
                self.barrier()
                return
            pU0 = ps("ffn_pU0", [128, 512])
            pU1 = ps("ffn_pU1", [128, 512])
            pG0 = ps("ffn_pG0", [128, 512])
            pG1 = ps("ffn_pG1", [128, 512])
            upc = sb("ffn_upc", [128, 4, 512], BF16)
            gtc = sb("ffn_gtc", [128, 4, 512], BF16)
            red = sb("ffn_red", [128, 1])
            self.V.memset(mx2[:], 0.0)
            load_slices(0, 0)
            utok = [self.tok(), self.tok()]
            gtok = [self.tok(), self.tok()]
            sttok = [self.tok()] * 4
            for ic in range(IC):
                wpar = ic % 2
                self.wsld(nc.tensor)
                if ic + 1 < IC:
                    load_slices(ic + 1, 1 - wpar)
                for tc in range(8):
                    par = tc % 2
                    pu = pU0 if par == 0 else pU1
                    pg = pG0 if par == 0 else pG1
                    self.wtok(nc.tensor, utok[par])
                    for k in range(DC):
                        ins = nc.tensor.matmul(
                            pu[:], wu[wpar][:, k, :],
                            hq[:, k, 512 * tc:512 * (tc + 1)],
                            start=(k == 0), stop=(k == DC - 1))
                    self.tse.inc(ins)
                    self.wtok(nc.tensor, gtok[par])
                    for k in range(DC):
                        ins = nc.tensor.matmul(
                            pg[:], wg[wpar][:, k, :],
                            hq[:, k, 512 * tc:512 * (tc + 1)],
                            start=(k == 0), stop=(k == DC - 1))
                    self.tse.inc(ins)
                    slot = (ic * 8 + tc) % 4
                    self.wt(nc.vector)
                    self.wtok(nc.vector, sttok[slot])
                    self.V.tensor_copy(upc[:, slot, :], pu[:])
                    utok[par] = self.tok()
                    self.wt(nc.scalar)
                    self.wtok(nc.scalar, sttok[slot])
                    self.A.activation(
                        gtc[:, slot, :], pg[:], AF.Identity, bias=0.0,
                        scale=1.0)
                    gtok[par] = self.tok()
                    self.wa(nc.vector)
                    self.V.tensor_reduce(red[:], upc[:, slot, :], AX.X,
                                            ALU.max, apply_absolute_value=True)
                    self.V.tensor_max(mx2[:, 0:1], mx2[:, 0:1], red[:])
                    self.V.tensor_reduce(red[:], gtc[:, slot, :], AX.X,
                                            ALU.max, apply_absolute_value=True)
                    self.V.tensor_max(mx2[:, 1:2],
                                                      mx2[:, 1:2], red[:])
                    self.wv(nc.sync)
                    self.wa(nc.sync)
                    self.sld.dinc(nc.sync.dma_start(
                        out=self.d_up[128 * ic:128 * (ic + 1),
                                      512 * tc:512 * (tc + 1)],
                        in_=upc[:, slot, :]))
                    self.sld.dinc(nc.sync.dma_start(
                        out=self.d_gt[128 * ic:128 * (ic + 1),
                                      512 * tc:512 * (tc + 1)],
                        in_=gtc[:, slot, :]))
                    if slot == 3:
                        self.scommit()
                        for s_ in range(4):
                            sttok[s_] = (0, 0, 0, self.scm.v)
        self.barrier()
        if self.upto == "s1":
            if "up" in self.taps:
                self._tap("up", self.d_up.ap(), [I3, T], BF16)
            if "gt" in self.taps:
                self._tap("gt", self.d_gt.ap(), [I3, T], BF16)
            return
        self.global_scale(1, mx2[:], nvals=2)
        self.barrier()
        ctx = ExitStack()
        sb = lambda name, shape, dt=F32: ctx.enter_context(
            nc.sbuf_tensor(name, shape, dt))
        with ctx:
            bufA = sb("ffn_bA", [128, 3, T], BF16)
            bufB = sb("ffn_bB", [128, 3, T], BF16)
            kk = sb("ffn_kk", [128, 2, T], I32)
            kb = sb("ffn_kb", [128, 2, T], I32)
            mxc = sb("ffn_mxc", [128, IC])
            ltok = [None, None, None]
            kfree = [None, None]
            tA1 = [None] * IC
            tV2 = [None] * IC
            tA3 = [None] * IC

            def s2_front(j):
                p3, p2 = j % 3, j % 2
                if ltok[p3] is not None:
                    nc.sync.wait_ge(self.gcm.h, ltok[p3])
                self.sld.dinc(nc.sync.dma_start(
                    out=bufA[:, p3, :],
                    in_=self.d_gt[128 * j:128 * (j + 1), :]))
                self.wsld(nc.scalar)
                if kfree[p2] is not None:
                    nc.scalar.wait_ge(self.vse.h, kfree[p2])
                self.A.activation(kk[:, p2, :].bitcast(F32), bufA[:, p3, :],
                                  AF.Identity, bias=0.0,
                                  scale=self.sc_col[:, 5:6])
                tA1[j] = self.ase.v
                nc.vector.wait_ge(self.ase.h, tA1[j])
                self.V.tensor_copy(kk[:, p2, :], kk[:, p2, :].bitcast(F32))
                self.V.tensor_scalar(bufB[:, p3, :], kk[:, p2, :],
                                     self.sc_col[:, 4:5], None, ALU.mult)
                kfree[p2] = self.vse.v
                tV2[j] = self.vse.v

            def s2_back(j):
                p3, p2 = j % 3, j % 2
                nc.scalar.wait_ge(self.vse.h, tV2[j])
                self.A.activation(kb[:, p2, :].bitcast(F32), bufB[:, p3, :],
                                  AF.Sigmoid, bias=0.0, scale=1.0)
                self.A.activation(kb[:, p2, :].bitcast(F32),
                                  kb[:, p2, :].bitcast(F32),
                                  AF.Identity, bias=0.0, scale=65536.0)
                tA3[j] = self.ase.v
                nc.vector.wait_ge(self.ase.h, tA3[j])
                self.V.tensor_copy(kb[:, p2, :], kb[:, p2, :].bitcast(F32))
                self.V.scalar_tensor_tensor(
                    bufA[:, p3, :], kb[:, p2, :], 65535.0, bufB[:, p3, :],
                    ALU.min, ALU.mult)
                self.V.tensor_reduce(mxc[:, j:j + 1], bufA[:, p3, :], AX.X,
                                     ALU.max, apply_absolute_value=True)
                nc.gpsimd.wait_ge(self.vse.h, self.vse.v)
                self.gld.dinc(nc.gpsimd.dma_start(
                    out=self.d_g2[128 * j:128 * (j + 1), :],
                    in_=bufA[:, p3, :]))
                self.gcommit()
                ltok[p3] = self.gcm.v

            s2_front(0)
            for ic in range(IC):
                if ic + 1 < IC:
                    s2_front(ic + 1)
                s2_back(ic)
            self.V.tensor_reduce(mx2[:, 0:1], mxc[:], AX.X, ALU.max)
            self.barrier()
            if self.upto == "s2":
                if "g2" in self.taps:
                    self._tap("g2", self.d_g2.ap(), [I3, T], BF16)
                return
            self.global_scale(3, mx2[:, 0:1], nvals=1)
            self.barrier()
            ltok = [None, None, None]
            kfree = [None, None]
            tA3 = [None] * IC

            def s3_front(j):
                p3, p2 = j % 3, j % 2
                if ltok[p3] is not None:
                    nc.sync.wait_ge(self.gcm.h, ltok[p3])
                self.sld.dinc(nc.sync.dma_start(
                    out=bufA[:, p3, :],
                    in_=self.d_g2[128 * j:128 * (j + 1), :]))
                self.sld.dinc(nc.sync.dma_start(
                    out=bufB[:, p3, :],
                    in_=self.d_up[128 * j:128 * (j + 1), :]))
                self.wsld(nc.scalar)
                if kfree[p2] is not None:
                    nc.scalar.wait_ge(self.vse.h, kfree[p2])
                self.A.activation(kk[:, p2, :].bitcast(F32), bufA[:, p3, :],
                                  AF.Identity, bias=0.0,
                                  scale=self.sc_col[:, 7:8])
                self.A.activation(kb[:, p2, :].bitcast(F32), bufB[:, p3, :],
                                  AF.Identity, bias=0.0,
                                  scale=self.sc_col[:, 3:4])
                tA3[j] = self.ase.v

            def s3_back(j):
                p3, p2 = j % 3, j % 2
                nc.vector.wait_ge(self.ase.h, tA3[j])
                self.V.tensor_copy(kk[:, p2, :], kk[:, p2, :].bitcast(F32))
                self.V.tensor_copy(kb[:, p2, :], kb[:, p2, :].bitcast(F32))
                self.V.tensor_mul(bufA[:, p3, :], kk[:, p2, :], kb[:, p2, :])
                self.V.tensor_reduce(mxc[:, j:j + 1], bufA[:, p3, :], AX.X,
                                     ALU.max, apply_absolute_value=True)
                kfree[p2] = self.vse.v
                nc.gpsimd.wait_ge(self.vse.h, self.vse.v)
                self.gld.dinc(nc.gpsimd.dma_start(
                    out=self.d_o[128 * j:128 * (j + 1), :],
                    in_=bufA[:, p3, :]))
                self.gcommit()
                ltok[p3] = self.gcm.v

            s3_front(0)
            for ic in range(IC):
                if ic + 1 < IC:
                    s3_front(ic + 1)
                s3_back(ic)
            self.V.tensor_reduce(mx2[:, 0:1], mxc[:], AX.X, ALU.max)
            self.V.tensor_scalar(mx2[:, 0:1], mx2[:, 0:1],
                                 self.sc_col[:, 6:7], None, ALU.mult)
            self.V.tensor_scalar(mx2[:, 0:1], mx2[:, 0:1],
                                 self.sc_col[:, 2:3], None, ALU.mult)
            self.V.tensor_scalar(mx2[:, 0:1], mx2[:, 0:1], 1.0 / 65536.0,
                                 None, ALU.mult)
        self.barrier()
        if self.upto == "s3":
            if "o" in self.taps:
                self._tap("o", self.d_o.ap(), [I3, T], BF16)
            return
        self.global_scale(0, mx2[:, 0:1], nvals=1)
        self.V.tensor_mul(self.sc_col[:, 9:10], self.sc_col[:, 6:7],
                          self.sc_col[:, 2:3])
        self.V.tensor_scalar(self.sc_col[:, 9:10], self.sc_col[:, 9:10],
                             self.sc_col[:, 1:2], None, ALU.mult)
        self.V.tensor_scalar(self.sc_col[:, 9:10], self.sc_col[:, 9:10],
                             1.0 / 65536.0, None, ALU.mult)
        self.barrier()
        ctx = ExitStack()
        sb = lambda name, shape, dt=F32: ctx.enter_context(
            nc.sbuf_tensor(name, shape, dt))
        ps = lambda name, shape, dt=F32: ctx.enter_context(
            nc.psum_tensor(name, shape, dt))
        with ctx:
            wdnT = sb("ffn_wdnT", [128, IC, D], BF16)
            ot = sb("ffn_ot", [128, 2, IC, 128], BF16)
            otf = sb("ffn_otf", [128, 2, IC * 128])
            oki = sb("ffn_oki", [128, 2, IC * 128], I32)
            x2tc = sb("ffn_x2tc", [128, 2, 2, 512])
            osb = sb("ffn_osb", [128, 2, 512])
            pD0 = ps("ffn_pD0", [128, 512])
            pD1 = ps("ffn_pD1", [128, 512])
            for c0 in range(0, IC, 8):
                self.sld.dinc(nc.sync.dma_start(
                    out=wdnT[:, c0:c0 + 8, :],
                    in_=self.d_dnT.ap().rearrange("(c p) d -> p c d", p=128)
                    [:, c0:c0 + 8, :]))
            self.wsld(nc.tensor)
            dtok = [self.tok(), self.tok()]
            ostok = [self.tok(), self.tok()]
            sotok = [self.tok(), self.tok()]
            qtoks = [None] * 33

            def s4_load(tt):
                par = tt % 2
                t0 = 128 * tt
                self.wtok(nc.gpsimd, dtok[par])
                nc.gpsimd.wait_ge(self.gld.h, self.gld.v)
                for c0 in range(0, IC, 8):
                    self.gld.dinc(nc.gpsimd.dma_start(
                        out=ot[:, par, c0:c0 + 8, :], in_=self.d_o.ap()
                        .rearrange("(c p) t -> p c t", p=128)
                        [:, c0:c0 + 8, t0:t0 + 128]))
                self.wtok(nc.sync, dtok[par])
                self.sld.dinc(nc.sync.dma_start(
                    out=x2tc[:, par, :, :],
                    in_=self.d_x2t.ap()[t0:t0 + 128, :]
                    .rearrange("p (a f) -> p a f", f=512)))

            def s4_qdq(tt):
                par = tt % 2
                self.wgld(nc.vector)
                self.wsld(nc.vector)
                otv = ot[:, par, :, :].rearrange("p c t -> p (c t)")
                self.V.tensor_scalar(otf[:, par, :], otv,
                                     self.sc_col[:, 9:10], None, ALU.mult)
                self.V.tensor_copy(oki[:, par, :], otf[:, par, :])
                self.V.tensor_scalar(
                    otv, oki[:, par, :], self.sc_col[:, 0:1], None, ALU.mult)
                qtoks[tt] = self.tok()

            s4_load(0)
            s4_qdq(0)
            s4_load(1)
            for tt in range(32):
                b = tt // 16
                t0 = 128 * tt
                par = tt % 2
                if tt + 1 < 32:
                    s4_qdq(tt + 1)
                for a in range(2):
                    pz = pD0 if a == 0 else pD1
                    self.wtok(nc.tensor, qtoks[tt])
                    self.wtok(nc.tensor, ostok[a])
                    for c in range(IC):
                        ins = nc.tensor.matmul(
                            pz[:], ot[:, par, c, :],
                            wdnT[:, c, 512 * a:512 * (a + 1)],
                            start=(c == 0), stop=(c == IC - 1))
                    self.tse.inc(ins)
                    self.wt(nc.vector)
                    self.wtok(nc.vector, sotok[a])
                    self.V.tensor_mul(
                        osb[:, a, :], pz[:],
                        self.gbbc[:, b, 512 * a:512 * (a + 1)])
                    self.V.tensor_add(
                        osb[:, a, :], osb[:, a, :], x2tc[:, par, a, :])
                    ostok[a] = self.tok()
                dtok[par] = (self.vse.v, 0, 0, 0, self.tse.v)
                if tt + 2 < 32:
                    s4_load(tt + 2)
                self.wv(nc.sync)
                self.sld.dinc(nc.sync.dma_start(
                    out=self.d_out.ap()[b, (t0 - b * S):(t0 - b * S) + 128, :],
                    in_=osb[:].rearrange("p a f -> p (a f)")))
                self.scommit()
                sotok[0] = (0, 0, 0, self.scm.v)
                sotok[1] = (0, 0, 0, self.scm.v)

    def build(self):
        nc = self.nc
        self.setup()
        if self.upto == "setup":
            self._finish()
            return nc
        actx = ExitStack()
        sb = lambda name, shape, dt=F32: actx.enter_context(
            nc.sbuf_tensor(name, shape, dt))
        with actx:
            self.zln = sb("zln", [128, DC, S], BF16)
            self.hv = sb("hv", [128, 16, D], BF16)
            self.outfm = sb("outfm", [128, DC, S], BF16)
            for b in range(BL):
                self.a1(b)
                self.barrier()
                if self.upto == "a1" and b == 0:
                    if "zln" in self.taps:
                        self._tap("zln",
                                  self.zln[:].rearrange("p k t -> p (k t)"),
                                  [128, DC * S], BF16)
                    self._finish()
                    return nc
                self.a2(b)
                self.barrier()
                if self.upto == "a2" and b == 0:
                    self._finish()
                    return nc
                self.wo_epilogue(b)
                self.barrier()
        if self.upto == "attn":
            if "x2f" in self.taps:
                self._tap("x2f", self.d_x2f.ap(), [D, T], F32)
            if "x2t" in self.taps:
                self._tap("x2t", self.d_x2t.ap(), [T, D], F32)
            if "h2" in self.taps:
                self._tap("h2", self.d_h2.ap(), [D, T], BF16)
            self._finish()
            return nc
        self.global_scale(0, self.h2max[:], nvals=1)
        self.barrier()
        self.ffn()
        self.barrier()
        self._finish()
        return nc

    def _finish(self):
        nc = self.nc
        self.gcommit()
        self.scommit()
        nc.sync.wait_ge(self.sld.h, self.sld.v)
        nc.gpsimd.wait_ge(self.gld.h, self.gld.v)
        nc.gpsimd.wait_ge(self.cc.h, self.cc.v)


def build_program(upto="all", taps=()):
    p = Prog(upto=upto, taps=taps)
    nc = p.build()
    return nc, p


def make_feeds(inputs):
    x = np.asarray(inputs["L0_attn_query"])
    c = np.asarray(inputs["L0_c"])
    base = {
        "wmod": np.ascontiguousarray(inputs["L0_wmod"]),
        "Wz": np.ascontiguousarray(inputs["L0_attn_Wz"]),
        "Wq": np.ascontiguousarray(inputs["L0_attn_Wq"]),
        "Wr": np.ascontiguousarray(inputs["L0_attn_Wr"]),
        "Wh": np.ascontiguousarray(inputs["L0_attn_Wh"]),
        "Wo": np.ascontiguousarray(inputs["L0_attn_Wo"]),
        "Wphi": np.ascontiguousarray(inputs["L0_attn_Wphi"]),
        "bphi": np.ascontiguousarray(
            np.asarray(inputs["L0_attn_bphi"]).reshape(1, HD)),
        "vecs": np.ascontiguousarray(np.stack([
            np.asarray(inputs["L0_attn_bz"]),
            np.asarray(inputs["L0_attn_ln_g"]),
            np.asarray(inputs["L0_attn_ln_b"]),
            np.asarray(inputs["L0_attn_bo"]),
            np.asarray(inputs["L0_attn_bq"]),
            np.asarray(inputs["L0_attn_br"]),
            np.asarray(inputs["L0_attn_bh"]),
        ], axis=0)),
    }
    wup = np.asarray(inputs["L0_ffn_w_up"])
    wgt = np.asarray(inputs["L0_ffn_w_gate"])
    wdn = np.asarray(inputs["L0_ffn_w_down"])
    feeds = []
    for core in range(NCORES):
        f = dict(base)
        xb = x[2 * core:2 * core + 2]
        f["x_fm"] = np.ascontiguousarray(xb.transpose(0, 2, 1))
        f["c2"] = np.ascontiguousarray(c[2 * core:2 * core + 2])
        f["wup_sl"] = np.ascontiguousarray(wup[:, 128 * core:128 * (core + 1)])
        f["wgt_sl"] = np.ascontiguousarray(wgt[:, 128 * core:128 * (core + 1)])
        f["wdn_sl"] = np.ascontiguousarray(wdn[:, 384 * core:384 * (core + 1)])
        feeds.append(f)
    return feeds


def kernel(**inputs):
    feeds = make_feeds(inputs)
    nc, _ = build_program()
    res = run_bass_kernel_spmd(nc, feeds, core_ids=list(range(NCORES)))
    out = np.concatenate([np.asarray(r["out"]) for r in res.results], axis=0)
    return out.astype(np.float32, copy=False)
